# version 01
# baseline (speedup 1.0000x reference)
import contextlib
import math

import numpy as np
import ml_dtypes

import concourse.bass as bass
import concourse.mybir as mybir
from concourse.bass_utils import run_bass_kernel_spmd

F32 = mybir.dt.float32
BF16 = mybir.dt.bfloat16
AF = mybir.ActivationFunctionType
ALU = mybir.AluOpType
AXX = mybir.AxisListType.X
BFD = ml_dtypes.bfloat16

NCORES = 8
NB = 2
T = 2048
D = 1024
NT = 16
NH = 16
NE = 9
BSHIFT = 12.0

_UPTO = None
_TAPS = []


def _slopes():
    start = 2 ** (-2 ** (-(math.log2(NH) - 3)))
    return np.array([start * start**i for i in range(NH)], dtype=np.float32)


def host_layouts(inp):
    f32 = np.float32
    Wqkv = np.ascontiguousarray(inp["L0_attn_W_qkv"], dtype=f32)
    Wout = np.ascontiguousarray(inp["L0_attn_W_out"], dtype=f32)
    gw = np.ascontiguousarray(inp["L0_ffn_gate_w"], dtype=f32)
    ew1 = np.asarray(inp["L0_ffn_ew1"], dtype=f32)
    ew2 = np.asarray(inp["L0_ffn_ew2"], dtype=f32)
    ew3 = np.asarray(inp["L0_ffn_ew3"], dtype=f32)
    sw1 = np.asarray(inp["L0_ffn_sw1"], dtype=f32)
    sw2 = np.asarray(inp["L0_ffn_sw2"], dtype=f32)
    sw3 = np.asarray(inp["L0_ffn_sw3"], dtype=f32)
    wmod = np.ascontiguousarray(inp["L0_wmod"], dtype=f32)

    wqkv_l = np.ascontiguousarray(
        Wqkv.reshape(1024, 3, 8, 128).transpose(2, 0, 1, 3).reshape(8, 1024, 384))
    wout_l = np.ascontiguousarray(Wout.reshape(8, 128, 1024))
    gate_l = np.ascontiguousarray(gw.reshape(8, 128, 8))
    wmod_l = np.ascontiguousarray(wmod.reshape(8, 128, 6144))
    w1 = np.concatenate([ew1, sw1[None]], axis=0)
    w2 = np.concatenate([ew2, sw2[None]], axis=0)
    w3 = np.concatenate([ew3, sw3[None]], axis=0)
    t1 = w1.reshape(9, 8, 128, 16, 128).transpose(0, 3, 2, 1, 4).reshape(9, 16, 128, 1024)
    t2 = w2.reshape(9, 8, 128, 16, 128).transpose(0, 3, 2, 1, 4).reshape(9, 16, 128, 1024)
    w12_l = np.ascontiguousarray(np.concatenate([t1, t2], axis=3))
    w3_l = np.ascontiguousarray(w3.reshape(9, 16, 128, 1024))
    return dict(wqkv=wqkv_l, wout=wout_l, gate=gate_l, wmod=wmod_l,
                w12=w12_l, w3=w3_l)


def host_consts():
    sl = _slopes()
    k = np.arange(T, dtype=np.float32)
    bias = -sl[:, None] * k[None, :]
    bhi = bias.astype(BFD).astype(np.float32)
    blo = (bias - bhi).astype(BFD)
    qrow = (sl[:, None] * k[None, :]).astype(np.float32)
    ones = np.ones((16, T), np.float32)
    zero = np.zeros((16, T), np.float32)
    khlA = np.stack([bhi, blo.astype(np.float32), ones, zero], 1).astype(BFD)
    qA = np.stack([ones, ones, qrow, zero], 1).astype(BFD)
    khlB = np.stack([ones, bhi, blo.astype(np.float32), zero], 1).astype(BFD)
    qB = np.stack([qrow, ones, ones, zero], 1).astype(BFD)
    identb = np.eye(128, dtype=BFD)
    identf = np.eye(128, dtype=np.float32)
    kk = np.arange(128)
    tri = np.where(kk[:, None] >= kk[None, :], 0.0, -1e30).astype(BFD)
    ind32 = ((kk[:, None] == 32) * np.ones((1, 128))).astype(BFD)
    ind64 = ((kk[:, None] == 64) * np.ones((1, 128))).astype(BFD)
    indat = np.ascontiguousarray(np.concatenate([ind32, ind64], axis=1))
    e8 = np.arange(8)
    ind8 = np.repeat((e8[:, None, None] == e8[None, :, None]), 128, axis=2)
    ind8 = np.ascontiguousarray(ind8.reshape(8, 1024).astype(BFD))
    cvals = np.ascontiguousarray(
        np.tile(np.array([[1e-5, -BSHIFT]], np.float32), (128, 1)))
    return dict(khlA=np.ascontiguousarray(khlA), qA=np.ascontiguousarray(qA),
                khlB=np.ascontiguousarray(khlB), qB=np.ascontiguousarray(qB),
                identb=identb, identf=identf, tri=tri, indat=indat, ind8=ind8,
                cvals=cvals)


class Prog:
    def __init__(self):
        self.ops = {k: [] for k in "tvsgy"}
        self.cnt = {}
        self.sem = {}
        self.mark = {}

    def raw(self, eng, fn):
        self.ops[eng].append(fn)

    def wait(self, eng, sem, val=None):
        v = self.cnt.get(sem, 0) if val is None else val
        assert v <= self.cnt.get(sem, 0), f"wait on future event {sem} {v} > {self.cnt.get(sem, 0)}"
        if v > 0:
            self.raw(eng, lambda e, s=sem, vv=v: e.wait_ge(self.sem[s], vv))

    def vdep(self):
        self.wait('v', 's_v')

    def do(self, eng, fn, inc=None, by=1):
        if inc is None:
            self.raw(eng, fn)
            return None
        self.cnt[inc] = self.cnt.get(inc, 0) + by
        def run(e, fn=fn, s=inc, b=by):
            fn(e).then_inc(self.sem[s], b)
        self.raw(eng, run)
        return self.cnt[inc]


STAGES = ["ln1", "attn", "oproj", "ln2", "gate", "ffn", "out"]


def _before(upto, b, stage):
    if upto is None:
        return True
    ub, us = upto
    if b != ub:
        return b < ub
    return STAGES.index(stage) <= STAGES.index(us)


def build_program(taps=(), upto=None):
    nc = bass.Bass("TRN2", target_bir_lowering=False)
    C = host_consts()

    x_d = nc.dram_tensor("x", [NB, T, D], F32, kind="ExternalInput").ap()
    c_d = nc.dram_tensor("cvec", [NB, D], F32, kind="ExternalInput").ap()
    wqkv_d = nc.dram_tensor("wqkv", [8, 1024, 384], F32, kind="ExternalInput").ap()
    wout_d = nc.dram_tensor("wout", [8, 128, 1024], F32, kind="ExternalInput").ap()
    gate_d = nc.dram_tensor("gatew", [8, 128, 8], F32, kind="ExternalInput").ap()
    wmod_d = nc.dram_tensor("wmod", [8, 128, 6144], F32, kind="ExternalInput").ap()
    w12_d = nc.dram_tensor("w12", [9, 16, 128, 2048], F32, kind="ExternalInput").ap()
    w3_d = nc.dram_tensor("w3", [9, 16, 128, 1024], F32, kind="ExternalInput").ap()
    y_d = nc.dram_tensor("y", [NB, T, D], F32, kind="ExternalOutput").ap()

    khlA_d = nc.inline_tensor(C["khlA"], name="khlA").ap()
    qA_d = nc.inline_tensor(C["qA"], name="qA").ap()
    khlB_d = nc.inline_tensor(C["khlB"], name="khlB").ap()
    qB_d = nc.inline_tensor(C["qB"], name="qB").ap()
    identb_d = nc.inline_tensor(C["identb"], name="identb").ap()
    identf_d = nc.inline_tensor(C["identf"], name="identf").ap()
    tri_d = nc.inline_tensor(C["tri"], name="tri").ap()
    indat_d = nc.inline_tensor(C["indat"], name="indat").ap()
    ind8_d = nc.inline_tensor(C["ind8"], name="ind8").ap()
    cvals_d = nc.inline_tensor(C["cvals"], name="cvals").ap()

    dbg = {}
    def tap(name, shape, dtype):
        if name in taps:
            dbg[name] = nc.dram_tensor("dbg_" + name, shape, dtype,
                                       kind="ExternalOutput").ap()
            return True
        return False

    ARENA = nc.sbuf_bytes_remaining - 2048
    base = nc.bump_sbuf(ARENA)[0]
    cur = [base]
    def alloc(name, shape, dtype):
        a = (cur[0] + 31) & ~31
        t = nc.alloc_sbuf_tensor_at(name, shape, dtype, offset=a)
        cur[0] = a + shape[1] * {F32: 4, BF16: 2}[dtype]
        return t

    x_buf = alloc("x_buf", [128, NT * 1024], F32)
    XOFF = nc.lookup_mloc(x_buf).addr
    h_buf = alloc("h_buf", [128, 8 * 2048], BF16)
    og = alloc("og", [128, 16384], BF16)
    w3sb = alloc("w3sb", [128, 16 * 1024], BF16)
    w12sb = alloc("w12sb", [128, 2 * 2048], BF16)
    Wb = alloc("Wb", [128, 1024], BF16)
    tmp_si = alloc("tmp_si", [128, 1024], BF16)
    tmp_gb = alloc("tmp_gb", [128, 1024], BF16)
    S1 = alloc("S1", [128, 1024], BF16)
    S2 = alloc("S2", [128, 1024], BF16)
    S3 = alloc("S3", [128, 1024], BF16)
    Wrow = alloc("Wrow", [8, 2048], BF16)
    Wg = alloc("Wg", [128, 128], F32)
    gatewsb = alloc("gatewsb", [128, 64], BF16)
    modsb = alloc("modsb", [128, 96], F32)
    dpb = alloc("dpb", [128, 96], BF16)
    gtmp = alloc("gtmp", [128, 32], F32)

    cfmb = alloc("cfmb", [128, 16], BF16)
    rowt = alloc("rowt", [1, 1024], BF16)
    identb_s = alloc("identb_s", [128, 128], BF16)
    identf_s = alloc("identf_s", [128, 128], F32)
    tri_s = alloc("tri_s", [128, 128], BF16)
    indat_s = alloc("indat_s", [128, 256], BF16)
    ind8_s = alloc("ind8_s", [8, 1024], BF16)
    onesb = alloc("onesb", [1, 128], BF16)
    s_sums = alloc("s_sums", [128, 512], BF16)
    sq_tmp = alloc("sq_tmp", [128, 1024], F32)
    csil = nc.alloc_sbuf_tensor_at("csil", [2, 1024], F32,
                                   offset=nc.lookup_mloc(sq_tmp).addr)
    xnb = alloc("xnb", [128, 1024], BF16)
    stats_a = alloc("stats_a", [128, 4], F32)
    stats_b = alloc("stats_b", [128, 4], F32)
    stats_c = alloc("stats_c", [128, 4], F32)
    gA = alloc("gA", [128, 16], F32)
    gB = alloc("gB", [128, 16], F32)
    gC = alloc("gC", [128, 16], F32)
    gtmp2 = alloc("gtmp2", [128, 32], F32)
    xnb2 = alloc("xnb2", [128, 1024], BF16)
    tmp_si2 = alloc("tmp_si2", [128, 1024], BF16)
    cvals_s = alloc("cvals_s", [128, 2], F32)
    nc.const_aps.register(F32, 1e-5, cvals_s[:, 0:1])
    nc.const_aps.register(F32, -BSHIFT, cvals_s[:, 1:2])
    assert cur[0] <= base + ARENA, f"sbuf over: {cur[0] - base} > {ARENA}"

    Qta = nc.alloc_sbuf_tensor_at("Qta", [128, 2048], BF16, offset=XOFF + 0)
    Qtb = nc.alloc_sbuf_tensor_at("Qtb", [128, 2048], BF16, offset=XOFF + 4096)
    Kta = nc.alloc_sbuf_tensor_at("Kta", [128, 2048], BF16, offset=XOFF + 8192)
    Ktb = nc.alloc_sbuf_tensor_at("Ktb", [128, 2048], BF16, offset=XOFF + 12288)
    Vab = nc.alloc_sbuf_tensor_at("Vab", [128, 16 * 196], BF16, offset=XOFF + 16384)
    Pt = nc.alloc_sbuf_tensor_at("Pt", [128, 1024], BF16, offset=XOFF + 22688)
    wqkvsb = nc.alloc_sbuf_tensor_at("wqkvsb", [128, 3072], BF16, offset=XOFF + 24768)
    xstg = nc.alloc_sbuf_tensor_at("xstg", [128, 2048], F32, offset=XOFF + 57344)
    csb = nc.alloc_sbuf_tensor_at("csb", [2, 1024], F32, offset=XOFF + 40960)

    PS = nc.alloc_psum_tensor("PS", [128, 3072], F32)
    PSb = nc.alloc_psum_tensor("PSb", [128, 1024], BF16)
    PSb2 = nc.alloc_psum_tensor("PSb2", [128, 1024], BF16)
    def bank(i, w=512):
        return PS[:, i * 512:i * 512 + w]

    P = Prog()
    free_bank = {}

    def bank_wait(eng, *banks):
        for bk in banks:
            ev = free_bank.pop(bk, None)
            if ev is not None:
                P.wait(eng, ev[0], ev[1])

    for dst, src in [(identb_s, identb_d), (identf_s, identf_d), (tri_s, tri_d),
                     (indat_s, indat_d), (ind8_s, ind8_d)]:
        P.do('g', lambda e, d=dst, s=src: e.dma_start(out=d[:], in_=s), 'd_misc', 16)
    P.do('g', lambda e: e.dma_start(out=gatewsb[:],
         in_=gate_d.rearrange("kc kp e -> kp kc e")), 'd_misc', 16)
    P.do('g', lambda e: e.dma_start(out=csb[:], in_=c_d), 'd_misc', 16)
    P.do('g', lambda e: e.dma_start(out=cvals_s[:], in_=cvals_d), 'd_misc', 16)
    misc_total = P.cnt['d_misc']
    P.do('v', lambda e: e.memset(onesb[:], 1.0), 's_v')
    P.do('v', lambda e: e.memset(s_sums[:], 1.0), 's_v')

    P.wait('s', 'd_misc', misc_total)
    P.do('s', lambda e: e.activation(csil[:], csb[:], AF.Silu), 's_s')
    P.wait('t', 'd_misc', misc_total)
    P.wait('t', 's_s')
    cfm_ev = None
    for kc in range(8):
        if cfm_ev is not None:
            P.wait('t', 's_v', cfm_ev)
        tr = P.do('t', lambda e, kc=kc: e.transpose(
            PS[:, 5 * 512: 5 * 512 + 2],
            csil[0:2, kc * 128:(kc + 1) * 128], identf_s[0:2, 0:2]), 's_t')
        P.wait('v', 's_t', tr)
        cfm_ev = P.do('v', lambda e, kc=kc: e.tensor_copy(
            cfmb[:, kc * 2:(kc + 1) * 2], PS[:, 5 * 512:5 * 512 + 2]), 's_v')

    wm_done = {}
    def emit_wmod_dma(kc):
        if kc - 2 in wm_done:
            P.wait('g', 's_t', wm_done[kc - 2])
        sem = 'd_wm' if kc % 2 == 0 else 'd_wmb'
        P.do('g', lambda e, kc=kc: e.dma_start(
            out=h_buf[:, (kc % 2) * 6144:((kc % 2) + 1) * 6144], in_=wmod_d[kc]),
            sem, 16)
        P.mark[('wm', kc)] = (sem, P.cnt[sem])
    emit_wmod_dma(0)
    emit_wmod_dma(1)
    P.wait('t', 's_v', cfm_ev)
    for kc in range(8):
        P.wait('t', *P.mark[('wm', kc)])
        for Mc in range(48):
            P.do('t', lambda e, kc=kc, Mc=Mc: e.matmul(
                PS[:, 4 * 512 + Mc * 2: 4 * 512 + Mc * 2 + 2],
                h_buf[:, (kc % 2) * 6144 + Mc * 128: (kc % 2) * 6144 + (Mc + 1) * 128],
                cfmb[:, kc * 2:(kc + 1) * 2],
                start=(kc == 0 and Mc == 0), stop=(kc == 7 and Mc == 47),
                skip_group_check=True), 's_t')
        wm_done[kc] = P.cnt['s_t']
        if kc + 2 < 8:
            emit_wmod_dma(kc + 2)
    P.wait('v', 's_t')
    P.do('v', lambda e: e.tensor_copy(modsb[:], PS[:, 4 * 512:4 * 512 + 96]), 's_v')
    free_bank[4] = ('s_v', P.cnt['s_v'])
    free_bank[5] = ('s_v', P.cnt['s_v'])

    for sp in range(6):
        for bi in range(NB):
            sl = slice(sp * 16 + bi, sp * 16 + 16, 2)
            src, dst = modsb[:, sl], dpb[:, sl]
            if sp in (1, 4):
                P.do('v', lambda e, s=src, d=dst: e.tensor_scalar_add(d, s, 1.0), 's_v')
            elif sp in (0, 3):
                P.do('v', lambda e, s=src, d=dst: e.tensor_copy(d, s), 's_v')
            else:
                p_, n_, t_ = gtmp[:, 0:8], gtmp[:, 8:16], gtmp[:, 16:24]
                m_, d_ = gtmp2[:, 0:8], gtmp2[:, 8:16]
                P.do('v', lambda e, s=src, p=p_: e.tensor_scalar_max(p, s, 0.25), 's_v')
                P.do('v', lambda e, s=src, n=n_: e.tensor_scalar_min(n, s, -0.25), 's_v')
                P.do('v', lambda e, s=src, m=m_: e.tensor_single_scalar(m, s, 0.0, ALU.is_ge), 's_v')
                P.vdep()
                P.do('v', lambda e, p=p_, n=n_, d=d_: e.tensor_sub(d, p, n), 's_v')
                P.vdep()
                P.do('v', lambda e, m=m_, d=d_, t=t_: e.tensor_mul(t, m, d), 's_v')
                P.vdep()
                P.do('v', lambda e, n=n_, t=t_, ds=dst: e.tensor_add(ds, n, t), 's_v')
    mod_done_v = P.cnt['s_v']

    if tap("mod", [128, 96], F32):
        P.wait('y', 's_v', mod_done_v)
        P.do('y', lambda e: e.dma_start(out=dbg["mod"], in_=modsb[:]), 'd_db', 16)
    if tap("dpb", [128, 96], BF16):
        P.wait('y', 's_v', mod_done_v)
        P.do('y', lambda e: e.dma_start(out=dbg["dpb"], in_=dpb[:]), 'd_db', 16)

    def build_bcast(sp, bi, dst):
        bank_wait('t', 4)
        bank_wait('t', 5)
        P.wait('t', 's_v', mod_done_v)
        last = None
        for fc in range(8):
            col = sp * 16 + fc * 2 + bi
            last = P.do('t', lambda e, col=col, fc=fc: e.matmul(
                PS[0:1, 2048 + fc * 128:2048 + (fc + 1) * 128],
                dpb[:, col:col + 1], identb_s[:],
                start=(fc % 4 == 0), stop=(fc % 4 == 3),
                skip_group_check=True), 's_t')
        P.wait('v', 's_t', last)
        cp = P.do('v', lambda e: e.tensor_copy(rowt[0:1, :], PS[0:1, 2048:3072]), 's_v')
        P.wait('t', 's_v', cp)
        for nck in range(2):
            last = P.do('t', lambda e, nck=nck: e.matmul(
                PS[:, 2048 + nck * 512:2048 + (nck + 1) * 512],
                onesb[0:1, 0:128], rowt[0:1, nck * 512:(nck + 1) * 512],
                start=True, stop=True), 's_t')
        P.wait('v', 's_t', last)
        ev = P.do('v', lambda e, d=dst: e.tensor_copy(d[:], PS[:, 2048:3072]), 's_v')
        free_bank[4] = ('s_v', ev)
        free_bank[5] = ('s_v', ev)
        return ev

    if "bct" in taps:
        dbg["bct"] = nc.dram_tensor("dbg_bct", [128, 2048], BF16, kind="ExternalOutput").ap()
        build_bcast(1, 0, S1)
        bev = build_bcast(0, 0, S2)
        P.wait('y', 's_v', bev)
        P.do('y', lambda e: e.dma_start(out=dbg["bct"][:, 0:1024], in_=S1[:]), 'd_db', 16)
        P.do('y', lambda e: e.dma_start(out=dbg["bct"][:, 1024:2048], in_=S2[:]), 'd_db', 16)

    def ln_phase(b, which):
        sev = build_bcast(4 if which == 2 else 1, b, S1)
        if "s1early" in taps and b == 0 and which == 1:
            if "s1early" not in dbg:
                dbg["s1early"] = nc.dram_tensor("dbg_s1early", [128, 1024], BF16,
                                                kind="ExternalOutput").ap()
            P.wait('y', 's_v', sev)
            P.do('y', lambda e: e.dma_start(out=dbg["s1early"], in_=S1[:]), 'd_db', 16)
        ev2 = build_bcast(3 if which == 2 else 0, b, S2)
        P.wait('y', 's_v', ev2)
        P.wait('s', 's_v')
        apply_done = {}
        red_done = {}

        def stage_dma(t):
            slot = t % 2
            sem = 'd_xsa' if slot == 0 else 'd_xsb'
            if t - 2 in apply_done:
                P.wait('y', 's_v', apply_done[t - 2])
            P.do('y', lambda e, b=b, t=t, slot=slot: e.dma_start(
                out=xstg[:, slot * 1024:(slot + 1) * 1024],
                in_=x_d[b, t * 128:(t + 1) * 128, :]), sem, 16)
            P.mark[('lnx', b, which, t)] = P.cnt[sem]

        if which == 1:
            stage_dma(0)
            stage_dma(1)
        for t in range(NT):
            if which == 1:
                slot = t % 2
                sem = 'd_xsa' if slot == 0 else 'd_xsb'
                P.wait('s', sem, P.mark[('lnx', b, which, t)])
                P.wait('v', sem, P.mark[('lnx', b, which, t)])
                xt = xstg[:, slot * 1024:(slot + 1) * 1024]
            else:
                xt = x_buf[:, t * 1024:(t + 1) * 1024]
            if t - 1 in red_done:
                P.wait('s', 's_v', red_done[t - 1])
            sq = P.do('s', lambda e, xt=xt: e.activation(sq_tmp[:], xt, AF.Square), 's_s')
            P.do('v', lambda e, xt=xt: e.reduce_sum(stats_a[:, 0:1], xt, axis=AXX, negate=True), 's_v')
            P.wait('v', 's_s', sq)
            rd = P.do('v', lambda e: e.reduce_sum(stats_a[:, 1:2], sq_tmp[:], axis=AXX), 's_v')
            red_done[t] = rd
            P.vdep()
            P.do('v', lambda e: e.tensor_scalar_mul(stats_b[:, 0:1], stats_a[:, 0:1], 1.0 / D), 's_v')
            P.do('v', lambda e: e.tensor_scalar_mul(stats_b[:, 1:2], stats_a[:, 1:2], 1.0 / D), 's_v')
            P.vdep()
            P.do('v', lambda e: e.tensor_mul(stats_c[:, 0:1], stats_b[:, 0:1], stats_b[:, 0:1]), 's_v')
            P.vdep()
            vv = P.do('v', lambda e: e.tensor_sub(stats_a[:, 2:3], stats_b[:, 1:2], stats_c[:, 0:1]), 's_v')
            P.wait('s', 's_v', vv)
            st = P.do('s', lambda e: e.activation(stats_b[:, 2:3], stats_a[:, 2:3], AF.Sqrt, bias=1e-5), 's_s')
            P.wait('v', 's_s', st)
            P.do('v', lambda e: e.reciprocal(stats_c[:, 1:2], stats_b[:, 2:3]), 's_v')
            P.vdep()
            nm = P.do('v', lambda e: e.tensor_mul(stats_a[:, 3:4], stats_b[:, 0:1], stats_c[:, 1:2]), 's_v')
            P.vdep()
            ap = P.do('v', lambda e, xt=xt: e.tensor_scalar(
                xnb[:], xt, stats_c[:, 1:2], stats_a[:, 3:4], ALU.mult, ALU.add), 's_v')
            apply_done[t] = ap
            if which == 1 and t + 2 < NT:
                stage_dma(t + 2)
            P.wait('v', 's_v', ev2)
            P.vdep()
            P.do('v', lambda e: e.tensor_mul(xnb2[:], xnb[:], S1[:]), 's_v')
            P.vdep()
            md = P.do('v', lambda e: e.tensor_add(xnb[:], xnb2[:], S2[:]), 's_v')
            P.wait('t', 's_v', md)
            for j in range(8):
                pb = PSb if j % 2 == 0 else PSb2
                ev = free_bank.pop(('tr', j % 2), None)
                if ev is not None:
                    P.wait('t', ev[0], ev[1])
                tr = P.do('t', lambda e, pb=pb, j=j: e.transpose(
                    pb[:, 0:128], xnb[:, j * 128:(j + 1) * 128], identb_s[:]), 's_t')
                P.wait('v', 's_t', tr)
                cp = P.do('v', lambda e, pb=pb, j=j, t=t: e.tensor_copy(
                    h_buf[:, j * 2048 + t * 128: j * 2048 + (t + 1) * 128],
                    pb[:, 0:128]), 's_v')
                free_bank[('tr', j % 2)] = ('s_v', cp)
        return P.cnt['s_v']

    ffn_evac = {}
    out_dma_total = 0
    w12_seq = [0]

    for b in range(NB):
        if not _before(upto, b, "ln1"):
            break
        if b == 1:
            for eng in "tvsgy":
                P.wait(eng, 'd_xo', out_dma_total)

        hv = ln_phase(b, 1)
        if b == 0 and tap("S1", [128, 1024], BF16):
            P.wait('y', 's_v')
            P.do('y', lambda e: e.dma_start(out=dbg["S1"], in_=S1[:]), 'd_db', 16)
        if b == 0 and tap("S2", [128, 1024], BF16):
            P.wait('y', 's_v')
            P.do('y', lambda e: e.dma_start(out=dbg["S2"], in_=S2[:]), 'd_db', 16)
        if b == 0 and tap("rowt", [1, 1024], BF16):
            P.wait('y', 's_v')
            P.do('y', lambda e: e.dma_start(out=dbg["rowt"], in_=rowt[:]), 'd_db', 16)
        if b == 0 and tap("xnb", [128, 1024], BF16):
            P.wait('y', 's_v')
            P.do('y', lambda e: e.dma_start(out=dbg["xnb"], in_=xnb[:]), 'd_db', 16)
        if b == 0 and tap("sts", [128, 12], F32):
            P.wait('y', 's_v')
            P.do('y', lambda e: e.dma_start(out=dbg["sts"][:, 0:4], in_=stats_a[:]), 'd_db', 16)
            P.do('y', lambda e: e.dma_start(out=dbg["sts"][:, 4:8], in_=stats_b[:]), 'd_db', 16)
            P.do('y', lambda e: e.dma_start(out=dbg["sts"][:, 8:12], in_=stats_c[:]), 'd_db', 16)
        if b == 0 and tap("ln1", [128, 16384], BF16):
            P.wait('y', 's_v', hv)
            P.do('y', lambda e: e.dma_start(out=dbg["ln1"], in_=h_buf[:]), 'd_db', 16)
        if not _before(upto, b, "attn"):
            break

        for buf in (Qta, Qtb, Kta, Ktb):
            P.do('v', lambda e, bf=buf: e.memset(bf[:], 0.0), 's_v')
        P.do('v', lambda e: e.memset(Vab[:], 0.0), 's_v')
        P.do('v', lambda e: e.memset(Vab[:, 64:16 * 196:196], 1.0), 's_v')
        ms_done = P.do('v', lambda e: e.memset(Vab[:, 100:16 * 196:196], 1.0), 's_v')

        if b == 1:
            P.wait('g', 's_t', P.cnt['s_t'])
        P.do('g', lambda e: e.dma_start(
            out=w3sb[:, 0:8192],
            in_=wout_d.rearrange("kc kp m -> kp kc m")), 'd_w3', 16)
        wout_target = P.cnt['d_w3']

        def load_pair_tables(p):
            P.do('y', lambda e, p=p: e.dma_start(out=Kta[64:68, :], in_=khlA_d[2 * p]), 'd_kr', 16)
            P.do('y', lambda e, p=p: e.dma_start(out=Qta[64:68, :], in_=qA_d[2 * p]), 'd_kr', 16)
            P.do('y', lambda e, p=p: e.dma_start(out=Ktb[32:36, :], in_=khlB_d[2 * p + 1]), 'd_kr', 16)
            P.do('y', lambda e, p=p: e.dma_start(out=Qtb[32:36, :], in_=qB_d[2 * p + 1]), 'd_kr', 16)

        def load_pair_w(p):
            P.do('g', lambda e, p=p: e.dma_start(
                out=wqkvsb[:],
                in_=wqkv_d[p].rearrange("(kc kp) j -> kp kc j", kp=128)), 'd_wq', 16)

        P.wait('y', 's_v', ms_done)
        load_pair_tables(0)
        P.wait('g', 's_v', hv)
        load_pair_w(0)

        exp_bank = {0: 0, 1: 0}
        pv_slot = {0: 0, 1: 0}

        for p in range(8):
            P.wait('t', 'd_wq', 16 * (p + 1) + (16 * 8 * b))
            P.wait('t', 's_v', hv)
            for role, dstA, dstB, scale in ((0, Qta, Qtb, 0.125), (1, Kta, Ktb, 1.0)):
                for bki in range(4):
                    bank_wait('t', bki)
                if p == 0 and role == 0:
                    P.wait('t', 's_v', ms_done)
                mm = None
                for K in range(8):
                    for ncx in range(4):
                        mm = P.do('t', lambda e, K=K, ncx=ncx, role=role: e.matmul(
                            bank(ncx),
                            wqkvsb[:, K * 384 + role * 128: K * 384 + (role + 1) * 128],
                            h_buf[:, K * 2048 + ncx * 512: K * 2048 + (ncx + 1) * 512],
                            start=(K == 0), stop=(K == 7)), 's_t')
                for ncx in range(4):
                    P.wait('s', 's_t', mm)
                    P.do('s', lambda e, dstA=dstA, ncx=ncx, scale=scale: e.activation(
                        dstA[0:64, ncx * 512:(ncx + 1) * 512],
                        PS[0:64, ncx * 512:(ncx + 1) * 512], AF.Copy, scale=scale), 's_s')
                    ev = P.do('s', lambda e, dstB=dstB, ncx=ncx, scale=scale: e.activation(
                        dstB[64:128, ncx * 512:(ncx + 1) * 512],
                        PS[64:128, ncx * 512:(ncx + 1) * 512], AF.Copy, scale=scale), 's_s')
                    free_bank[ncx] = ('s_s', ev)
            for tt in range(NT):
                bki = 4 + tt % 2
                bank_wait('t', bki)
                mm = None
                for K in range(8):
                    mm = P.do('t', lambda e, K=K, tt=tt, bki=bki: e.matmul(
                        PS[:, bki * 512:bki * 512 + 128],
                        h_buf[:, K * 2048 + tt * 128: K * 2048 + (tt + 1) * 128],
                        wqkvsb[:, K * 384 + 256: K * 384 + 384],
                        start=(K == 0), stop=(K == 7)), 's_t')
                P.wait('v', 's_t', mm)
                if p == 0 and tt < 2:
                    P.wait('v', 's_v', ms_done)
                P.do('v', lambda e, tt=tt, bki=bki: e.tensor_copy(
                    Vab[:, tt * 196: tt * 196 + 64], PS[:, bki * 512:bki * 512 + 64]), 's_v')
                ev = P.do('v', lambda e, tt=tt, bki=bki: e.tensor_copy(
                    Vab[:, tt * 196 + 132: tt * 196 + 196],
                    PS[:, bki * 512 + 64:bki * 512 + 128]), 's_v')
                free_bank[bki] = ('s_v', ev)
            qkv_done_t = P.cnt['s_t']
            qkv_done_s = P.cnt['s_s']
            qkv_done_v = P.cnt['s_v']

            if p < 7:
                P.wait('g', 's_t', qkv_done_t)
                load_pair_w(p + 1)

            P.wait('t', 'd_kr', P.cnt['d_kr'])
            P.wait('t', 's_s', qkv_done_s)
            P.wait('t', 's_v', qkv_done_v)
            pv = None
            for hh in range(2):
                if hh == 0:
                    Qx, Kx, krows, obank, sbank, vlo, vhi, orow = \
                        Qta, Kta, 67, 2, 3, 0, 65, 64
                    ind = indat_s[:, 128:256]
                    orows = slice(0, 64)
                else:
                    Qx, Kx, krows, obank, sbank, vlo, vhi, orow = \
                        Qtb, Ktb, 128, 4, 5, 68, 196, 32
                    ind = indat_s[:, 0:128]
                    orows = slice(64, 128)
                for qb in range(4):
                    q0 = qb * 512
                    kc0 = 4 * qb
                    bank_wait('t', obank)
                    for kc in range(kc0, 16):
                        sbk = kc % 2
                        slot = kc % 2
                        P.wait('t', 's_s', exp_bank[sbk])
                        mmS = P.do('t', lambda e, Kx=Kx, kc=kc, q0=q0, krows=krows, sbk=sbk, Qx=Qx: e.matmul(
                            bank(sbk), Kx[0:krows, kc * 128:(kc + 1) * 128],
                            Qx[0:krows, q0:q0 + 512], start=True, stop=True), 's_t')
                        diag = kc0 <= kc < kc0 + 4
                        r = kc * 128 - q0
                        vmask = None
                        if diag:
                            P.wait('v', 's_t', mmS)
                            vmask = P.do('v', lambda e, sbk=sbk, r=r: e.tensor_add(
                                PS[:, sbk * 512 + r: sbk * 512 + r + 128],
                                PS[:, sbk * 512 + r: sbk * 512 + r + 128], tri_s[:]), 's_v')
                        w = r + 128 if diag else 512
                        P.wait('s', 's_t', mmS)
                        if vmask is not None:
                            P.wait('s', 's_v', vmask)
                        P.wait('s', 's_t', pv_slot[slot])
                        ex = P.do('s', lambda e, slot=slot, sbk=sbk, w=w: e.activation(
                            Pt[:, slot * 512: slot * 512 + w],
                            PS[:, sbk * 512: sbk * 512 + w], AF.Exp, bias=-BSHIFT), 's_s')
                        exp_bank[sbk] = ex
                        pmem = None
                        if diag and w < 512:
                            P.wait('v', 's_t', pv_slot[slot])
                            pmem = P.do('v', lambda e, slot=slot, w=w: e.memset(
                                Pt[:, slot * 512 + w: (slot + 1) * 512], 0.0), 's_v')
                        P.wait('t', 's_s', ex)
                        if pmem is not None:
                            P.wait('t', 's_v', pmem)
                        pv = P.do('t', lambda e, kc=kc, slot=slot, obank=obank, vlo=vlo, vhi=vhi, kc0=kc0: e.matmul(
                            PS[0:(vhi - vlo), obank * 512:(obank + 1) * 512],
                            Vab[:, kc * 196 + vlo: kc * 196 + vhi],
                            Pt[:, slot * 512:(slot + 1) * 512],
                            start=(kc == kc0), stop=(kc == 15)), 's_t')
                        pv_slot[slot] = pv
                    P.wait('v', 's_t', pv)
                    rc = P.do('v', lambda e, orow=orow, obank=obank: e.reciprocal(
                        s_sums[orow:orow + 1, :],
                        PS[orow:orow + 1, obank * 512:(obank + 1) * 512]), 's_v')
                    bank_wait('t', sbank)
                    P.wait('t', 's_v', rc)
                    sb = P.do('t', lambda e, ind=ind, sbank=sbank: e.matmul(
                        bank(sbank), ind, s_sums[:], start=True, stop=True), 's_t')
                    P.wait('s', 's_t', pv)
                    oc = P.do('s', lambda e, orows=orows, obank=obank: e.activation(
                        tmp_gb[orows, 0:512], PS[orows, obank * 512:(obank + 1) * 512],
                        AF.Copy), 's_s')
                    free_bank[obank] = ('s_s', oc)
                    P.wait('v', 's_t', sb)
                    P.wait('v', 's_s', oc)
                    nm = P.do('v', lambda e, orows=orows, sbank=sbank, p=p, q0=q0: e.tensor_mul(
                        og[orows, p * 2048 + q0: p * 2048 + q0 + 512],
                        tmp_gb[orows, 0:512],
                        PS[orows, sbank * 512:(sbank + 1) * 512]), 's_v')
                    free_bank[sbank] = ('s_v', nm)
            if p < 7:
                P.wait('y', 's_t', P.cnt['s_t'])
                load_pair_tables(p + 1)

        attn_done_v = P.cnt['s_v']
        if b == 0 and tap("og", [128, 16384], BF16):
            P.wait('y', 's_v', attn_done_v)
            P.do('y', lambda e: e.dma_start(out=dbg["og"], in_=og[:]), 'd_db', 16)
        if not _before(upto, b, "oproj"):
            break

        ga_ev = build_bcast(2, b, S3)
        P.wait('t', 'd_w3', wout_target)
        P.wait('t', 's_v', attn_done_v)
        evac_op = {}

        def oproj_dma(t):
            slot = t % 2
            sem = 'd_xsa' if slot == 0 else 'd_xsb'
            if t - 2 in evac_op:
                P.wait('y', 's_v', evac_op[t - 2])
            else:
                P.wait('y', 's_v', attn_done_v)
            P.do('y', lambda e, b=b, t=t, slot=slot: e.dma_start(
                out=xstg[:, slot * 1024:(slot + 1) * 1024],
                in_=x_d[b, t * 128:(t + 1) * 128, :]), sem, 16)
            P.mark[('opx', b, t)] = P.cnt[sem]

        oproj_dma(0)
        oproj_dma(1)
        for t in range(NT):
            slot = t % 2
            sem = 'd_xsa' if slot == 0 else 'd_xsb'
            b0k, b1k = 2 * (t % 2), 2 * (t % 2) + 1
            bank_wait('t', b0k)
            bank_wait('t', b1k)
            mm = None
            for K in range(8):
                for ncx in range(2):
                    mm = P.do('t', lambda e, K=K, t=t, b0k=b0k, ncx=ncx: e.matmul(
                        PS[:, b0k * 512 + ncx * 512: b0k * 512 + (ncx + 1) * 512],
                        og[:, K * 2048 + t * 128: K * 2048 + (t + 1) * 128],
                        w3sb[:, K * 1024 + ncx * 512: K * 1024 + (ncx + 1) * 512],
                        start=(K == 0), stop=(K == 7)), 's_t')
            P.wait('v', 's_t', mm)
            P.wait('v', sem, P.mark[('opx', b, t)])
            P.wait('v', 's_v', ga_ev)
            P.do('v', lambda e, b0k=b0k: e.tensor_mul(
                tmp_gb[:], PS[:, b0k * 512: b0k * 512 + 1024], S3[:]), 's_v')
            P.vdep()
            ev = P.do('v', lambda e, t=t, slot=slot: e.tensor_add(
                x_buf[:, t * 1024:(t + 1) * 1024],
                xstg[:, slot * 1024:(slot + 1) * 1024], tmp_gb[:]), 's_v')
            evac_op[t] = ev
            free_bank[b0k] = ('s_v', ev)
            free_bank[b1k] = ('s_v', ev)
            if t + 2 < NT:
                oproj_dma(t + 2)
        oproj_done_v = P.cnt['s_v']
        if b == 0 and tap("x2", [128, 16384], F32):
            P.wait('y', 's_v', oproj_done_v)
            P.do('y', lambda e: e.dma_start(out=dbg["x2"], in_=x_buf[:]), 'd_db', 16)
        if not _before(upto, b, "ln2"):
            break

        P.wait('v', 's_t', P.cnt['s_t'])
        hv2 = ln_phase(b, 2)
        if b == 0 and tap("h2", [128, 16384], BF16):
            P.wait('y', 's_v', hv2)
            P.do('y', lambda e: e.dma_start(out=dbg["h2"], in_=h_buf[:]), 'd_db', 16)
        if not _before(upto, b, "gate"):
            break

        wg_done = {}
        for t in range(NT):
            bank_wait('t', 4)
            P.wait('t', 's_v', hv2)
            mm = None
            for K in range(8):
                mm = P.do('t', lambda e, K=K, t=t: e.matmul(
                    PS[:, 4 * 512: 4 * 512 + 8],
                    h_buf[:, K * 2048 + t * 128: K * 2048 + (t + 1) * 128],
                    gatewsb[:, K * 8:(K + 1) * 8],
                    start=(K == 0), stop=(K == 7)), 's_t')
            P.wait('v', 's_t', mm)
            P.wait('v', 's_s')
            P.do('v', lambda e: e.tensor_copy(gA[:, 0:8], PS[:, 4 * 512:4 * 512 + 8]), 's_v')
            P.vdep()
            ev = P.do('v', lambda e: e.reduce_max(gB[:, 0:1], gA[:, 0:8], axis=AXX, negate=True), 's_v')
            free_bank[4] = ('s_v', ev)
            P.vdep()
            P.do('v', lambda e: e.tensor_scalar_add(gC[:, 0:8], gA[:, 0:8], gB[:, 0:1]), 's_v')
            P.vdep()
            P.do('v', lambda e: e.tensor_single_scalar(gB[:, 1:9], gC[:, 0:8], 0.0, ALU.is_ge), 's_v')
            P.vdep()
            P.do('v', lambda e: e.scalar_tensor_tensor(
                gC[:, 8:16], gB[:, 1:9], -1e30, gA[:, 0:8], ALU.mult, ALU.add), 's_v')
            P.vdep()
            vm = P.do('v', lambda e: e.reduce_max(gB[:, 9:10], gC[:, 8:16], axis=AXX, negate=True), 's_v')
            P.wait('s', 's_v', vm)
            ex = P.do('s', lambda e: e.activation(gC[:, 0:8], gA[:, 0:8], AF.Exp, bias=gB[:, 0:1]), 's_s')
            P.vdep()
            P.do('v', lambda e: e.tensor_scalar_add(gtmp[:, 0:8], gA[:, 0:8], gB[:, 9:10]), 's_v')
            P.vdep()
            P.do('v', lambda e: e.tensor_single_scalar(gB[:, 1:9], gtmp[:, 0:8], 0.0, ALU.is_ge), 's_v')
            P.wait('v', 's_s', ex)
            P.vdep()
            P.do('v', lambda e: e.tensor_mul(gA[:, 8:16], gC[:, 0:8], gB[:, 1:9]), 's_v')
            P.vdep()
            P.do('v', lambda e: e.reduce_sum(gB[:, 10:11], gA[:, 8:16], axis=AXX), 's_v')
            P.vdep()
            P.do('v', lambda e: e.reciprocal(gC[:, 1:2], gB[:, 10:11]), 's_v')
            P.vdep()
            wg = P.do('v', lambda e, t=t: e.tensor_scalar_mul(
                Wg[:, t * 8:(t + 1) * 8], gA[:, 8:16], gC[:, 1:2]), 's_v')
            wg_done[t] = wg
        bank_wait('t', 4)
        bank_wait('t', 5)
        for t in range(NT):
            sl = (4 + t % 2) * 512
            ev = free_bank.pop(('gt', t % 2), None)
            if ev is not None:
                P.wait('t', ev[0], ev[1])
            P.wait('t', 's_v', wg_done[t])
            tr = P.do('t', lambda e, t=t, sl=sl: e.transpose(
                PS[0:8, sl:sl + 128], Wg[:, t * 8:(t + 1) * 8], identf_s[:]), 's_t')
            P.wait('v', 's_t', tr)
            cp = P.do('v', lambda e, t=t, sl=sl: e.tensor_copy(
                Wrow[0:8, t * 128:(t + 1) * 128], PS[0:8, sl:sl + 128]), 's_v')
            free_bank[('gt', t % 2)] = ('s_v', cp)
        gate_done_v = P.cnt['s_v']
        free_bank[4] = ('s_v', gate_done_v)
        free_bank[5] = ('s_v', gate_done_v)
        if b == 0 and tap("wrow", [8, 2048], BF16):
            P.wait('y', 's_v', gate_done_v)
            P.do('y', lambda e: e.dma_start(out=dbg["wrow"], in_=Wrow[:]), 'd_db', 16)
        if not _before(upto, b, "ffn"):
            break

        gb_ev = build_bcast(5, b, S3)
        P.wait('v', 's_t', P.cnt['s_t'])
        og_free_v = P.cnt['s_v']
        w12_done = {}
        w12_fill = {}
        glast = None

        def w12_dma(e_i, s, M):
            blk = w12_seq[0]
            w12_seq[0] += 1
            slot12 = blk % 2
            sem12 = 'd_w12a' if slot12 == 0 else 'd_w12b'
            if blk - 2 in w12_done:
                P.wait('g', 's_t', w12_done[blk - 2])
            P.do('g', lambda e, e_i=e_i, M=M, slot12=slot12: e.dma_start(
                out=w12sb[:, slot12 * 2048:(slot12 + 1) * 2048],
                in_=w12_d[e_i, M]), sem12, 16)
            w12_fill[blk] = P.cnt[sem12]
            return blk

        pending = []
        for e_i in range(NE):
            for s in range(2):
                for M in range(16):
                    pending.append((e_i, s, M))
        pend_i = [0]
        def pump_w12():
            if pend_i[0] < len(pending):
                e_i, s, M = pending[pend_i[0]]
                pend_i[0] += 1
                return w12_dma(e_i, s, M)
            return None
        blk0 = pump_w12()
        blk1 = pump_w12()

        blkc = [blk0 if blk0 is not None else 0]
        blk_iter = [w12_seq[0] - 2]

        for e_i in range(NE):
            P.wait('g', 's_t', P.cnt['s_t'])
            P.do('g', lambda e, e_i=e_i: e.dma_start(
                out=w3sb[:],
                in_=w3_d[e_i].rearrange("kh kp m -> kp kh m")), 'd_w3', 16)
            w3_target = P.cnt['d_w3']
            for s in range(2):
                if e_i < 8:
                    bank_wait('t', 4)
                    bank_wait('t', 5)
                    P.wait('t', 's_v', gate_done_v)
                    P.do('t', lambda e, e_i=e_i, s=s: e.matmul(
                        PS[:, 2048:2048 + 512], ind8_s[:, e_i * 128:(e_i + 1) * 128],
                        Wrow[:, s * 1024: s * 1024 + 512], start=True, stop=True), 's_t')
                    wbm = P.do('t', lambda e, e_i=e_i, s=s: e.matmul(
                        PS[:, 2560:2560 + 512], ind8_s[:, e_i * 128:(e_i + 1) * 128],
                        Wrow[:, s * 1024 + 512: (s + 1) * 1024], start=True, stop=True), 's_t')
                    P.wait('v', 's_t', wbm)
                    wbe = P.do('v', lambda e: e.tensor_copy(Wb[:], PS[:, 2048:3072]), 's_v')
                    free_bank[4] = ('s_v', wbe)
                    free_bank[5] = ('s_v', wbe)
                else:
                    wbe = None
                for M in range(16):
                    blk = blk_iter[0]
                    blk_iter[0] += 1
                    slot12 = blk % 2
                    sem12 = 'd_w12a' if slot12 == 0 else 'd_w12b'
                    sA = (2 * M) % 3
                    sB = (2 * M + 1) % 3
                    P.wait('t', sem12, w12_fill[blk])
                    if M == 0:
                        P.wait('t', 's_v', hv2)
                        if wbe is not None:
                            P.wait('t', 's_v', wbe)
                        P.wait('t', 's_v', og_free_v)
                    mh1 = mh2 = None
                    for half, bki in ((0, 2 * sA), (1, 2 * sA + 1)):
                        bank_wait('t', bki)
                        for K in range(8):
                            mh1 = P.do('t', lambda e, K=K, half=half, bki=bki, slot12=slot12, s=s: e.matmul(
                                bank(bki),
                                w12sb[:, slot12 * 2048 + K * 128:slot12 * 2048 + (K + 1) * 128],
                                h_buf[:, K * 2048 + s * 1024 + half * 512: K * 2048 + s * 1024 + (half + 1) * 512],
                                start=(K == 0), stop=(K == 7)), 's_t')
                    for half, bki in ((0, 2 * sB), (1, 2 * sB + 1)):
                        bank_wait('t', bki)
                        for K in range(8):
                            mh2 = P.do('t', lambda e, K=K, half=half, bki=bki, slot12=slot12, s=s: e.matmul(
                                bank(bki),
                                w12sb[:, slot12 * 2048 + 1024 + K * 128:slot12 * 2048 + 1024 + (K + 1) * 128],
                                h_buf[:, K * 2048 + s * 1024 + half * 512: K * 2048 + s * 1024 + (half + 1) * 512],
                                start=(K == 0), stop=(K == 7)), 's_t')
                    w12_done[blk] = P.cnt['s_t']
                    pump_w12()
                    P.wait('s', 's_t', mh1)
                    if glast is not None:
                        P.wait('s', 's_v', glast)
                    si = P.do('s', lambda e, sA=sA: e.activation(
                        tmp_si[:], PS[:, 2 * sA * 512: 2 * sA * 512 + 1024], AF.Silu), 's_s')
                    free_bank[2 * sA] = ('s_s', si)
                    free_bank[2 * sA + 1] = ('s_s', si)
                    P.wait('v', 's_t', mh2)
                    P.wait('v', 's_s', si)
                    siop = tmp_si
                    if e_i < 8:
                        P.vdep()
                        P.do('v', lambda e: e.tensor_mul(tmp_si2[:], tmp_si[:], Wb[:]), 's_v')
                        P.vdep()
                        siop = tmp_si2
                    gv = P.do('v', lambda e, sB=sB, M=M, siop=siop: e.tensor_mul(
                        og[:, M * 1024:(M + 1) * 1024], siop[:],
                        PS[:, 2 * sB * 512: 2 * sB * 512 + 1024]), 's_v')
                    free_bank[2 * sB] = ('s_v', gv)
                    free_bank[2 * sB + 1] = ('s_v', gv)
                    glast = gv
                P.wait('t', 'd_w3', w3_target)
                P.wait('t', 's_v', glast)
                for t in range(8):
                    tt = 8 * s + t
                    b0k, b1k = 2 * (t % 2), 2 * (t % 2) + 1
                    bank_wait('t', b0k)
                    bank_wait('t', b1k)
                    mm = None
                    for K in range(16):
                        for ncx in range(2):
                            mm = P.do('t', lambda e, K=K, t=t, b0k=b0k, ncx=ncx: e.matmul(
                                PS[:, b0k * 512 + ncx * 512: b0k * 512 + (ncx + 1) * 512],
                                og[:, K * 1024 + t * 128: K * 1024 + (t + 1) * 128],
                                w3sb[:, K * 1024 + ncx * 512: K * 1024 + (ncx + 1) * 512],
                                start=(K == 0), stop=(K == 15)), 's_t')
                    P.wait('v', 's_t', mm)
                    P.wait('v', 's_v', gb_ev)
                    P.do('v', lambda e, b0k=b0k: e.tensor_mul(
                        tmp_gb[:], PS[:, b0k * 512: b0k * 512 + 1024], S3[:]), 's_v')
                    P.vdep()
                    ev = P.do('v', lambda e, tt=tt: e.tensor_add(
                        x_buf[:, tt * 1024:(tt + 1) * 1024],
                        x_buf[:, tt * 1024:(tt + 1) * 1024], tmp_gb[:]), 's_v')
                    ffn_evac[(b, s, t)] = ev
                    free_bank[b0k] = ('s_v', ev)
                    free_bank[b1k] = ('s_v', ev)
        if not _before(upto, b, "out"):
            break

        for tt in range(NT):
            P.wait('y', 's_v', ffn_evac[(b, tt // 8, tt % 8)])
            P.do('y', lambda e, b=b, tt=tt: e.dma_start(
                out=y_d[b, tt * 128:(tt + 1) * 128, :],
                in_=x_buf[:, tt * 1024:(tt + 1) * 1024]), 'd_xo', 16)
        out_dma_total = P.cnt['d_xo']

    P.wait('y', 'd_xo', P.cnt.get('d_xo', 0))
    P.wait('y', 'd_db', P.cnt.get('d_db', 0))

    semnames = ['s_t', 's_v', 's_s', 'd_xsa', 'd_xsb', 'd_xo', 'd_kr', 'd_wq',
                'd_w12a', 'd_w12b', 'd_w3', 'd_wm', 'd_wmb', 'd_misc', 'd_db']
    with contextlib.ExitStack() as stack:
        stack.enter_context(nc.allow_low_precision(
            reason="bf16 matmul operands / normalizers; fp32 accumulation kept"))
        for s in semnames:
            P.sem[s] = stack.enter_context(nc.semaphore(s))
        block = stack.enter_context(nc.Block())
        block.gpsimd(lambda e: [f(e) for f in P.ops['g']])
        block.scalar(lambda e: [f(e) for f in P.ops['s']])
        block.tensor(lambda e: [f(e) for f in P.ops['t']])
        block.vector(lambda e: [f(e) for f in P.ops['v']])
        block.sync(lambda e: [f(e) for f in P.ops['y']])
    return nc, dbg


def kernel(**inputs):
    lay = host_layouts(inputs)
    x = np.ascontiguousarray(np.asarray(inputs["L0_attn_x"], dtype=np.float32))
    cc = np.ascontiguousarray(np.asarray(inputs["L0_c"], dtype=np.float32))
    nc, dbg = build_program(taps=tuple(_TAPS), upto=_UPTO)
    in_maps = []
    for c in range(NCORES):
        in_maps.append({
            "x": np.ascontiguousarray(x[2 * c:2 * c + 2]),
            "cvec": np.ascontiguousarray(cc[2 * c:2 * c + 2]),
            "wqkv": lay["wqkv"], "wout": lay["wout"], "gatew": lay["gate"],
            "wmod": lay["wmod"], "w12": lay["w12"], "w3": lay["w3"],
        })
    res = run_bass_kernel_spmd(nc, in_maps, core_ids=list(range(NCORES)))
    out = np.empty((16, T, D), dtype=np.float32)
    for c in range(NCORES):
        out[2 * c:2 * c + 2] = res.results[c]["y"]
    if _TAPS:
        kernel._dbg = [{k: r["dbg_" + k] for k in _TAPS} for r in res.results]
    return out


# version 02
# speedup vs baseline: 1.0217x; 1.0217x over previous
import contextlib
import math

import numpy as np
import ml_dtypes

import concourse.bass as bass
import concourse.mybir as mybir
from concourse.bass_utils import run_bass_kernel_spmd

F32 = mybir.dt.float32
BF16 = mybir.dt.bfloat16
AF = mybir.ActivationFunctionType
ALU = mybir.AluOpType
AXX = mybir.AxisListType.X
BFD = ml_dtypes.bfloat16

NCORES = 8
NB = 2
T = 2048
D = 1024
NT = 16
NH = 16
NE = 9
BSHIFT = 12.0

_UPTO = None
_TAPS = []


def _slopes():
    start = 2 ** (-2 ** (-(math.log2(NH) - 3)))
    return np.array([start * start**i for i in range(NH)], dtype=np.float32)


def host_layouts(inp):
    f32 = np.float32
    Wqkv = np.ascontiguousarray(inp["L0_attn_W_qkv"], dtype=f32)
    Wout = np.ascontiguousarray(inp["L0_attn_W_out"], dtype=f32)
    gw = np.ascontiguousarray(inp["L0_ffn_gate_w"], dtype=f32)
    ew1 = np.asarray(inp["L0_ffn_ew1"], dtype=f32)
    ew2 = np.asarray(inp["L0_ffn_ew2"], dtype=f32)
    ew3 = np.asarray(inp["L0_ffn_ew3"], dtype=f32)
    sw1 = np.asarray(inp["L0_ffn_sw1"], dtype=f32)
    sw2 = np.asarray(inp["L0_ffn_sw2"], dtype=f32)
    sw3 = np.asarray(inp["L0_ffn_sw3"], dtype=f32)
    wmod = np.ascontiguousarray(inp["L0_wmod"], dtype=f32)

    wqkv_l = np.ascontiguousarray(
        Wqkv.reshape(1024, 3, 8, 128).transpose(2, 0, 1, 3).reshape(8, 1024, 384))
    wout_l = np.ascontiguousarray(Wout.reshape(8, 128, 1024))
    gate_l = np.ascontiguousarray(gw.reshape(8, 128, 8))
    wmod_l = np.ascontiguousarray(wmod.reshape(8, 128, 6144))
    w1 = np.concatenate([ew1, sw1[None]], axis=0)
    w2 = np.concatenate([ew2, sw2[None]], axis=0)
    w3 = np.concatenate([ew3, sw3[None]], axis=0)
    t1 = w1.reshape(9, 8, 128, 16, 128).transpose(0, 3, 2, 1, 4).reshape(9, 16, 128, 1024)
    t2 = w2.reshape(9, 8, 128, 16, 128).transpose(0, 3, 2, 1, 4).reshape(9, 16, 128, 1024)
    w12_l = np.ascontiguousarray(np.concatenate([t1, t2], axis=3))
    w3_l = np.ascontiguousarray(w3.reshape(9, 16, 128, 1024))
    return dict(wqkv=wqkv_l, wout=wout_l, gate=gate_l, wmod=wmod_l,
                w12=w12_l, w3=w3_l)


def host_consts():
    sl = _slopes()
    k = np.arange(T, dtype=np.float32)
    bias = -sl[:, None] * k[None, :]
    bhi = bias.astype(BFD).astype(np.float32)
    blo = (bias - bhi).astype(BFD)
    qrow = (sl[:, None] * k[None, :]).astype(np.float32)
    ones = np.ones((16, T), np.float32)
    zero = np.zeros((16, T), np.float32)
    khlA = np.stack([bhi, blo.astype(np.float32), ones, zero], 1).astype(BFD)
    qA = np.stack([ones, ones, qrow, zero], 1).astype(BFD)
    khlB = np.stack([ones, bhi, blo.astype(np.float32), zero], 1).astype(BFD)
    qB = np.stack([qrow, ones, ones, zero], 1).astype(BFD)
    identb = np.eye(128, dtype=BFD)
    identf = np.eye(128, dtype=np.float32)
    kk = np.arange(128)
    tri = np.where(kk[:, None] >= kk[None, :], 0.0, -1e30).astype(BFD)
    ind32 = ((kk[:, None] == 32) * np.ones((1, 128))).astype(BFD)
    ind64 = ((kk[:, None] == 64) * np.ones((1, 128))).astype(BFD)
    indat = np.ascontiguousarray(np.concatenate([ind32, ind64], axis=1))
    e8 = np.arange(8)
    ind8 = np.repeat((e8[:, None, None] == e8[None, :, None]), 128, axis=2)
    ind8 = np.ascontiguousarray(ind8.reshape(8, 1024).astype(BFD))
    cvals = np.ascontiguousarray(
        np.tile(np.array([[1e-5, -BSHIFT]], np.float32), (128, 1)))
    return dict(khlA=np.ascontiguousarray(khlA), qA=np.ascontiguousarray(qA),
                khlB=np.ascontiguousarray(khlB), qB=np.ascontiguousarray(qB),
                identb=identb, identf=identf, tri=tri, indat=indat, ind8=ind8,
                cvals=cvals)


class Prog:
    def __init__(self):
        self.ops = {k: [] for k in "tvsgy"}
        self.cnt = {}
        self.sem = {}
        self.mark = {}

    def raw(self, eng, fn):
        self.ops[eng].append(fn)

    def wait(self, eng, sem, val=None):
        v = self.cnt.get(sem, 0) if val is None else val
        assert v <= self.cnt.get(sem, 0), f"wait on future event {sem} {v} > {self.cnt.get(sem, 0)}"
        if v > 0:
            self.raw(eng, lambda e, s=sem, vv=v: e.wait_ge(self.sem[s], vv))

    def vdep(self):
        self.wait('v', 's_v')

    def do(self, eng, fn, inc=None, by=1):
        if inc is None:
            self.raw(eng, fn)
            return None
        self.cnt[inc] = self.cnt.get(inc, 0) + by
        def run(e, fn=fn, s=inc, b=by):
            fn(e).then_inc(self.sem[s], b)
        self.raw(eng, run)
        return self.cnt[inc]


STAGES = ["ln1", "attn", "oproj", "ln2", "gate", "ffn", "out"]


def _before(upto, b, stage):
    if upto is None:
        return True
    ub, us = upto
    if b != ub:
        return b < ub
    return STAGES.index(stage) <= STAGES.index(us)


def build_program(taps=(), upto=None):
    nc = bass.Bass("TRN2", target_bir_lowering=False)
    C = host_consts()

    x_d = nc.dram_tensor("x", [NB, T, D], F32, kind="ExternalInput").ap()
    c_d = nc.dram_tensor("cvec", [NB, D], F32, kind="ExternalInput").ap()
    wqkv_d = nc.dram_tensor("wqkv", [8, 1024, 384], F32, kind="ExternalInput").ap()
    wout_d = nc.dram_tensor("wout", [8, 128, 1024], F32, kind="ExternalInput").ap()
    gate_d = nc.dram_tensor("gatew", [8, 128, 8], F32, kind="ExternalInput").ap()
    wmod_d = nc.dram_tensor("wmod", [8, 128, 6144], F32, kind="ExternalInput").ap()
    w12_d = nc.dram_tensor("w12", [9, 16, 128, 2048], F32, kind="ExternalInput").ap()
    w3_d = nc.dram_tensor("w3", [9, 16, 128, 1024], F32, kind="ExternalInput").ap()
    y_d = nc.dram_tensor("y", [NB, T, D], F32, kind="ExternalOutput").ap()

    khlA_d = nc.inline_tensor(C["khlA"], name="khlA").ap()
    qA_d = nc.inline_tensor(C["qA"], name="qA").ap()
    khlB_d = nc.inline_tensor(C["khlB"], name="khlB").ap()
    qB_d = nc.inline_tensor(C["qB"], name="qB").ap()
    identb_d = nc.inline_tensor(C["identb"], name="identb").ap()
    identf_d = nc.inline_tensor(C["identf"], name="identf").ap()
    tri_d = nc.inline_tensor(C["tri"], name="tri").ap()
    indat_d = nc.inline_tensor(C["indat"], name="indat").ap()
    ind8_d = nc.inline_tensor(C["ind8"], name="ind8").ap()
    cvals_d = nc.inline_tensor(C["cvals"], name="cvals").ap()

    dbg = {}
    def tap(name, shape, dtype):
        if name in taps:
            dbg[name] = nc.dram_tensor("dbg_" + name, shape, dtype,
                                       kind="ExternalOutput").ap()
            return True
        return False

    ARENA = nc.sbuf_bytes_remaining - 2048
    base = nc.bump_sbuf(ARENA)[0]
    cur = [base]
    def alloc(name, shape, dtype):
        a = (cur[0] + 31) & ~31
        t = nc.alloc_sbuf_tensor_at(name, shape, dtype, offset=a)
        cur[0] = a + shape[1] * {F32: 4, BF16: 2}[dtype]
        return t

    x_buf = alloc("x_buf", [128, NT * 1024], F32)
    XOFF = nc.lookup_mloc(x_buf).addr
    h_buf = alloc("h_buf", [128, 8 * 2048], BF16)
    og = alloc("og", [128, 16384], BF16)
    w3sb = alloc("w3sb", [128, 16 * 1024], BF16)
    w12sb = alloc("w12sb", [128, 2 * 2048], BF16)
    Wb = alloc("Wb", [128, 1024], BF16)
    tmp_si = alloc("tmp_si", [128, 1024], BF16)
    tmp_gb = alloc("tmp_gb", [128, 1024], BF16)
    S1 = alloc("S1", [128, 1024], BF16)
    S2 = alloc("S2", [128, 1024], BF16)
    S3 = alloc("S3", [128, 1024], BF16)
    Wrow = alloc("Wrow", [8, 2048], BF16)
    Wg = alloc("Wg", [128, 128], F32)
    gatewsb = alloc("gatewsb", [128, 64], BF16)
    modsb = alloc("modsb", [128, 96], F32)
    dpb = alloc("dpb", [128, 96], BF16)
    gtmp = alloc("gtmp", [128, 32], F32)

    cfmb = alloc("cfmb", [128, 16], BF16)
    rowt = alloc("rowt", [1, 1024], BF16)
    identb_s = alloc("identb_s", [128, 128], BF16)
    identf_s = alloc("identf_s", [128, 128], F32)
    tri_s = alloc("tri_s", [128, 128], BF16)
    indat_s = alloc("indat_s", [128, 256], BF16)
    ind8_s = alloc("ind8_s", [8, 1024], BF16)
    onesb = alloc("onesb", [1, 128], BF16)
    s_sums = alloc("s_sums", [128, 512], BF16)
    sq_tmp = alloc("sq_tmp", [128, 1024], F32)
    csil = nc.alloc_sbuf_tensor_at("csil", [2, 1024], F32,
                                   offset=nc.lookup_mloc(sq_tmp).addr)
    xnb = alloc("xnb", [128, 1024], BF16)
    stats_a = alloc("stats_a", [128, 4], F32)
    stats_b = alloc("stats_b", [128, 4], F32)
    stats_c = alloc("stats_c", [128, 4], F32)
    gA = alloc("gA", [128, 16], F32)
    gB = alloc("gB", [128, 16], F32)
    gC = alloc("gC", [128, 16], F32)
    gtmp2 = alloc("gtmp2", [128, 32], F32)
    xnb2 = alloc("xnb2", [128, 1024], BF16)
    tmp_si2 = alloc("tmp_si2", [128, 1024], BF16)
    cvals_s = alloc("cvals_s", [128, 2], F32)
    nc.const_aps.register(F32, 1e-5, cvals_s[:, 0:1])
    nc.const_aps.register(F32, -BSHIFT, cvals_s[:, 1:2])
    assert cur[0] <= base + ARENA, f"sbuf over: {cur[0] - base} > {ARENA}"

    Qta = nc.alloc_sbuf_tensor_at("Qta", [128, 2048], BF16, offset=XOFF + 0)
    Qtb = nc.alloc_sbuf_tensor_at("Qtb", [128, 2048], BF16, offset=XOFF + 4096)
    Kta = nc.alloc_sbuf_tensor_at("Kta", [128, 2048], BF16, offset=XOFF + 8192)
    Ktb = nc.alloc_sbuf_tensor_at("Ktb", [128, 2048], BF16, offset=XOFF + 12288)
    Vab = nc.alloc_sbuf_tensor_at("Vab", [128, 16 * 196], BF16, offset=XOFF + 16384)

    wqkvsb = nc.alloc_sbuf_tensor_at("wqkvsb", [128, 3072], BF16, offset=XOFF + 24768)
    xstg = nc.alloc_sbuf_tensor_at("xstg", [128, 2048], F32, offset=XOFF + 57344)
    W3OFF = nc.lookup_mloc(w3sb).addr
    Pt = nc.alloc_sbuf_tensor_at("Pt", [128, 2048], BF16, offset=W3OFF + 16384)
    csb = nc.alloc_sbuf_tensor_at("csb", [2, 1024], F32, offset=XOFF + 40960)

    PS = nc.alloc_psum_tensor("PS", [128, 3072], F32)
    PSb = nc.alloc_psum_tensor("PSb", [128, 1024], BF16)
    PSb2 = nc.alloc_psum_tensor("PSb2", [128, 1024], BF16)
    def bank(i, w=512):
        return PS[:, i * 512:i * 512 + w]

    P = Prog()
    free_bank = {}

    def bank_wait(eng, *banks):
        for bk in banks:
            ev = free_bank.pop(bk, None)
            if ev is not None:
                P.wait(eng, ev[0], ev[1])

    for dst, src in [(identb_s, identb_d), (identf_s, identf_d), (tri_s, tri_d),
                     (indat_s, indat_d), (ind8_s, ind8_d)]:
        P.do('g', lambda e, d=dst, s=src: e.dma_start(out=d[:], in_=s), 'd_misc', 16)
    P.do('g', lambda e: e.dma_start(out=gatewsb[:],
         in_=gate_d.rearrange("kc kp e -> kp kc e")), 'd_misc', 16)
    P.do('g', lambda e: e.dma_start(out=csb[:], in_=c_d), 'd_misc', 16)
    P.do('g', lambda e: e.dma_start(out=cvals_s[:], in_=cvals_d), 'd_misc', 16)
    misc_total = P.cnt['d_misc']
    P.do('v', lambda e: e.memset(onesb[:], 1.0), 's_v')
    P.do('v', lambda e: e.memset(s_sums[:], 1.0), 's_v')

    P.wait('s', 'd_misc', misc_total)
    P.do('s', lambda e: e.activation(csil[:], csb[:], AF.Silu), 's_s')
    P.wait('t', 'd_misc', misc_total)
    P.wait('t', 's_s')
    cfm_ev = None
    for kc in range(8):
        if cfm_ev is not None:
            P.wait('t', 's_v', cfm_ev)
        tr = P.do('t', lambda e, kc=kc: e.transpose(
            PS[:, 5 * 512: 5 * 512 + 2],
            csil[0:2, kc * 128:(kc + 1) * 128], identf_s[0:2, 0:2]), 's_t')
        P.wait('v', 's_t', tr)
        cfm_ev = P.do('v', lambda e, kc=kc: e.tensor_copy(
            cfmb[:, kc * 2:(kc + 1) * 2], PS[:, 5 * 512:5 * 512 + 2]), 's_v')

    wm_done = {}
    def emit_wmod_dma(kc):
        if kc - 2 in wm_done:
            P.wait('g', 's_t', wm_done[kc - 2])
        sem = 'd_wm' if kc % 2 == 0 else 'd_wmb'
        P.do('g', lambda e, kc=kc: e.dma_start(
            out=h_buf[:, (kc % 2) * 6144:((kc % 2) + 1) * 6144], in_=wmod_d[kc]),
            sem, 16)
        P.mark[('wm', kc)] = (sem, P.cnt[sem])
    emit_wmod_dma(0)
    emit_wmod_dma(1)
    P.wait('t', 's_v', cfm_ev)
    for kc in range(8):
        P.wait('t', *P.mark[('wm', kc)])
        for Mc in range(48):
            P.do('t', lambda e, kc=kc, Mc=Mc: e.matmul(
                PS[:, 4 * 512 + Mc * 2: 4 * 512 + Mc * 2 + 2],
                h_buf[:, (kc % 2) * 6144 + Mc * 128: (kc % 2) * 6144 + (Mc + 1) * 128],
                cfmb[:, kc * 2:(kc + 1) * 2],
                start=(kc == 0 and Mc == 0), stop=(kc == 7 and Mc == 47),
                skip_group_check=True), 's_t')
        wm_done[kc] = P.cnt['s_t']
        if kc + 2 < 8:
            emit_wmod_dma(kc + 2)
    P.wait('v', 's_t')
    P.do('v', lambda e: e.tensor_copy(modsb[:], PS[:, 4 * 512:4 * 512 + 96]), 's_v')
    free_bank[4] = ('s_v', P.cnt['s_v'])
    free_bank[5] = ('s_v', P.cnt['s_v'])

    for sp in range(6):
        for bi in range(NB):
            sl = slice(sp * 16 + bi, sp * 16 + 16, 2)
            src, dst = modsb[:, sl], dpb[:, sl]
            if sp in (1, 4):
                P.do('v', lambda e, s=src, d=dst: e.tensor_scalar_add(d, s, 1.0), 's_v')
            elif sp in (0, 3):
                P.do('v', lambda e, s=src, d=dst: e.tensor_copy(d, s), 's_v')
            else:
                p_, n_, t_ = gtmp[:, 0:8], gtmp[:, 8:16], gtmp[:, 16:24]
                m_, d_ = gtmp2[:, 0:8], gtmp2[:, 8:16]
                P.do('v', lambda e, s=src, p=p_: e.tensor_scalar_max(p, s, 0.25), 's_v')
                P.do('v', lambda e, s=src, n=n_: e.tensor_scalar_min(n, s, -0.25), 's_v')
                P.do('v', lambda e, s=src, m=m_: e.tensor_single_scalar(m, s, 0.0, ALU.is_ge), 's_v')
                P.vdep()
                P.do('v', lambda e, p=p_, n=n_, d=d_: e.tensor_sub(d, p, n), 's_v')
                P.vdep()
                P.do('v', lambda e, m=m_, d=d_, t=t_: e.tensor_mul(t, m, d), 's_v')
                P.vdep()
                P.do('v', lambda e, n=n_, t=t_, ds=dst: e.tensor_add(ds, n, t), 's_v')
    mod_done_v = P.cnt['s_v']

    if tap("mod", [128, 96], F32):
        P.wait('y', 's_v', mod_done_v)
        P.do('y', lambda e: e.dma_start(out=dbg["mod"], in_=modsb[:]), 'd_db', 16)
    if tap("dpb", [128, 96], BF16):
        P.wait('y', 's_v', mod_done_v)
        P.do('y', lambda e: e.dma_start(out=dbg["dpb"], in_=dpb[:]), 'd_db', 16)

    def build_bcast(sp, bi, dst):
        bank_wait('t', 4)
        bank_wait('t', 5)
        P.wait('t', 's_v', mod_done_v)
        last = None
        for fc in range(8):
            col = sp * 16 + fc * 2 + bi
            last = P.do('t', lambda e, col=col, fc=fc: e.matmul(
                PS[0:1, 2048 + fc * 128:2048 + (fc + 1) * 128],
                dpb[:, col:col + 1], identb_s[:],
                start=(fc % 4 == 0), stop=(fc % 4 == 3),
                skip_group_check=True), 's_t')
        P.wait('v', 's_t', last)
        cp = P.do('v', lambda e: e.tensor_copy(rowt[0:1, :], PS[0:1, 2048:3072]), 's_v')
        P.wait('t', 's_v', cp)
        for nck in range(2):
            last = P.do('t', lambda e, nck=nck: e.matmul(
                PS[:, 2048 + nck * 512:2048 + (nck + 1) * 512],
                onesb[0:1, 0:128], rowt[0:1, nck * 512:(nck + 1) * 512],
                start=True, stop=True), 's_t')
        P.wait('v', 's_t', last)
        ev = P.do('v', lambda e, d=dst: e.tensor_copy(d[:], PS[:, 2048:3072]), 's_v')
        free_bank[4] = ('s_v', ev)
        free_bank[5] = ('s_v', ev)
        return ev

    if "bct" in taps:
        dbg["bct"] = nc.dram_tensor("dbg_bct", [128, 2048], BF16, kind="ExternalOutput").ap()
        build_bcast(1, 0, S1)
        bev = build_bcast(0, 0, S2)
        P.wait('y', 's_v', bev)
        P.do('y', lambda e: e.dma_start(out=dbg["bct"][:, 0:1024], in_=S1[:]), 'd_db', 16)
        P.do('y', lambda e: e.dma_start(out=dbg["bct"][:, 1024:2048], in_=S2[:]), 'd_db', 16)

    def ln_phase(b, which):
        sev = build_bcast(4 if which == 2 else 1, b, S1)
        if "s1early" in taps and b == 0 and which == 1:
            if "s1early" not in dbg:
                dbg["s1early"] = nc.dram_tensor("dbg_s1early", [128, 1024], BF16,
                                                kind="ExternalOutput").ap()
            P.wait('y', 's_v', sev)
            P.do('y', lambda e: e.dma_start(out=dbg["s1early"], in_=S1[:]), 'd_db', 16)
        ev2 = build_bcast(3 if which == 2 else 0, b, S2)
        P.wait('y', 's_v', ev2)
        P.wait('s', 's_v')
        apply_done = {}
        red_done = {}

        def stage_dma(t):
            slot = t % 2
            sem = 'd_xsa' if slot == 0 else 'd_xsb'
            if t - 2 in apply_done:
                P.wait('y', 's_v', apply_done[t - 2])
            P.do('y', lambda e, b=b, t=t, slot=slot: e.dma_start(
                out=xstg[:, slot * 1024:(slot + 1) * 1024],
                in_=x_d[b, t * 128:(t + 1) * 128, :]), sem, 16)
            P.mark[('lnx', b, which, t)] = P.cnt[sem]

        if which == 1:
            stage_dma(0)
            stage_dma(1)
        for t in range(NT):
            if which == 1:
                slot = t % 2
                sem = 'd_xsa' if slot == 0 else 'd_xsb'
                P.wait('s', sem, P.mark[('lnx', b, which, t)])
                P.wait('v', sem, P.mark[('lnx', b, which, t)])
                xt = xstg[:, slot * 1024:(slot + 1) * 1024]
            else:
                xt = x_buf[:, t * 1024:(t + 1) * 1024]
            if t - 1 in red_done:
                P.wait('s', 's_v', red_done[t - 1])
            sq = P.do('s', lambda e, xt=xt: e.activation(sq_tmp[:], xt, AF.Square), 's_s')
            P.do('v', lambda e, xt=xt: e.reduce_sum(stats_a[:, 0:1], xt, axis=AXX, negate=True), 's_v')
            P.wait('v', 's_s', sq)
            rd = P.do('v', lambda e: e.reduce_sum(stats_a[:, 1:2], sq_tmp[:], axis=AXX), 's_v')
            red_done[t] = rd
            P.vdep()
            P.do('v', lambda e: e.tensor_scalar_mul(stats_b[:, 0:1], stats_a[:, 0:1], 1.0 / D), 's_v')
            P.do('v', lambda e: e.tensor_scalar_mul(stats_b[:, 1:2], stats_a[:, 1:2], 1.0 / D), 's_v')
            P.vdep()
            P.do('v', lambda e: e.tensor_mul(stats_c[:, 0:1], stats_b[:, 0:1], stats_b[:, 0:1]), 's_v')
            P.vdep()
            vv = P.do('v', lambda e: e.tensor_sub(stats_a[:, 2:3], stats_b[:, 1:2], stats_c[:, 0:1]), 's_v')
            P.wait('s', 's_v', vv)
            st = P.do('s', lambda e: e.activation(stats_b[:, 2:3], stats_a[:, 2:3], AF.Sqrt, bias=1e-5), 's_s')
            P.wait('v', 's_s', st)
            P.do('v', lambda e: e.reciprocal(stats_c[:, 1:2], stats_b[:, 2:3]), 's_v')
            P.vdep()
            nm = P.do('v', lambda e: e.tensor_mul(stats_a[:, 3:4], stats_b[:, 0:1], stats_c[:, 1:2]), 's_v')
            P.vdep()
            ap = P.do('v', lambda e, xt=xt: e.tensor_scalar(
                xnb[:], xt, stats_c[:, 1:2], stats_a[:, 3:4], ALU.mult, ALU.add), 's_v')
            apply_done[t] = ap
            if which == 1 and t + 2 < NT:
                stage_dma(t + 2)
            P.wait('v', 's_v', ev2)
            P.vdep()
            P.do('v', lambda e: e.tensor_mul(xnb2[:], xnb[:], S1[:]), 's_v')
            P.vdep()
            md = P.do('v', lambda e: e.tensor_add(xnb[:], xnb2[:], S2[:]), 's_v')
            P.wait('t', 's_v', md)
            for j in range(8):
                pb = PSb if j % 2 == 0 else PSb2
                ev = free_bank.pop(('tr', j % 2), None)
                if ev is not None:
                    P.wait('t', ev[0], ev[1])
                tr = P.do('t', lambda e, pb=pb, j=j: e.transpose(
                    pb[:, 0:128], xnb[:, j * 128:(j + 1) * 128], identb_s[:]), 's_t')
                P.wait('v', 's_t', tr)
                cp = P.do('v', lambda e, pb=pb, j=j, t=t: e.tensor_copy(
                    h_buf[:, j * 2048 + t * 128: j * 2048 + (t + 1) * 128],
                    pb[:, 0:128]), 's_v')
                free_bank[('tr', j % 2)] = ('s_v', cp)
        return P.cnt['s_v']

    ffn_evac = {}
    out_dma_total = 0
    w12_seq = [0]

    for b in range(NB):
        if not _before(upto, b, "ln1"):
            break
        if b == 1:
            for eng in "tvsgy":
                P.wait(eng, 'd_xo', out_dma_total)

        hv = ln_phase(b, 1)
        if b == 0 and tap("S1", [128, 1024], BF16):
            P.wait('y', 's_v')
            P.do('y', lambda e: e.dma_start(out=dbg["S1"], in_=S1[:]), 'd_db', 16)
        if b == 0 and tap("S2", [128, 1024], BF16):
            P.wait('y', 's_v')
            P.do('y', lambda e: e.dma_start(out=dbg["S2"], in_=S2[:]), 'd_db', 16)
        if b == 0 and tap("rowt", [1, 1024], BF16):
            P.wait('y', 's_v')
            P.do('y', lambda e: e.dma_start(out=dbg["rowt"], in_=rowt[:]), 'd_db', 16)
        if b == 0 and tap("xnb", [128, 1024], BF16):
            P.wait('y', 's_v')
            P.do('y', lambda e: e.dma_start(out=dbg["xnb"], in_=xnb[:]), 'd_db', 16)
        if b == 0 and tap("sts", [128, 12], F32):
            P.wait('y', 's_v')
            P.do('y', lambda e: e.dma_start(out=dbg["sts"][:, 0:4], in_=stats_a[:]), 'd_db', 16)
            P.do('y', lambda e: e.dma_start(out=dbg["sts"][:, 4:8], in_=stats_b[:]), 'd_db', 16)
            P.do('y', lambda e: e.dma_start(out=dbg["sts"][:, 8:12], in_=stats_c[:]), 'd_db', 16)
        if b == 0 and tap("ln1", [128, 16384], BF16):
            P.wait('y', 's_v', hv)
            P.do('y', lambda e: e.dma_start(out=dbg["ln1"], in_=h_buf[:]), 'd_db', 16)
        if not _before(upto, b, "attn"):
            break

        for buf in (Qta, Qtb, Kta, Ktb):
            P.do('v', lambda e, bf=buf: e.memset(bf[:], 0.0), 's_v')
        P.do('v', lambda e: e.memset(Vab[:], 0.0), 's_v')
        P.do('v', lambda e: e.memset(Vab[:, 64:16 * 196:196], 1.0), 's_v')
        ms_done = P.do('v', lambda e: e.memset(Vab[:, 100:16 * 196:196], 1.0), 's_v')

        if b == 1:
            P.wait('g', 's_t', P.cnt['s_t'])
        P.do('g', lambda e: e.dma_start(
            out=w3sb[:, 0:8192],
            in_=wout_d.rearrange("kc kp m -> kp kc m")), 'd_w3', 16)
        wout_target = P.cnt['d_w3']

        def load_pair_tables(p):
            P.do('y', lambda e, p=p: e.dma_start(out=Kta[64:68, :], in_=khlA_d[2 * p]), 'd_kr', 16)
            P.do('y', lambda e, p=p: e.dma_start(out=Qta[64:68, :], in_=qA_d[2 * p]), 'd_kr', 16)
            P.do('y', lambda e, p=p: e.dma_start(out=Ktb[32:36, :], in_=khlB_d[2 * p + 1]), 'd_kr', 16)
            P.do('y', lambda e, p=p: e.dma_start(out=Qtb[32:36, :], in_=qB_d[2 * p + 1]), 'd_kr', 16)

        def load_pair_w(p):
            P.do('g', lambda e, p=p: e.dma_start(
                out=wqkvsb[:],
                in_=wqkv_d[p].rearrange("(kc kp) j -> kp kc j", kp=128)), 'd_wq', 16)

        P.wait('y', 's_v', ms_done)
        load_pair_tables(0)
        P.wait('g', 's_v', hv)
        load_pair_w(0)

        exp_bank = {}
        pv_slot = {0: 0, 1: 0, 2: 0, 3: 0}

        for p in range(8):
            P.wait('t', 'd_wq', 16 * (p + 1) + (16 * 8 * b))
            P.wait('t', 's_v', hv)
            for role, dstA, dstB, scale in ((0, Qta, Qtb, 0.125), (1, Kta, Ktb, 1.0)):
                for bki in range(4):
                    bank_wait('t', bki)
                if p == 0 and role == 0:
                    P.wait('t', 's_v', ms_done)
                mm = None
                for K in range(8):
                    for ncx in range(4):
                        mm = P.do('t', lambda e, K=K, ncx=ncx, role=role: e.matmul(
                            bank(ncx),
                            wqkvsb[:, K * 384 + role * 128: K * 384 + (role + 1) * 128],
                            h_buf[:, K * 2048 + ncx * 512: K * 2048 + (ncx + 1) * 512],
                            start=(K == 0), stop=(K == 7)), 's_t')
                for ncx in range(4):
                    P.wait('s', 's_t', mm)
                    P.do('s', lambda e, dstA=dstA, ncx=ncx, scale=scale: e.activation(
                        dstA[0:64, ncx * 512:(ncx + 1) * 512],
                        PS[0:64, ncx * 512:(ncx + 1) * 512], AF.Copy, scale=scale), 's_s')
                    ev = P.do('s', lambda e, dstB=dstB, ncx=ncx, scale=scale: e.activation(
                        dstB[64:128, ncx * 512:(ncx + 1) * 512],
                        PS[64:128, ncx * 512:(ncx + 1) * 512], AF.Copy, scale=scale), 's_s')
                    free_bank[ncx] = ('s_s', ev)
            for tt in range(NT):
                bki = 4 + tt % 2
                bank_wait('t', bki)
                mm = None
                for K in range(8):
                    mm = P.do('t', lambda e, K=K, tt=tt, bki=bki: e.matmul(
                        PS[:, bki * 512:bki * 512 + 128],
                        h_buf[:, K * 2048 + tt * 128: K * 2048 + (tt + 1) * 128],
                        wqkvsb[:, K * 384 + 256: K * 384 + 384],
                        start=(K == 0), stop=(K == 7)), 's_t')
                P.wait('v', 's_t', mm)
                if p == 0 and tt < 2:
                    P.wait('v', 's_v', ms_done)
                P.do('v', lambda e, tt=tt, bki=bki: e.tensor_copy(
                    Vab[:, tt * 196: tt * 196 + 64], PS[:, bki * 512:bki * 512 + 64]), 's_v')
                ev = P.do('v', lambda e, tt=tt, bki=bki: e.tensor_copy(
                    Vab[:, tt * 196 + 132: tt * 196 + 196],
                    PS[:, bki * 512 + 64:bki * 512 + 128]), 's_v')
                free_bank[bki] = ('s_v', ev)
            qkv_done_t = P.cnt['s_t']
            qkv_done_s = P.cnt['s_s']
            qkv_done_v = P.cnt['s_v']

            if p < 7:
                P.wait('g', 's_t', qkv_done_t)
                load_pair_w(p + 1)

            P.wait('t', 'd_kr', P.cnt['d_kr'])
            P.wait('t', 's_s', qkv_done_s)
            P.wait('t', 's_v', qkv_done_v)
            pend = [None]

            def emit_tail():
                if pend[0] is None:
                    return
                orow, orows, obank, pp, qq0, tsl, rc = pend[0]
                pend[0] = None
                P.wait('s', 's_t', P.mark[('pvlast', obank)])
                if ('ntail', tsl) in P.mark:
                    P.wait('s', 's_v', P.mark[('ntail', tsl)])
                oc = P.do('s', lambda e, orows=orows, obank=obank, tsl=tsl: e.activation(
                    tmp_gb[orows, tsl * 512:(tsl + 1) * 512],
                    PS[orows, obank * 512:(obank + 1) * 512], AF.Copy), 's_s')
                P.wait('t', 's_v', rc)
                P.wait('t', 's_s', oc)
                sb = P.do('t', lambda e, orow=orow, obank=obank: e.matmul(
                    bank(obank), indat_s[:, 128:256] if orow == 64 else indat_s[:, 0:128],
                    s_sums[:], start=True, stop=True), 's_t')
                P.wait('v', 's_t', sb)
                P.wait('v', 's_s', oc)
                nm = P.do('v', lambda e, orows=orows, obank=obank, pp=pp, qq0=qq0, tsl=tsl: e.tensor_mul(
                    og[orows, pp * 2048 + qq0: pp * 2048 + qq0 + 512],
                    tmp_gb[orows, tsl * 512:(tsl + 1) * 512],
                    PS[orows, obank * 512:(obank + 1) * 512]), 's_v')
                P.mark[('ntail', tsl)] = nm
                free_bank[obank] = ('s_v', nm)

            for hh in range(2):
                if hh == 0:
                    Qx, Kx, krows, orow = Qta, Kta, 128, 64
                    orows = slice(0, 64)
                    vlo, vhi = 0, 128
                else:
                    Qx, Kx, krows, orow = Qtb, Ktb, 128, 32
                    orows = slice(64, 128)
                    vlo, vhi = 68, 196
                sbanks = [0, 1, 4, 5]
                for qb in range(4):
                    q0 = qb * 512
                    kc0 = 4 * qb
                    oq = P.mark.get('oq', 0)
                    P.mark['oq'] = oq + 1
                    obank = 2 + oq % 2
                    tsl = oq % 2
                    bank_wait('t', obank)
                    pv = None
                    for kc in range(kc0, 16):
                        sbk = sbanks[kc % 4]
                        slot = kc % 4
                        bank_wait('t', sbk)
                        P.wait('t', 's_s', exp_bank.get(sbk, 0))
                        mmS = P.do('t', lambda e, Kx=Kx, kc=kc, q0=q0, krows=krows, sbk=sbk, Qx=Qx: e.matmul(
                            bank(sbk), Kx[0:krows, kc * 128:(kc + 1) * 128],
                            Qx[0:krows, q0:q0 + 512], start=True, stop=True), 's_t')
                        if kc == min(kc0 + 3, 15):
                            emit_tail()
                        diag = kc0 <= kc < kc0 + 4
                        r = kc * 128 - q0
                        vmask = None
                        if diag:
                            P.wait('v', 's_t', mmS)
                            vmask = P.do('v', lambda e, sbk=sbk, r=r: e.tensor_add(
                                PS[:, sbk * 512 + r: sbk * 512 + r + 128],
                                PS[:, sbk * 512 + r: sbk * 512 + r + 128], tri_s[:]), 's_v')
                        w = r + 128 if diag else 512
                        P.wait('s', 's_t', mmS)
                        if vmask is not None:
                            P.wait('s', 's_v', vmask)
                        P.wait('s', 's_t', pv_slot[slot])
                        ex = P.do('s', lambda e, slot=slot, sbk=sbk, w=w: e.activation(
                            Pt[:, slot * 512: slot * 512 + w],
                            PS[:, sbk * 512: sbk * 512 + w], AF.Exp, bias=-BSHIFT), 's_s')
                        exp_bank[sbk] = ex
                        pmem = None
                        if diag and w < 512:
                            P.wait('v', 's_t', pv_slot[slot])
                            pmem = P.do('v', lambda e, slot=slot, w=w: e.memset(
                                Pt[:, slot * 512 + w: (slot + 1) * 512], 0.0), 's_v')
                        P.wait('t', 's_s', ex)
                        if pmem is not None:
                            P.wait('t', 's_v', pmem)
                        pv = P.do('t', lambda e, kc=kc, slot=slot, obank=obank, vlo=vlo, vhi=vhi, kc0=kc0: e.matmul(
                            PS[0:(vhi - vlo), obank * 512:(obank + 1) * 512],
                            Vab[:, kc * 196 + vlo: kc * 196 + vhi],
                            Pt[:, slot * 512:(slot + 1) * 512],
                            start=(kc == kc0), stop=(kc == 15)), 's_t')
                        pv_slot[slot] = pv
                    P.mark[('pvlast', obank)] = pv
                    P.wait('v', 's_t', pv)
                    rc = P.do('v', lambda e, orow=orow, obank=obank: e.reciprocal(
                        s_sums[orow:orow + 1, :],
                        PS[orow:orow + 1, obank * 512:(obank + 1) * 512]), 's_v')
                    pend[0] = (orow, orows, obank, p, q0, tsl, rc)
            emit_tail()
            if p < 7:
                P.wait('y', 's_t', P.cnt['s_t'])
                load_pair_tables(p + 1)

        attn_done_v = P.cnt['s_v']
        if b == 0 and tap("og", [128, 16384], BF16):
            P.wait('y', 's_v', attn_done_v)
            P.do('y', lambda e: e.dma_start(out=dbg["og"], in_=og[:]), 'd_db', 16)
        if not _before(upto, b, "oproj"):
            break

        ga_ev = build_bcast(2, b, S3)
        P.wait('t', 'd_w3', wout_target)
        P.wait('t', 's_v', attn_done_v)
        evac_op = {}

        def oproj_dma(t):
            slot = t % 2
            sem = 'd_xsa' if slot == 0 else 'd_xsb'
            if t - 2 in evac_op:
                P.wait('y', 's_v', evac_op[t - 2])
            else:
                P.wait('y', 's_v', attn_done_v)
            P.do('y', lambda e, b=b, t=t, slot=slot: e.dma_start(
                out=xstg[:, slot * 1024:(slot + 1) * 1024],
                in_=x_d[b, t * 128:(t + 1) * 128, :]), sem, 16)
            P.mark[('opx', b, t)] = P.cnt[sem]

        oproj_dma(0)
        oproj_dma(1)
        for t in range(NT):
            slot = t % 2
            sem = 'd_xsa' if slot == 0 else 'd_xsb'
            b0k, b1k = 2 * (t % 2), 2 * (t % 2) + 1
            bank_wait('t', b0k)
            bank_wait('t', b1k)
            mm = None
            for K in range(8):
                for ncx in range(2):
                    mm = P.do('t', lambda e, K=K, t=t, b0k=b0k, ncx=ncx: e.matmul(
                        PS[:, b0k * 512 + ncx * 512: b0k * 512 + (ncx + 1) * 512],
                        og[:, K * 2048 + t * 128: K * 2048 + (t + 1) * 128],
                        w3sb[:, K * 1024 + ncx * 512: K * 1024 + (ncx + 1) * 512],
                        start=(K == 0), stop=(K == 7)), 's_t')
            P.wait('v', 's_t', mm)
            P.wait('v', sem, P.mark[('opx', b, t)])
            P.wait('v', 's_v', ga_ev)
            P.do('v', lambda e, b0k=b0k: e.tensor_mul(
                tmp_gb[:], PS[:, b0k * 512: b0k * 512 + 1024], S3[:]), 's_v')
            P.vdep()
            ev = P.do('v', lambda e, t=t, slot=slot: e.tensor_add(
                x_buf[:, t * 1024:(t + 1) * 1024],
                xstg[:, slot * 1024:(slot + 1) * 1024], tmp_gb[:]), 's_v')
            evac_op[t] = ev
            free_bank[b0k] = ('s_v', ev)
            free_bank[b1k] = ('s_v', ev)
            if t + 2 < NT:
                oproj_dma(t + 2)
        oproj_done_v = P.cnt['s_v']
        if b == 0 and tap("x2", [128, 16384], F32):
            P.wait('y', 's_v', oproj_done_v)
            P.do('y', lambda e: e.dma_start(out=dbg["x2"], in_=x_buf[:]), 'd_db', 16)
        if not _before(upto, b, "ln2"):
            break

        P.wait('v', 's_t', P.cnt['s_t'])
        hv2 = ln_phase(b, 2)
        if b == 0 and tap("h2", [128, 16384], BF16):
            P.wait('y', 's_v', hv2)
            P.do('y', lambda e: e.dma_start(out=dbg["h2"], in_=h_buf[:]), 'd_db', 16)
        if not _before(upto, b, "gate"):
            break

        wg_done = {}
        for t in range(NT):
            bank_wait('t', 4)
            P.wait('t', 's_v', hv2)
            mm = None
            for K in range(8):
                mm = P.do('t', lambda e, K=K, t=t: e.matmul(
                    PS[:, 4 * 512: 4 * 512 + 8],
                    h_buf[:, K * 2048 + t * 128: K * 2048 + (t + 1) * 128],
                    gatewsb[:, K * 8:(K + 1) * 8],
                    start=(K == 0), stop=(K == 7)), 's_t')
            P.wait('v', 's_t', mm)
            P.wait('v', 's_s')
            P.do('v', lambda e: e.tensor_copy(gA[:, 0:8], PS[:, 4 * 512:4 * 512 + 8]), 's_v')
            P.vdep()
            ev = P.do('v', lambda e: e.reduce_max(gB[:, 0:1], gA[:, 0:8], axis=AXX, negate=True), 's_v')
            free_bank[4] = ('s_v', ev)
            P.vdep()
            P.do('v', lambda e: e.tensor_scalar_add(gC[:, 0:8], gA[:, 0:8], gB[:, 0:1]), 's_v')
            P.vdep()
            P.do('v', lambda e: e.tensor_single_scalar(gB[:, 1:9], gC[:, 0:8], 0.0, ALU.is_ge), 's_v')
            P.vdep()
            P.do('v', lambda e: e.scalar_tensor_tensor(
                gC[:, 8:16], gB[:, 1:9], -1e30, gA[:, 0:8], ALU.mult, ALU.add), 's_v')
            P.vdep()
            vm = P.do('v', lambda e: e.reduce_max(gB[:, 9:10], gC[:, 8:16], axis=AXX, negate=True), 's_v')
            P.wait('s', 's_v', vm)
            ex = P.do('s', lambda e: e.activation(gC[:, 0:8], gA[:, 0:8], AF.Exp, bias=gB[:, 0:1]), 's_s')
            P.vdep()
            P.do('v', lambda e: e.tensor_scalar_add(gtmp[:, 0:8], gA[:, 0:8], gB[:, 9:10]), 's_v')
            P.vdep()
            P.do('v', lambda e: e.tensor_single_scalar(gB[:, 1:9], gtmp[:, 0:8], 0.0, ALU.is_ge), 's_v')
            P.wait('v', 's_s', ex)
            P.vdep()
            P.do('v', lambda e: e.tensor_mul(gA[:, 8:16], gC[:, 0:8], gB[:, 1:9]), 's_v')
            P.vdep()
            P.do('v', lambda e: e.reduce_sum(gB[:, 10:11], gA[:, 8:16], axis=AXX), 's_v')
            P.vdep()
            P.do('v', lambda e: e.reciprocal(gC[:, 1:2], gB[:, 10:11]), 's_v')
            P.vdep()
            wg = P.do('v', lambda e, t=t: e.tensor_scalar_mul(
                Wg[:, t * 8:(t + 1) * 8], gA[:, 8:16], gC[:, 1:2]), 's_v')
            wg_done[t] = wg
        bank_wait('t', 4)
        bank_wait('t', 5)
        for t in range(NT):
            sl = (4 + t % 2) * 512
            ev = free_bank.pop(('gt', t % 2), None)
            if ev is not None:
                P.wait('t', ev[0], ev[1])
            P.wait('t', 's_v', wg_done[t])
            tr = P.do('t', lambda e, t=t, sl=sl: e.transpose(
                PS[0:8, sl:sl + 128], Wg[:, t * 8:(t + 1) * 8], identf_s[:]), 's_t')
            P.wait('v', 's_t', tr)
            cp = P.do('v', lambda e, t=t, sl=sl: e.tensor_copy(
                Wrow[0:8, t * 128:(t + 1) * 128], PS[0:8, sl:sl + 128]), 's_v')
            free_bank[('gt', t % 2)] = ('s_v', cp)
        gate_done_v = P.cnt['s_v']
        free_bank[4] = ('s_v', gate_done_v)
        free_bank[5] = ('s_v', gate_done_v)
        if b == 0 and tap("wrow", [8, 2048], BF16):
            P.wait('y', 's_v', gate_done_v)
            P.do('y', lambda e: e.dma_start(out=dbg["wrow"], in_=Wrow[:]), 'd_db', 16)
        if not _before(upto, b, "ffn"):
            break

        gb_ev = build_bcast(5, b, S3)
        P.wait('v', 's_t', P.cnt['s_t'])
        og_free_v = P.cnt['s_v']
        w12_done = {}
        w12_fill = {}
        glast = None

        def w12_dma(e_i, s, M):
            blk = w12_seq[0]
            w12_seq[0] += 1
            slot12 = blk % 2
            sem12 = 'd_w12a' if slot12 == 0 else 'd_w12b'
            if blk - 2 in w12_done:
                P.wait('g', 's_t', w12_done[blk - 2])
            P.do('g', lambda e, e_i=e_i, M=M, slot12=slot12: e.dma_start(
                out=w12sb[:, slot12 * 2048:(slot12 + 1) * 2048],
                in_=w12_d[e_i, M]), sem12, 16)
            w12_fill[blk] = P.cnt[sem12]
            return blk

        pending = []
        for e_i in range(NE):
            for s in range(2):
                for M in range(16):
                    pending.append((e_i, s, M))
        pend_i = [0]
        def pump_w12():
            if pend_i[0] < len(pending):
                e_i, s, M = pending[pend_i[0]]
                pend_i[0] += 1
                return w12_dma(e_i, s, M)
            return None
        blk0 = pump_w12()
        blk1 = pump_w12()

        blkc = [blk0 if blk0 is not None else 0]
        blk_iter = [w12_seq[0] - 2]

        for e_i in range(NE):
            P.wait('g', 's_t', P.cnt['s_t'])
            P.do('g', lambda e, e_i=e_i: e.dma_start(
                out=w3sb[:],
                in_=w3_d[e_i].rearrange("kh kp m -> kp kh m")), 'd_w3', 16)
            w3_target = P.cnt['d_w3']
            for s in range(2):
                if e_i < 8:
                    bank_wait('t', 4)
                    bank_wait('t', 5)
                    P.wait('t', 's_v', gate_done_v)
                    P.do('t', lambda e, e_i=e_i, s=s: e.matmul(
                        PS[:, 2048:2048 + 512], ind8_s[:, e_i * 128:(e_i + 1) * 128],
                        Wrow[:, s * 1024: s * 1024 + 512], start=True, stop=True), 's_t')
                    wbm = P.do('t', lambda e, e_i=e_i, s=s: e.matmul(
                        PS[:, 2560:2560 + 512], ind8_s[:, e_i * 128:(e_i + 1) * 128],
                        Wrow[:, s * 1024 + 512: (s + 1) * 1024], start=True, stop=True), 's_t')
                    P.wait('v', 's_t', wbm)
                    wbe = P.do('v', lambda e: e.tensor_copy(Wb[:], PS[:, 2048:3072]), 's_v')
                    free_bank[4] = ('s_v', wbe)
                    free_bank[5] = ('s_v', wbe)
                else:
                    wbe = None
                for M in range(16):
                    blk = blk_iter[0]
                    blk_iter[0] += 1
                    slot12 = blk % 2
                    sem12 = 'd_w12a' if slot12 == 0 else 'd_w12b'
                    sA = (2 * M) % 3
                    sB = (2 * M + 1) % 3
                    P.wait('t', sem12, w12_fill[blk])
                    if M == 0:
                        P.wait('t', 's_v', hv2)
                        if wbe is not None:
                            P.wait('t', 's_v', wbe)
                        P.wait('t', 's_v', og_free_v)
                    mh1 = mh2 = None
                    for half, bki in ((0, 2 * sA), (1, 2 * sA + 1)):
                        bank_wait('t', bki)
                        for K in range(8):
                            mh1 = P.do('t', lambda e, K=K, half=half, bki=bki, slot12=slot12, s=s: e.matmul(
                                bank(bki),
                                w12sb[:, slot12 * 2048 + K * 128:slot12 * 2048 + (K + 1) * 128],
                                h_buf[:, K * 2048 + s * 1024 + half * 512: K * 2048 + s * 1024 + (half + 1) * 512],
                                start=(K == 0), stop=(K == 7)), 's_t')
                    for half, bki in ((0, 2 * sB), (1, 2 * sB + 1)):
                        bank_wait('t', bki)
                        for K in range(8):
                            mh2 = P.do('t', lambda e, K=K, half=half, bki=bki, slot12=slot12, s=s: e.matmul(
                                bank(bki),
                                w12sb[:, slot12 * 2048 + 1024 + K * 128:slot12 * 2048 + 1024 + (K + 1) * 128],
                                h_buf[:, K * 2048 + s * 1024 + half * 512: K * 2048 + s * 1024 + (half + 1) * 512],
                                start=(K == 0), stop=(K == 7)), 's_t')
                    w12_done[blk] = P.cnt['s_t']
                    pump_w12()
                    P.wait('s', 's_t', mh1)
                    if glast is not None:
                        P.wait('s', 's_v', glast)
                    si = P.do('s', lambda e, sA=sA: e.activation(
                        tmp_si[:], PS[:, 2 * sA * 512: 2 * sA * 512 + 1024], AF.Silu), 's_s')
                    free_bank[2 * sA] = ('s_s', si)
                    free_bank[2 * sA + 1] = ('s_s', si)
                    P.wait('v', 's_t', mh2)
                    P.wait('v', 's_s', si)
                    siop = tmp_si
                    if e_i < 8:
                        P.vdep()
                        P.do('v', lambda e: e.tensor_mul(tmp_si2[:], tmp_si[:], Wb[:]), 's_v')
                        P.vdep()
                        siop = tmp_si2
                    gv = P.do('v', lambda e, sB=sB, M=M, siop=siop: e.tensor_mul(
                        og[:, M * 1024:(M + 1) * 1024], siop[:],
                        PS[:, 2 * sB * 512: 2 * sB * 512 + 1024]), 's_v')
                    free_bank[2 * sB] = ('s_v', gv)
                    free_bank[2 * sB + 1] = ('s_v', gv)
                    glast = gv
                P.wait('t', 'd_w3', w3_target)
                P.wait('t', 's_v', glast)
                for t in range(8):
                    tt = 8 * s + t
                    b0k, b1k = 2 * (t % 2), 2 * (t % 2) + 1
                    bank_wait('t', b0k)
                    bank_wait('t', b1k)
                    mm = None
                    for K in range(16):
                        for ncx in range(2):
                            mm = P.do('t', lambda e, K=K, t=t, b0k=b0k, ncx=ncx: e.matmul(
                                PS[:, b0k * 512 + ncx * 512: b0k * 512 + (ncx + 1) * 512],
                                og[:, K * 1024 + t * 128: K * 1024 + (t + 1) * 128],
                                w3sb[:, K * 1024 + ncx * 512: K * 1024 + (ncx + 1) * 512],
                                start=(K == 0), stop=(K == 15)), 's_t')
                    P.wait('v', 's_t', mm)
                    P.wait('v', 's_v', gb_ev)
                    P.do('v', lambda e, b0k=b0k: e.tensor_mul(
                        tmp_gb[:], PS[:, b0k * 512: b0k * 512 + 1024], S3[:]), 's_v')
                    P.vdep()
                    ev = P.do('v', lambda e, tt=tt: e.tensor_add(
                        x_buf[:, tt * 1024:(tt + 1) * 1024],
                        x_buf[:, tt * 1024:(tt + 1) * 1024], tmp_gb[:]), 's_v')
                    ffn_evac[(b, s, t)] = ev
                    free_bank[b0k] = ('s_v', ev)
                    free_bank[b1k] = ('s_v', ev)
        if not _before(upto, b, "out"):
            break

        for tt in range(NT):
            P.wait('y', 's_v', ffn_evac[(b, tt // 8, tt % 8)])
            P.do('y', lambda e, b=b, tt=tt: e.dma_start(
                out=y_d[b, tt * 128:(tt + 1) * 128, :],
                in_=x_buf[:, tt * 1024:(tt + 1) * 1024]), 'd_xo', 16)
        out_dma_total = P.cnt['d_xo']

    P.wait('y', 'd_xo', P.cnt.get('d_xo', 0))
    P.wait('y', 'd_db', P.cnt.get('d_db', 0))

    semnames = ['s_t', 's_v', 's_s', 'd_xsa', 'd_xsb', 'd_xo', 'd_kr', 'd_wq',
                'd_w12a', 'd_w12b', 'd_w3', 'd_wm', 'd_wmb', 'd_misc', 'd_db']
    with contextlib.ExitStack() as stack:
        stack.enter_context(nc.allow_low_precision(
            reason="bf16 matmul operands / normalizers; fp32 accumulation kept"))
        for s in semnames:
            P.sem[s] = stack.enter_context(nc.semaphore(s))
        block = stack.enter_context(nc.Block())
        block.gpsimd(lambda e: [f(e) for f in P.ops['g']])
        block.scalar(lambda e: [f(e) for f in P.ops['s']])
        block.tensor(lambda e: [f(e) for f in P.ops['t']])
        block.vector(lambda e: [f(e) for f in P.ops['v']])
        block.sync(lambda e: [f(e) for f in P.ops['y']])
    return nc, dbg


def kernel(**inputs):
    lay = host_layouts(inputs)
    x = np.ascontiguousarray(np.asarray(inputs["L0_attn_x"], dtype=np.float32))
    cc = np.ascontiguousarray(np.asarray(inputs["L0_c"], dtype=np.float32))
    nc, dbg = build_program(taps=tuple(_TAPS), upto=_UPTO)
    in_maps = []
    for c in range(NCORES):
        in_maps.append({
            "x": np.ascontiguousarray(x[2 * c:2 * c + 2]),
            "cvec": np.ascontiguousarray(cc[2 * c:2 * c + 2]),
            "wqkv": lay["wqkv"], "wout": lay["wout"], "gatew": lay["gate"],
            "wmod": lay["wmod"], "w12": lay["w12"], "w3": lay["w3"],
        })
    res = run_bass_kernel_spmd(nc, in_maps, core_ids=list(range(NCORES)))
    out = np.empty((16, T, D), dtype=np.float32)
    for c in range(NCORES):
        out[2 * c:2 * c + 2] = res.results[c]["y"]
    if _TAPS:
        kernel._dbg = [{k: r["dbg_" + k] for k in _TAPS} for r in res.results]
    return out


# version 03
# speedup vs baseline: 1.1904x; 1.1650x over previous
import contextlib
import math

import numpy as np
import ml_dtypes

import concourse.bass as bass
import concourse.mybir as mybir
from concourse.bass_utils import run_bass_kernel_spmd

F32 = mybir.dt.float32
BF16 = mybir.dt.bfloat16
AF = mybir.ActivationFunctionType
ALU = mybir.AluOpType
AXX = mybir.AxisListType.X
BFD = ml_dtypes.bfloat16

NCORES = 8
NB = 2
T = 2048
D = 1024
NT = 16
NH = 16
NE = 9
BSHIFT = 12.0

_UPTO = None
_TAPS = []


def _slopes():
    start = 2 ** (-2 ** (-(math.log2(NH) - 3)))
    return np.array([start * start**i for i in range(NH)], dtype=np.float32)


def host_layouts(inp):
    f32 = np.float32
    Wqkv = np.ascontiguousarray(inp["L0_attn_W_qkv"], dtype=f32)
    Wout = np.ascontiguousarray(inp["L0_attn_W_out"], dtype=f32)
    gw = np.ascontiguousarray(inp["L0_ffn_gate_w"], dtype=f32)
    ew1 = np.asarray(inp["L0_ffn_ew1"], dtype=f32)
    ew2 = np.asarray(inp["L0_ffn_ew2"], dtype=f32)
    ew3 = np.asarray(inp["L0_ffn_ew3"], dtype=f32)
    sw1 = np.asarray(inp["L0_ffn_sw1"], dtype=f32)
    sw2 = np.asarray(inp["L0_ffn_sw2"], dtype=f32)
    sw3 = np.asarray(inp["L0_ffn_sw3"], dtype=f32)
    wmod = np.ascontiguousarray(inp["L0_wmod"], dtype=f32)

    wqkv_l = np.ascontiguousarray(
        Wqkv.reshape(1024, 3, 8, 128).transpose(2, 0, 1, 3).reshape(8, 1024, 384))
    wout_l = np.ascontiguousarray(Wout.reshape(8, 128, 1024))
    gate_l = np.ascontiguousarray(gw.reshape(8, 128, 8))
    wmod_l = np.ascontiguousarray(wmod.reshape(8, 128, 6144))
    w1 = np.concatenate([ew1, sw1[None]], axis=0)
    w2 = np.concatenate([ew2, sw2[None]], axis=0)
    w3 = np.concatenate([ew3, sw3[None]], axis=0)
    t1 = w1.reshape(9, 8, 128, 16, 128).transpose(0, 3, 2, 1, 4).reshape(9, 16, 128, 1024)
    t2 = w2.reshape(9, 8, 128, 16, 128).transpose(0, 3, 2, 1, 4).reshape(9, 16, 128, 1024)
    w12_l = np.ascontiguousarray(np.concatenate([t1, t2], axis=3))
    w3_l = np.ascontiguousarray(w3.reshape(9, 16, 128, 1024))
    return dict(wqkv=wqkv_l, wout=wout_l, gate=gate_l, wmod=wmod_l,
                w12=w12_l, w3=w3_l)


def host_consts():
    sl = _slopes()
    k = np.arange(T, dtype=np.float32)
    bias = -sl[:, None] * k[None, :]
    bhi = bias.astype(BFD).astype(np.float32)
    blo = (bias - bhi).astype(BFD)
    qrow = (sl[:, None] * k[None, :]).astype(np.float32)
    ones = np.ones((16, T), np.float32)
    zero = np.zeros((16, T), np.float32)
    khlA = np.stack([bhi, blo.astype(np.float32), ones, zero], 1).astype(BFD)
    qA = np.stack([ones, ones, qrow, zero], 1).astype(BFD)
    khlB = np.stack([ones, bhi, blo.astype(np.float32), zero], 1).astype(BFD)
    qB = np.stack([qrow, ones, ones, zero], 1).astype(BFD)
    identb = np.eye(128, dtype=BFD)
    identf = np.eye(128, dtype=np.float32)
    kk = np.arange(128)
    tri = np.where(kk[:, None] >= kk[None, :], 0.0, -1e30).astype(BFD)
    ind32 = ((kk[:, None] == 32) * np.ones((1, 128))).astype(BFD)
    ind64 = ((kk[:, None] == 64) * np.ones((1, 128))).astype(BFD)
    indat = np.ascontiguousarray(np.concatenate([ind32, ind64], axis=1))
    e8 = np.arange(8)
    ind8 = np.repeat((e8[:, None, None] == e8[None, :, None]), 128, axis=2)
    ind8 = np.ascontiguousarray(ind8.reshape(8, 1024).astype(BFD))
    cvals = np.ascontiguousarray(
        np.tile(np.array([[1e-5, -BSHIFT]], np.float32), (128, 1)))
    return dict(khlA=np.ascontiguousarray(khlA), qA=np.ascontiguousarray(qA),
                khlB=np.ascontiguousarray(khlB), qB=np.ascontiguousarray(qB),
                identb=identb, identf=identf, tri=tri, indat=indat, ind8=ind8,
                cvals=cvals)


class Prog:
    def __init__(self):
        self.ops = {k: [] for k in "tvsgy"}
        self.cnt = {}
        self.sem = {}
        self.mark = {}

    def raw(self, eng, fn):
        self.ops[eng].append(fn)

    def wait(self, eng, sem, val=None):
        v = self.cnt.get(sem, 0) if val is None else val
        assert v <= self.cnt.get(sem, 0), f"wait on future event {sem} {v} > {self.cnt.get(sem, 0)}"
        if v > 0:
            self.raw(eng, lambda e, s=sem, vv=v: e.wait_ge(self.sem[s], vv))

    def vdep(self):
        self.wait('v', 's_v')

    def do(self, eng, fn, inc=None, by=1):
        if inc is None:
            self.raw(eng, fn)
            return None
        self.cnt[inc] = self.cnt.get(inc, 0) + by
        def run(e, fn=fn, s=inc, b=by):
            fn(e).then_inc(self.sem[s], b)
        self.raw(eng, run)
        return self.cnt[inc]


STAGES = ["ln1", "attn", "oproj", "ln2", "gate", "ffn", "out"]


def _before(upto, b, stage):
    if upto is None:
        return True
    ub, us = upto
    if b != ub:
        return b < ub
    return STAGES.index(stage) <= STAGES.index(us)


def build_program(taps=(), upto=None):
    nc = bass.Bass("TRN2", target_bir_lowering=False)
    C = host_consts()

    x_d = nc.dram_tensor("x", [NB, T, D], F32, kind="ExternalInput").ap()
    c_d = nc.dram_tensor("cvec", [NB, D], F32, kind="ExternalInput").ap()
    wqkv_d = nc.dram_tensor("wqkv", [8, 1024, 384], F32, kind="ExternalInput").ap()
    wout_d = nc.dram_tensor("wout", [8, 128, 1024], F32, kind="ExternalInput").ap()
    gate_d = nc.dram_tensor("gatew", [8, 128, 8], F32, kind="ExternalInput").ap()
    wmod_d = nc.dram_tensor("wmod", [8, 128, 6144], F32, kind="ExternalInput").ap()
    w12_d = nc.dram_tensor("w12", [9, 16, 128, 2048], F32, kind="ExternalInput").ap()
    w3_d = nc.dram_tensor("w3", [9, 16, 128, 1024], F32, kind="ExternalInput").ap()
    y_d = nc.dram_tensor("y", [NB, T, D], F32, kind="ExternalOutput").ap()

    khlA_d = nc.inline_tensor(C["khlA"], name="khlA").ap()
    qA_d = nc.inline_tensor(C["qA"], name="qA").ap()
    khlB_d = nc.inline_tensor(C["khlB"], name="khlB").ap()
    qB_d = nc.inline_tensor(C["qB"], name="qB").ap()
    identb_d = nc.inline_tensor(C["identb"], name="identb").ap()
    identf_d = nc.inline_tensor(C["identf"], name="identf").ap()
    tri_d = nc.inline_tensor(C["tri"], name="tri").ap()
    indat_d = nc.inline_tensor(C["indat"], name="indat").ap()
    ind8_d = nc.inline_tensor(C["ind8"], name="ind8").ap()
    cvals_d = nc.inline_tensor(C["cvals"], name="cvals").ap()

    dbg = {}
    def tap(name, shape, dtype):
        if name in taps:
            dbg[name] = nc.dram_tensor("dbg_" + name, shape, dtype,
                                       kind="ExternalOutput").ap()
            return True
        return False

    ARENA = nc.sbuf_bytes_remaining - 2048
    base = nc.bump_sbuf(ARENA)[0]
    cur = [base]
    def alloc(name, shape, dtype):
        a = (cur[0] + 31) & ~31
        t = nc.alloc_sbuf_tensor_at(name, shape, dtype, offset=a)
        cur[0] = a + shape[1] * {F32: 4, BF16: 2}[dtype]
        return t

    x_buf = alloc("x_buf", [128, NT * 1024], F32)
    XOFF = nc.lookup_mloc(x_buf).addr
    h_buf = alloc("h_buf", [128, 8 * 2048], BF16)
    og = alloc("og", [128, 16384], BF16)
    w3sb = alloc("w3sb", [128, 16 * 1024], BF16)
    w12sb = alloc("w12sb", [128, 2 * 2048], BF16)
    Wb = alloc("Wb", [128, 1024], BF16)
    tmp_si = alloc("tmp_si", [128, 1024], BF16)
    tmp_gb = alloc("tmp_gb", [128, 1024], BF16)
    S1 = alloc("S1", [128, 1024], BF16)
    S2 = alloc("S2", [128, 1024], BF16)
    S3 = alloc("S3", [128, 1024], BF16)
    Wrow = alloc("Wrow", [8, 2048], BF16)
    Wg = alloc("Wg", [128, 128], F32)
    gatewsb = alloc("gatewsb", [128, 64], BF16)
    modsb = alloc("modsb", [128, 96], F32)
    dpb = alloc("dpb", [128, 96], BF16)
    gtmp = alloc("gtmp", [128, 32], F32)

    cfmb = alloc("cfmb", [128, 16], BF16)
    rowt = alloc("rowt", [1, 1024], BF16)
    identb_s = alloc("identb_s", [128, 128], BF16)
    identf_s = alloc("identf_s", [128, 128], F32)
    tri_s = alloc("tri_s", [128, 128], BF16)
    indat_s = alloc("indat_s", [128, 256], BF16)
    ind8_s = alloc("ind8_s", [8, 1024], BF16)
    onesb = alloc("onesb", [1, 128], BF16)
    s_sums = alloc("s_sums", [128, 512], BF16)
    sq_tmp = alloc("sq_tmp", [128, 1024], F32)
    csil = nc.alloc_sbuf_tensor_at("csil", [2, 1024], F32,
                                   offset=nc.lookup_mloc(sq_tmp).addr)
    xnb = alloc("xnb", [128, 1024], BF16)
    stats_a = alloc("stats_a", [128, 4], F32)
    stats_b = alloc("stats_b", [128, 4], F32)
    stats_c = alloc("stats_c", [128, 4], F32)
    gA = alloc("gA", [128, 16], F32)
    gB = alloc("gB", [128, 16], F32)
    gC = alloc("gC", [128, 16], F32)
    gtmp2 = alloc("gtmp2", [128, 32], F32)
    xnb2 = alloc("xnb2", [128, 1024], BF16)
    tmp_si2 = alloc("tmp_si2", [128, 1024], BF16)
    cvals_s = alloc("cvals_s", [128, 2], F32)
    nc.const_aps.register(F32, 1e-5, cvals_s[:, 0:1])
    nc.const_aps.register(F32, -BSHIFT, cvals_s[:, 1:2])
    assert cur[0] <= base + ARENA, f"sbuf over: {cur[0] - base} > {ARENA}"

    Qta = nc.alloc_sbuf_tensor_at("Qta", [128, 2048], BF16, offset=XOFF + 0)
    Qtb = nc.alloc_sbuf_tensor_at("Qtb", [128, 2048], BF16, offset=XOFF + 4096)
    Kta = nc.alloc_sbuf_tensor_at("Kta", [128, 2048], BF16, offset=XOFF + 8192)
    Ktb = nc.alloc_sbuf_tensor_at("Ktb", [128, 2048], BF16, offset=XOFF + 12288)
    Vab = nc.alloc_sbuf_tensor_at("Vab", [128, 16 * 196], BF16, offset=XOFF + 16384)

    wqkvsb = nc.alloc_sbuf_tensor_at("wqkvsb", [128, 3072], BF16, offset=XOFF + 24768)
    xstg = nc.alloc_sbuf_tensor_at("xstg", [128, 2048], F32, offset=XOFF + 57344)
    W3OFF = nc.lookup_mloc(w3sb).addr
    Pt = nc.alloc_sbuf_tensor_at("Pt", [128, 2048], BF16, offset=W3OFF + 16384)
    csb = nc.alloc_sbuf_tensor_at("csb", [2, 1024], F32, offset=XOFF + 40960)

    PS = nc.alloc_psum_tensor("PS", [128, 3072], F32)
    PSb = nc.alloc_psum_tensor("PSb", [128, 1024], BF16)
    PSb2 = nc.alloc_psum_tensor("PSb2", [128, 1024], BF16)
    def bank(i, w=512):
        return PS[:, i * 512:i * 512 + w]

    P = Prog()
    free_bank = {}

    def bank_wait(eng, *banks):
        for bk in banks:
            ev = free_bank.pop(bk, None)
            if ev is not None:
                P.wait(eng, ev[0], ev[1])

    for dst, src in [(identb_s, identb_d), (identf_s, identf_d), (tri_s, tri_d),
                     (indat_s, indat_d), (ind8_s, ind8_d)]:
        P.do('g', lambda e, d=dst, s=src: e.dma_start(out=d[:], in_=s), 'd_misc', 16)
    P.do('g', lambda e: e.dma_start(out=gatewsb[:],
         in_=gate_d.rearrange("kc kp e -> kp kc e")), 'd_misc', 16)
    P.do('g', lambda e: e.dma_start(out=csb[:], in_=c_d), 'd_misc', 16)
    P.do('g', lambda e: e.dma_start(out=cvals_s[:], in_=cvals_d), 'd_misc', 16)
    misc_total = P.cnt['d_misc']
    P.do('v', lambda e: e.memset(onesb[:], 1.0), 's_v')
    P.do('v', lambda e: e.memset(s_sums[:], 1.0), 's_v')

    P.wait('s', 'd_misc', misc_total)
    P.do('s', lambda e: e.activation(csil[:], csb[:], AF.Silu), 's_s')
    P.wait('t', 'd_misc', misc_total)
    P.wait('t', 's_s')
    cfm_ev = None
    for kc in range(8):
        if cfm_ev is not None:
            P.wait('t', 's_v', cfm_ev)
        tr = P.do('t', lambda e, kc=kc: e.transpose(
            PS[:, 5 * 512: 5 * 512 + 2],
            csil[0:2, kc * 128:(kc + 1) * 128], identf_s[0:2, 0:2]), 's_t')
        P.wait('v', 's_t', tr)
        cfm_ev = P.do('v', lambda e, kc=kc: e.tensor_copy(
            cfmb[:, kc * 2:(kc + 1) * 2], PS[:, 5 * 512:5 * 512 + 2]), 's_v')

    wm_done = {}
    def emit_wmod_dma(kc):
        if kc - 2 in wm_done:
            P.wait('g', 's_t', wm_done[kc - 2])
        sem = 'd_wm' if kc % 2 == 0 else 'd_wmb'
        P.do('g', lambda e, kc=kc: e.dma_start(
            out=h_buf[:, (kc % 2) * 6144:((kc % 2) + 1) * 6144], in_=wmod_d[kc]),
            sem, 16)
        P.mark[('wm', kc)] = (sem, P.cnt[sem])
    emit_wmod_dma(0)
    emit_wmod_dma(1)
    P.wait('t', 's_v', cfm_ev)
    for kc in range(8):
        P.wait('t', *P.mark[('wm', kc)])
        for Mc in range(48):
            P.do('t', lambda e, kc=kc, Mc=Mc: e.matmul(
                PS[:, 4 * 512 + Mc * 2: 4 * 512 + Mc * 2 + 2],
                h_buf[:, (kc % 2) * 6144 + Mc * 128: (kc % 2) * 6144 + (Mc + 1) * 128],
                cfmb[:, kc * 2:(kc + 1) * 2],
                start=(kc == 0 and Mc == 0), stop=(kc == 7 and Mc == 47),
                skip_group_check=True), 's_t')
        wm_done[kc] = P.cnt['s_t']
        if kc + 2 < 8:
            emit_wmod_dma(kc + 2)
    P.wait('v', 's_t')
    P.do('v', lambda e: e.tensor_copy(modsb[:], PS[:, 4 * 512:4 * 512 + 96]), 's_v')
    free_bank[4] = ('s_v', P.cnt['s_v'])
    free_bank[5] = ('s_v', P.cnt['s_v'])

    for sp in range(6):
        for bi in range(NB):
            sl = slice(sp * 16 + bi, sp * 16 + 16, 2)
            src, dst = modsb[:, sl], dpb[:, sl]
            if sp in (1, 4):
                P.do('v', lambda e, s=src, d=dst: e.tensor_scalar_add(d, s, 1.0), 's_v')
            elif sp in (0, 3):
                P.do('v', lambda e, s=src, d=dst: e.tensor_copy(d, s), 's_v')
            else:
                p_, n_, t_ = gtmp[:, 0:8], gtmp[:, 8:16], gtmp[:, 16:24]
                m_, d_ = gtmp2[:, 0:8], gtmp2[:, 8:16]
                P.do('v', lambda e, s=src, p=p_: e.tensor_scalar_max(p, s, 0.25), 's_v')
                P.do('v', lambda e, s=src, n=n_: e.tensor_scalar_min(n, s, -0.25), 's_v')
                P.do('v', lambda e, s=src, m=m_: e.tensor_single_scalar(m, s, 0.0, ALU.is_ge), 's_v')
                P.vdep()
                P.do('v', lambda e, p=p_, n=n_, d=d_: e.tensor_sub(d, p, n), 's_v')
                P.vdep()
                P.do('v', lambda e, m=m_, d=d_, t=t_: e.tensor_mul(t, m, d), 's_v')
                P.vdep()
                P.do('v', lambda e, n=n_, t=t_, ds=dst: e.tensor_add(ds, n, t), 's_v')
    mod_done_v = P.cnt['s_v']

    if tap("mod", [128, 96], F32):
        P.wait('y', 's_v', mod_done_v)
        P.do('y', lambda e: e.dma_start(out=dbg["mod"], in_=modsb[:]), 'd_db', 16)
    if tap("dpb", [128, 96], BF16):
        P.wait('y', 's_v', mod_done_v)
        P.do('y', lambda e: e.dma_start(out=dbg["dpb"], in_=dpb[:]), 'd_db', 16)

    def build_bcast(sp, bi, dst):
        bank_wait('t', 4)
        bank_wait('t', 5)
        P.wait('t', 's_v', mod_done_v)
        last = None
        for fc in range(8):
            col = sp * 16 + fc * 2 + bi
            last = P.do('t', lambda e, col=col, fc=fc: e.matmul(
                PS[0:1, 2048 + fc * 128:2048 + (fc + 1) * 128],
                dpb[:, col:col + 1], identb_s[:],
                start=(fc % 4 == 0), stop=(fc % 4 == 3),
                skip_group_check=True), 's_t')
        P.wait('v', 's_t', last)
        cp = P.do('v', lambda e: e.tensor_copy(rowt[0:1, :], PS[0:1, 2048:3072]), 's_v')
        P.wait('t', 's_v', cp)
        for nck in range(2):
            last = P.do('t', lambda e, nck=nck: e.matmul(
                PS[:, 2048 + nck * 512:2048 + (nck + 1) * 512],
                onesb[0:1, 0:128], rowt[0:1, nck * 512:(nck + 1) * 512],
                start=True, stop=True), 's_t')
        P.wait('v', 's_t', last)
        ev = P.do('v', lambda e, d=dst: e.tensor_copy(d[:], PS[:, 2048:3072]), 's_v')
        free_bank[4] = ('s_v', ev)
        free_bank[5] = ('s_v', ev)
        return ev

    if "bct" in taps:
        dbg["bct"] = nc.dram_tensor("dbg_bct", [128, 2048], BF16, kind="ExternalOutput").ap()
        build_bcast(1, 0, S1)
        bev = build_bcast(0, 0, S2)
        P.wait('y', 's_v', bev)
        P.do('y', lambda e: e.dma_start(out=dbg["bct"][:, 0:1024], in_=S1[:]), 'd_db', 16)
        P.do('y', lambda e: e.dma_start(out=dbg["bct"][:, 1024:2048], in_=S2[:]), 'd_db', 16)

    def ln_phase(b, which):
        sev = build_bcast(4 if which == 2 else 1, b, S1)
        if "s1early" in taps and b == 0 and which == 1:
            if "s1early" not in dbg:
                dbg["s1early"] = nc.dram_tensor("dbg_s1early", [128, 1024], BF16,
                                                kind="ExternalOutput").ap()
            P.wait('y', 's_v', sev)
            P.do('y', lambda e: e.dma_start(out=dbg["s1early"], in_=S1[:]), 'd_db', 16)
        ev2 = build_bcast(3 if which == 2 else 0, b, S2)
        P.wait('y', 's_v', ev2)
        P.wait('s', 's_v')
        apply_done = {}
        red_done = {}

        def stage_dma(t):
            slot = t % 2
            sem = 'd_xsa' if slot == 0 else 'd_xsb'
            if t - 2 in apply_done:
                P.wait('y', 's_v', apply_done[t - 2])
            P.do('y', lambda e, b=b, t=t, slot=slot: e.dma_start(
                out=xstg[:, slot * 1024:(slot + 1) * 1024],
                in_=x_d[b, t * 128:(t + 1) * 128, :]), sem, 16)
            P.mark[('lnx', b, which, t)] = P.cnt[sem]

        if which == 1:
            stage_dma(0)
            stage_dma(1)
        for t in range(NT):
            if which == 1:
                slot = t % 2
                sem = 'd_xsa' if slot == 0 else 'd_xsb'
                P.wait('s', sem, P.mark[('lnx', b, which, t)])
                P.wait('v', sem, P.mark[('lnx', b, which, t)])
                xt = xstg[:, slot * 1024:(slot + 1) * 1024]
            else:
                xt = x_buf[:, t * 1024:(t + 1) * 1024]
            if t - 1 in red_done:
                P.wait('s', 's_v', red_done[t - 1])
            sq = P.do('s', lambda e, xt=xt: e.activation(sq_tmp[:], xt, AF.Square), 's_s')
            P.do('v', lambda e, xt=xt: e.reduce_sum(stats_a[:, 0:1], xt, axis=AXX, negate=True), 's_v')
            P.wait('v', 's_s', sq)
            rd = P.do('v', lambda e: e.reduce_sum(stats_a[:, 1:2], sq_tmp[:], axis=AXX), 's_v')
            red_done[t] = rd
            P.vdep()
            P.do('v', lambda e: e.tensor_scalar_mul(stats_b[:, 0:1], stats_a[:, 0:1], 1.0 / D), 's_v')
            P.do('v', lambda e: e.tensor_scalar_mul(stats_b[:, 1:2], stats_a[:, 1:2], 1.0 / D), 's_v')
            P.vdep()
            P.do('v', lambda e: e.tensor_mul(stats_c[:, 0:1], stats_b[:, 0:1], stats_b[:, 0:1]), 's_v')
            P.vdep()
            vv = P.do('v', lambda e: e.tensor_sub(stats_a[:, 2:3], stats_b[:, 1:2], stats_c[:, 0:1]), 's_v')
            P.wait('s', 's_v', vv)
            st = P.do('s', lambda e: e.activation(stats_b[:, 2:3], stats_a[:, 2:3], AF.Sqrt, bias=1e-5), 's_s')
            P.wait('v', 's_s', st)
            P.do('v', lambda e: e.reciprocal(stats_c[:, 1:2], stats_b[:, 2:3]), 's_v')
            P.vdep()
            nm = P.do('v', lambda e: e.tensor_mul(stats_a[:, 3:4], stats_b[:, 0:1], stats_c[:, 1:2]), 's_v')
            P.vdep()
            ap = P.do('v', lambda e, xt=xt: e.tensor_scalar(
                xnb[:], xt, stats_c[:, 1:2], stats_a[:, 3:4], ALU.mult, ALU.add), 's_v')
            apply_done[t] = ap
            if which == 1 and t + 2 < NT:
                stage_dma(t + 2)
            P.wait('v', 's_v', ev2)
            P.vdep()
            P.do('v', lambda e: e.tensor_mul(xnb2[:], xnb[:], S1[:]), 's_v')
            P.vdep()
            md = P.do('v', lambda e: e.tensor_add(xnb[:], xnb2[:], S2[:]), 's_v')
            P.wait('t', 's_v', md)
            for j in range(8):
                pb = PSb if j % 2 == 0 else PSb2
                ev = free_bank.pop(('tr', j % 2), None)
                if ev is not None:
                    P.wait('t', ev[0], ev[1])
                tr = P.do('t', lambda e, pb=pb, j=j: e.transpose(
                    pb[:, 0:128], xnb[:, j * 128:(j + 1) * 128], identb_s[:]), 's_t')
                P.wait('v', 's_t', tr)
                cp = P.do('v', lambda e, pb=pb, j=j, t=t: e.tensor_copy(
                    h_buf[:, j * 2048 + t * 128: j * 2048 + (t + 1) * 128],
                    pb[:, 0:128]), 's_v')
                free_bank[('tr', j % 2)] = ('s_v', cp)
        return P.cnt['s_v']

    ffn_evac = {}
    out_dma_total = 0
    w12_seq = [0]

    for b in range(NB):
        if not _before(upto, b, "ln1"):
            break
        if b == 1:
            for eng in "tvsgy":
                P.wait(eng, 'd_xo', out_dma_total)

        hv = ln_phase(b, 1)
        if b == 0 and tap("S1", [128, 1024], BF16):
            P.wait('y', 's_v')
            P.do('y', lambda e: e.dma_start(out=dbg["S1"], in_=S1[:]), 'd_db', 16)
        if b == 0 and tap("S2", [128, 1024], BF16):
            P.wait('y', 's_v')
            P.do('y', lambda e: e.dma_start(out=dbg["S2"], in_=S2[:]), 'd_db', 16)
        if b == 0 and tap("rowt", [1, 1024], BF16):
            P.wait('y', 's_v')
            P.do('y', lambda e: e.dma_start(out=dbg["rowt"], in_=rowt[:]), 'd_db', 16)
        if b == 0 and tap("xnb", [128, 1024], BF16):
            P.wait('y', 's_v')
            P.do('y', lambda e: e.dma_start(out=dbg["xnb"], in_=xnb[:]), 'd_db', 16)
        if b == 0 and tap("sts", [128, 12], F32):
            P.wait('y', 's_v')
            P.do('y', lambda e: e.dma_start(out=dbg["sts"][:, 0:4], in_=stats_a[:]), 'd_db', 16)
            P.do('y', lambda e: e.dma_start(out=dbg["sts"][:, 4:8], in_=stats_b[:]), 'd_db', 16)
            P.do('y', lambda e: e.dma_start(out=dbg["sts"][:, 8:12], in_=stats_c[:]), 'd_db', 16)
        if b == 0 and tap("ln1", [128, 16384], BF16):
            P.wait('y', 's_v', hv)
            P.do('y', lambda e: e.dma_start(out=dbg["ln1"], in_=h_buf[:]), 'd_db', 16)
        if not _before(upto, b, "attn"):
            break

        for buf in (Qta, Qtb, Kta, Ktb):
            P.do('v', lambda e, bf=buf: e.memset(bf[:], 0.0), 's_v')
        P.do('v', lambda e: e.memset(Vab[:], 0.0), 's_v')
        P.do('v', lambda e: e.memset(Vab[:, 64:16 * 196:196], 1.0), 's_v')
        ms_done = P.do('v', lambda e: e.memset(Vab[:, 100:16 * 196:196], 1.0), 's_v')

        if b == 1:
            P.wait('g', 's_t', P.cnt['s_t'])
        P.do('g', lambda e: e.dma_start(
            out=w3sb[:, 0:8192],
            in_=wout_d.rearrange("kc kp m -> kp kc m")), 'd_w3', 16)
        wout_target = P.cnt['d_w3']

        def load_pair_tables(p):
            P.do('y', lambda e, p=p: e.dma_start(out=Kta[64:68, :], in_=khlA_d[2 * p]), 'd_kr', 16)
            P.do('y', lambda e, p=p: e.dma_start(out=Qta[64:68, :], in_=qA_d[2 * p]), 'd_kr', 16)
            P.do('y', lambda e, p=p: e.dma_start(out=Ktb[32:36, :], in_=khlB_d[2 * p + 1]), 'd_kr', 16)
            P.do('y', lambda e, p=p: e.dma_start(out=Qtb[32:36, :], in_=qB_d[2 * p + 1]), 'd_kr', 16)

        def load_pair_w(p):
            P.do('g', lambda e, p=p: e.dma_start(
                out=wqkvsb[:],
                in_=wqkv_d[p].rearrange("(kc kp) j -> kp kc j", kp=128)), 'd_wq', 16)

        P.wait('y', 's_v', ms_done)
        load_pair_tables(0)
        P.wait('g', 's_v', hv)
        load_pair_w(0)

        exp_bank = {}
        pv_slot = {0: 0, 1: 0, 2: 0, 3: 0}

        for p in range(8):
            P.wait('t', 'd_wq', 16 * (p + 1) + (16 * 8 * b))
            P.wait('t', 's_v', hv)
            for role, dstA, dstB, scale in ((0, Qta, Qtb, 0.125), (1, Kta, Ktb, 1.0)):
                for bki in range(4):
                    bank_wait('t', bki)
                if p == 0 and role == 0:
                    P.wait('t', 's_v', ms_done)
                mm = None
                for K in range(8):
                    for ncx in range(4):
                        mm = P.do('t', lambda e, K=K, ncx=ncx, role=role: e.matmul(
                            bank(ncx),
                            wqkvsb[:, K * 384 + role * 128: K * 384 + (role + 1) * 128],
                            h_buf[:, K * 2048 + ncx * 512: K * 2048 + (ncx + 1) * 512],
                            start=(K == 0), stop=(K == 7)), 's_t')
                for ncx in range(4):
                    P.wait('s', 's_t', mm)
                    P.do('s', lambda e, dstA=dstA, ncx=ncx, scale=scale: e.activation(
                        dstA[0:64, ncx * 512:(ncx + 1) * 512],
                        PS[0:64, ncx * 512:(ncx + 1) * 512], AF.Copy, scale=scale), 's_s')
                    ev = P.do('s', lambda e, dstB=dstB, ncx=ncx, scale=scale: e.activation(
                        dstB[64:128, ncx * 512:(ncx + 1) * 512],
                        PS[64:128, ncx * 512:(ncx + 1) * 512], AF.Copy, scale=scale), 's_s')
                    free_bank[ncx] = ('s_s', ev)
            for tt in range(NT):
                bki = 4 + tt % 2
                bank_wait('t', bki)
                mm = None
                for K in range(8):
                    mm = P.do('t', lambda e, K=K, tt=tt, bki=bki: e.matmul(
                        PS[:, bki * 512:bki * 512 + 128],
                        h_buf[:, K * 2048 + tt * 128: K * 2048 + (tt + 1) * 128],
                        wqkvsb[:, K * 384 + 256: K * 384 + 384],
                        start=(K == 0), stop=(K == 7)), 's_t')
                P.wait('v', 's_t', mm)
                if p == 0 and tt < 2:
                    P.wait('v', 's_v', ms_done)
                P.do('v', lambda e, tt=tt, bki=bki: e.tensor_copy(
                    Vab[:, tt * 196: tt * 196 + 64], PS[:, bki * 512:bki * 512 + 64]), 's_v')
                ev = P.do('v', lambda e, tt=tt, bki=bki: e.tensor_copy(
                    Vab[:, tt * 196 + 132: tt * 196 + 196],
                    PS[:, bki * 512 + 64:bki * 512 + 128]), 's_v')
                free_bank[bki] = ('s_v', ev)
            qkv_done_t = P.cnt['s_t']
            qkv_done_s = P.cnt['s_s']
            qkv_done_v = P.cnt['s_v']

            if p < 7:
                P.wait('g', 's_t', qkv_done_t)
                load_pair_w(p + 1)

            P.wait('t', 'd_kr', P.cnt['d_kr'])
            P.wait('t', 's_s', qkv_done_s)
            P.wait('t', 's_v', qkv_done_v)
            pend = [None]

            def emit_tail():
                if pend[0] is None:
                    return
                orow, orows, obank, pp, qq0, tsl, rc = pend[0]
                pend[0] = None
                P.wait('s', 's_t', P.mark[('pvlast', obank)])
                if ('ntail', tsl) in P.mark:
                    P.wait('s', 's_v', P.mark[('ntail', tsl)])
                oc = P.do('s', lambda e, orows=orows, obank=obank, tsl=tsl: e.activation(
                    tmp_gb[orows, tsl * 512:(tsl + 1) * 512],
                    PS[orows, obank * 512:(obank + 1) * 512], AF.Copy), 's_s')
                P.wait('t', 's_v', rc)
                P.wait('t', 's_s', oc)
                sb = P.do('t', lambda e, orow=orow, obank=obank: e.matmul(
                    bank(obank), indat_s[:, 128:256] if orow == 64 else indat_s[:, 0:128],
                    s_sums[:], start=True, stop=True), 's_t')
                P.wait('v', 's_t', sb)
                P.wait('v', 's_s', oc)
                nm = P.do('v', lambda e, orows=orows, obank=obank, pp=pp, qq0=qq0, tsl=tsl: e.tensor_mul(
                    og[orows, pp * 2048 + qq0: pp * 2048 + qq0 + 512],
                    tmp_gb[orows, tsl * 512:(tsl + 1) * 512],
                    PS[orows, obank * 512:(obank + 1) * 512]), 's_v')
                P.mark[('ntail', tsl)] = nm
                free_bank[obank] = ('s_v', nm)

            for hh in range(2):
                if hh == 0:
                    Qx, Kx, krows, orow = Qta, Kta, 128, 64
                    orows = slice(0, 64)
                    vlo, vhi = 0, 128
                else:
                    Qx, Kx, krows, orow = Qtb, Ktb, 128, 32
                    orows = slice(64, 128)
                    vlo, vhi = 68, 196
                sbanks = [0, 1, 4, 5]
                for qb in range(4):
                    q0 = qb * 512
                    kc0 = 4 * qb
                    oq = P.mark.get('oq', 0)
                    P.mark['oq'] = oq + 1
                    obank = 2 + oq % 2
                    tsl = oq % 2
                    bank_wait('t', obank)
                    pv = None
                    kcs = list(range(kc0, 16))
                    LA = 3
                    exk = {}
                    pmk = {}

                    def emit_pv(kc):
                        nonlocal pv
                        slot = (kc - kc0) % 4
                        P.wait('t', 's_s', exk[kc])
                        if pmk.get(kc) is not None:
                            P.wait('t', 's_v', pmk[kc])
                        pv = P.do('t', lambda e, kc=kc, slot=slot, obank=obank, vlo=vlo, vhi=vhi, kc0=kc0: e.matmul(
                            PS[0:(vhi - vlo), obank * 512:(obank + 1) * 512],
                            Vab[:, kc * 196 + vlo: kc * 196 + vhi],
                            Pt[:, slot * 512:(slot + 1) * 512],
                            start=(kc == kc0), stop=(kc == 15)), 's_t')
                        pv_slot[slot] = pv

                    for i, kc in enumerate(kcs):
                        sbk = sbanks[(kc - kc0) % 4]
                        slot = (kc - kc0) % 4
                        bank_wait('t', sbk)
                        P.wait('t', 's_s', exp_bank.get(sbk, 0))
                        mmS = P.do('t', lambda e, Kx=Kx, kc=kc, q0=q0, krows=krows, sbk=sbk, Qx=Qx: e.matmul(
                            bank(sbk), Kx[0:krows, kc * 128:(kc + 1) * 128],
                            Qx[0:krows, q0:q0 + 512], start=True, stop=True), 's_t')
                        if i == min(3, len(kcs) - 1):
                            emit_tail()
                        diag = kc0 <= kc < kc0 + 4
                        r = kc * 128 - q0
                        vmask = None
                        if diag:
                            P.wait('v', 's_t', mmS)
                            vmask = P.do('v', lambda e, sbk=sbk, r=r: e.tensor_add(
                                PS[:, sbk * 512 + r: sbk * 512 + r + 128],
                                PS[:, sbk * 512 + r: sbk * 512 + r + 128], tri_s[:]), 's_v')
                        w = r + 128 if diag else 512
                        P.wait('s', 's_t', mmS)
                        if vmask is not None:
                            P.wait('s', 's_v', vmask)
                        P.wait('s', 's_t', pv_slot[slot])
                        ex = P.do('s', lambda e, slot=slot, sbk=sbk, w=w: e.activation(
                            Pt[:, slot * 512: slot * 512 + w],
                            PS[:, sbk * 512: sbk * 512 + w], AF.Exp, bias=-BSHIFT), 's_s')
                        exp_bank[sbk] = ex
                        exk[kc] = ex
                        pmem = None
                        if diag and w < 512:
                            P.wait('v', 's_t', pv_slot[slot])
                            pmem = P.do('v', lambda e, slot=slot, w=w: e.memset(
                                Pt[:, slot * 512 + w: (slot + 1) * 512], 0.0), 's_v')
                        pmk[kc] = pmem
                        if i >= LA:
                            emit_pv(kcs[i - LA])
                    for kc in kcs[-LA:]:
                        emit_pv(kc)
                    P.mark[('pvlast', obank)] = pv
                    P.wait('v', 's_t', pv)
                    rc = P.do('v', lambda e, orow=orow, obank=obank: e.reciprocal(
                        s_sums[orow:orow + 1, :],
                        PS[orow:orow + 1, obank * 512:(obank + 1) * 512]), 's_v')
                    pend[0] = (orow, orows, obank, p, q0, tsl, rc)
            emit_tail()
            if p < 7:
                P.wait('y', 's_t', P.cnt['s_t'])
                load_pair_tables(p + 1)

        attn_done_v = P.cnt['s_v']
        if b == 0 and tap("og", [128, 16384], BF16):
            P.wait('y', 's_v', attn_done_v)
            P.do('y', lambda e: e.dma_start(out=dbg["og"], in_=og[:]), 'd_db', 16)
        if not _before(upto, b, "oproj"):
            break

        ga_ev = build_bcast(2, b, S3)
        P.wait('t', 'd_w3', wout_target)
        P.wait('t', 's_v', attn_done_v)
        evac_op = {}

        def oproj_dma(t):
            slot = t % 2
            sem = 'd_xsa' if slot == 0 else 'd_xsb'
            if t - 2 in evac_op:
                P.wait('y', 's_v', evac_op[t - 2])
            else:
                P.wait('y', 's_v', attn_done_v)
            P.do('y', lambda e, b=b, t=t, slot=slot: e.dma_start(
                out=xstg[:, slot * 1024:(slot + 1) * 1024],
                in_=x_d[b, t * 128:(t + 1) * 128, :]), sem, 16)
            P.mark[('opx', b, t)] = P.cnt[sem]

        oproj_dma(0)
        oproj_dma(1)
        for t in range(NT):
            slot = t % 2
            sem = 'd_xsa' if slot == 0 else 'd_xsb'
            b0k, b1k = 2 * (t % 2), 2 * (t % 2) + 1
            bank_wait('t', b0k)
            bank_wait('t', b1k)
            mm = None
            for K in range(8):
                for ncx in range(2):
                    mm = P.do('t', lambda e, K=K, t=t, b0k=b0k, ncx=ncx: e.matmul(
                        PS[:, b0k * 512 + ncx * 512: b0k * 512 + (ncx + 1) * 512],
                        og[:, K * 2048 + t * 128: K * 2048 + (t + 1) * 128],
                        w3sb[:, K * 1024 + ncx * 512: K * 1024 + (ncx + 1) * 512],
                        start=(K == 0), stop=(K == 7)), 's_t')
            P.wait('v', 's_t', mm)
            P.wait('v', sem, P.mark[('opx', b, t)])
            P.wait('v', 's_v', ga_ev)
            P.do('v', lambda e, b0k=b0k: e.tensor_mul(
                tmp_gb[:], PS[:, b0k * 512: b0k * 512 + 1024], S3[:]), 's_v')
            P.vdep()
            ev = P.do('v', lambda e, t=t, slot=slot: e.tensor_add(
                x_buf[:, t * 1024:(t + 1) * 1024],
                xstg[:, slot * 1024:(slot + 1) * 1024], tmp_gb[:]), 's_v')
            evac_op[t] = ev
            free_bank[b0k] = ('s_v', ev)
            free_bank[b1k] = ('s_v', ev)
            if t + 2 < NT:
                oproj_dma(t + 2)
        oproj_done_v = P.cnt['s_v']
        if b == 0 and tap("x2", [128, 16384], F32):
            P.wait('y', 's_v', oproj_done_v)
            P.do('y', lambda e: e.dma_start(out=dbg["x2"], in_=x_buf[:]), 'd_db', 16)
        if not _before(upto, b, "ln2"):
            break

        P.wait('v', 's_t', P.cnt['s_t'])
        hv2 = ln_phase(b, 2)
        if b == 0 and tap("h2", [128, 16384], BF16):
            P.wait('y', 's_v', hv2)
            P.do('y', lambda e: e.dma_start(out=dbg["h2"], in_=h_buf[:]), 'd_db', 16)
        if not _before(upto, b, "gate"):
            break

        wg_done = {}
        for t in range(NT):
            bank_wait('t', 4)
            P.wait('t', 's_v', hv2)
            mm = None
            for K in range(8):
                mm = P.do('t', lambda e, K=K, t=t: e.matmul(
                    PS[:, 4 * 512: 4 * 512 + 8],
                    h_buf[:, K * 2048 + t * 128: K * 2048 + (t + 1) * 128],
                    gatewsb[:, K * 8:(K + 1) * 8],
                    start=(K == 0), stop=(K == 7)), 's_t')
            P.wait('v', 's_t', mm)
            P.wait('v', 's_s')
            P.do('v', lambda e: e.tensor_copy(gA[:, 0:8], PS[:, 4 * 512:4 * 512 + 8]), 's_v')
            P.vdep()
            ev = P.do('v', lambda e: e.reduce_max(gB[:, 0:1], gA[:, 0:8], axis=AXX, negate=True), 's_v')
            free_bank[4] = ('s_v', ev)
            P.vdep()
            P.do('v', lambda e: e.tensor_scalar_add(gC[:, 0:8], gA[:, 0:8], gB[:, 0:1]), 's_v')
            P.vdep()
            P.do('v', lambda e: e.tensor_single_scalar(gB[:, 1:9], gC[:, 0:8], 0.0, ALU.is_ge), 's_v')
            P.vdep()
            P.do('v', lambda e: e.scalar_tensor_tensor(
                gC[:, 8:16], gB[:, 1:9], -1e30, gA[:, 0:8], ALU.mult, ALU.add), 's_v')
            P.vdep()
            vm = P.do('v', lambda e: e.reduce_max(gB[:, 9:10], gC[:, 8:16], axis=AXX, negate=True), 's_v')
            P.wait('s', 's_v', vm)
            ex = P.do('s', lambda e: e.activation(gC[:, 0:8], gA[:, 0:8], AF.Exp, bias=gB[:, 0:1]), 's_s')
            P.vdep()
            P.do('v', lambda e: e.tensor_scalar_add(gtmp[:, 0:8], gA[:, 0:8], gB[:, 9:10]), 's_v')
            P.vdep()
            P.do('v', lambda e: e.tensor_single_scalar(gB[:, 1:9], gtmp[:, 0:8], 0.0, ALU.is_ge), 's_v')
            P.wait('v', 's_s', ex)
            P.vdep()
            P.do('v', lambda e: e.tensor_mul(gA[:, 8:16], gC[:, 0:8], gB[:, 1:9]), 's_v')
            P.vdep()
            P.do('v', lambda e: e.reduce_sum(gB[:, 10:11], gA[:, 8:16], axis=AXX), 's_v')
            P.vdep()
            P.do('v', lambda e: e.reciprocal(gC[:, 1:2], gB[:, 10:11]), 's_v')
            P.vdep()
            wg = P.do('v', lambda e, t=t: e.tensor_scalar_mul(
                Wg[:, t * 8:(t + 1) * 8], gA[:, 8:16], gC[:, 1:2]), 's_v')
            wg_done[t] = wg
        bank_wait('t', 4)
        bank_wait('t', 5)
        for t in range(NT):
            sl = (4 + t % 2) * 512
            ev = free_bank.pop(('gt', t % 2), None)
            if ev is not None:
                P.wait('t', ev[0], ev[1])
            P.wait('t', 's_v', wg_done[t])
            tr = P.do('t', lambda e, t=t, sl=sl: e.transpose(
                PS[0:8, sl:sl + 128], Wg[:, t * 8:(t + 1) * 8], identf_s[:]), 's_t')
            P.wait('v', 's_t', tr)
            cp = P.do('v', lambda e, t=t, sl=sl: e.tensor_copy(
                Wrow[0:8, t * 128:(t + 1) * 128], PS[0:8, sl:sl + 128]), 's_v')
            free_bank[('gt', t % 2)] = ('s_v', cp)
        gate_done_v = P.cnt['s_v']
        free_bank[4] = ('s_v', gate_done_v)
        free_bank[5] = ('s_v', gate_done_v)
        if b == 0 and tap("wrow", [8, 2048], BF16):
            P.wait('y', 's_v', gate_done_v)
            P.do('y', lambda e: e.dma_start(out=dbg["wrow"], in_=Wrow[:]), 'd_db', 16)
        if not _before(upto, b, "ffn"):
            break

        gb_ev = build_bcast(5, b, S3)
        P.wait('v', 's_t', P.cnt['s_t'])
        og_free_v = P.cnt['s_v']
        w12_done = {}
        w12_fill = {}
        glast = None

        def w12_dma(e_i, s, M):
            blk = w12_seq[0]
            w12_seq[0] += 1
            slot12 = blk % 2
            sem12 = 'd_w12a' if slot12 == 0 else 'd_w12b'
            if blk - 2 in w12_done:
                P.wait('g', 's_t', w12_done[blk - 2])
            P.do('g', lambda e, e_i=e_i, M=M, slot12=slot12: e.dma_start(
                out=w12sb[:, slot12 * 2048:(slot12 + 1) * 2048],
                in_=w12_d[e_i, M]), sem12, 16)
            w12_fill[blk] = P.cnt[sem12]
            return blk

        pending = []
        for e_i in range(NE):
            for s in range(2):
                for M in range(16):
                    pending.append((e_i, s, M))
        pend_i = [0]
        def pump_w12():
            if pend_i[0] < len(pending):
                e_i, s, M = pending[pend_i[0]]
                pend_i[0] += 1
                return w12_dma(e_i, s, M)
            return None
        blk0 = pump_w12()
        blk1 = pump_w12()

        blkc = [blk0 if blk0 is not None else 0]
        blk_iter = [w12_seq[0] - 2]

        for e_i in range(NE):
            P.wait('g', 's_t', P.cnt['s_t'])
            P.do('g', lambda e, e_i=e_i: e.dma_start(
                out=w3sb[:],
                in_=w3_d[e_i].rearrange("kh kp m -> kp kh m")), 'd_w3', 16)
            w3_target = P.cnt['d_w3']
            for s in range(2):
                if e_i < 8:
                    bank_wait('t', 4)
                    bank_wait('t', 5)
                    P.wait('t', 's_v', gate_done_v)
                    P.do('t', lambda e, e_i=e_i, s=s: e.matmul(
                        PS[:, 2048:2048 + 512], ind8_s[:, e_i * 128:(e_i + 1) * 128],
                        Wrow[:, s * 1024: s * 1024 + 512], start=True, stop=True), 's_t')
                    wbm = P.do('t', lambda e, e_i=e_i, s=s: e.matmul(
                        PS[:, 2560:2560 + 512], ind8_s[:, e_i * 128:(e_i + 1) * 128],
                        Wrow[:, s * 1024 + 512: (s + 1) * 1024], start=True, stop=True), 's_t')
                    P.wait('v', 's_t', wbm)
                    wbe = P.do('v', lambda e: e.tensor_copy(Wb[:], PS[:, 2048:3072]), 's_v')
                    free_bank[4] = ('s_v', wbe)
                    free_bank[5] = ('s_v', wbe)
                else:
                    wbe = None
                for M in range(16):
                    blk = blk_iter[0]
                    blk_iter[0] += 1
                    slot12 = blk % 2
                    sem12 = 'd_w12a' if slot12 == 0 else 'd_w12b'
                    sA = (2 * M) % 3
                    sB = (2 * M + 1) % 3
                    P.wait('t', sem12, w12_fill[blk])
                    if M == 0:
                        P.wait('t', 's_v', hv2)
                        if wbe is not None:
                            P.wait('t', 's_v', wbe)
                        P.wait('t', 's_v', og_free_v)
                    mh1 = mh2 = None
                    for half, bki in ((0, 2 * sA), (1, 2 * sA + 1)):
                        bank_wait('t', bki)
                        for K in range(8):
                            mh1 = P.do('t', lambda e, K=K, half=half, bki=bki, slot12=slot12, s=s: e.matmul(
                                bank(bki),
                                w12sb[:, slot12 * 2048 + K * 128:slot12 * 2048 + (K + 1) * 128],
                                h_buf[:, K * 2048 + s * 1024 + half * 512: K * 2048 + s * 1024 + (half + 1) * 512],
                                start=(K == 0), stop=(K == 7)), 's_t')
                    for half, bki in ((0, 2 * sB), (1, 2 * sB + 1)):
                        bank_wait('t', bki)
                        for K in range(8):
                            mh2 = P.do('t', lambda e, K=K, half=half, bki=bki, slot12=slot12, s=s: e.matmul(
                                bank(bki),
                                w12sb[:, slot12 * 2048 + 1024 + K * 128:slot12 * 2048 + 1024 + (K + 1) * 128],
                                h_buf[:, K * 2048 + s * 1024 + half * 512: K * 2048 + s * 1024 + (half + 1) * 512],
                                start=(K == 0), stop=(K == 7)), 's_t')
                    w12_done[blk] = P.cnt['s_t']
                    pump_w12()
                    P.wait('s', 's_t', mh1)
                    if glast is not None:
                        P.wait('s', 's_v', glast)
                    si = P.do('s', lambda e, sA=sA: e.activation(
                        tmp_si[:], PS[:, 2 * sA * 512: 2 * sA * 512 + 1024], AF.Silu), 's_s')
                    free_bank[2 * sA] = ('s_s', si)
                    free_bank[2 * sA + 1] = ('s_s', si)
                    P.wait('v', 's_t', mh2)
                    P.wait('v', 's_s', si)
                    siop = tmp_si
                    if e_i < 8:
                        P.vdep()
                        P.do('v', lambda e: e.tensor_mul(tmp_si2[:], tmp_si[:], Wb[:]), 's_v')
                        P.vdep()
                        siop = tmp_si2
                    gv = P.do('v', lambda e, sB=sB, M=M, siop=siop: e.tensor_mul(
                        og[:, M * 1024:(M + 1) * 1024], siop[:],
                        PS[:, 2 * sB * 512: 2 * sB * 512 + 1024]), 's_v')
                    free_bank[2 * sB] = ('s_v', gv)
                    free_bank[2 * sB + 1] = ('s_v', gv)
                    glast = gv
                P.wait('t', 'd_w3', w3_target)
                P.wait('t', 's_v', glast)
                for t in range(8):
                    tt = 8 * s + t
                    b0k, b1k = 2 * (t % 2), 2 * (t % 2) + 1
                    bank_wait('t', b0k)
                    bank_wait('t', b1k)
                    mm = None
                    for K in range(16):
                        for ncx in range(2):
                            mm = P.do('t', lambda e, K=K, t=t, b0k=b0k, ncx=ncx: e.matmul(
                                PS[:, b0k * 512 + ncx * 512: b0k * 512 + (ncx + 1) * 512],
                                og[:, K * 1024 + t * 128: K * 1024 + (t + 1) * 128],
                                w3sb[:, K * 1024 + ncx * 512: K * 1024 + (ncx + 1) * 512],
                                start=(K == 0), stop=(K == 15)), 's_t')
                    P.wait('v', 's_t', mm)
                    P.wait('v', 's_v', gb_ev)
                    P.do('v', lambda e, b0k=b0k: e.tensor_mul(
                        tmp_gb[:], PS[:, b0k * 512: b0k * 512 + 1024], S3[:]), 's_v')
                    P.vdep()
                    ev = P.do('v', lambda e, tt=tt: e.tensor_add(
                        x_buf[:, tt * 1024:(tt + 1) * 1024],
                        x_buf[:, tt * 1024:(tt + 1) * 1024], tmp_gb[:]), 's_v')
                    ffn_evac[(b, s, t)] = ev
                    free_bank[b0k] = ('s_v', ev)
                    free_bank[b1k] = ('s_v', ev)
        if not _before(upto, b, "out"):
            break

        for tt in range(NT):
            P.wait('y', 's_v', ffn_evac[(b, tt // 8, tt % 8)])
            P.do('y', lambda e, b=b, tt=tt: e.dma_start(
                out=y_d[b, tt * 128:(tt + 1) * 128, :],
                in_=x_buf[:, tt * 1024:(tt + 1) * 1024]), 'd_xo', 16)
        out_dma_total = P.cnt['d_xo']

    P.wait('y', 'd_xo', P.cnt.get('d_xo', 0))
    P.wait('y', 'd_db', P.cnt.get('d_db', 0))

    semnames = ['s_t', 's_v', 's_s', 'd_xsa', 'd_xsb', 'd_xo', 'd_kr', 'd_wq',
                'd_w12a', 'd_w12b', 'd_w3', 'd_wm', 'd_wmb', 'd_misc', 'd_db']
    with contextlib.ExitStack() as stack:
        stack.enter_context(nc.allow_low_precision(
            reason="bf16 matmul operands / normalizers; fp32 accumulation kept"))
        for s in semnames:
            P.sem[s] = stack.enter_context(nc.semaphore(s))
        block = stack.enter_context(nc.Block())
        block.gpsimd(lambda e: [f(e) for f in P.ops['g']])
        block.scalar(lambda e: [f(e) for f in P.ops['s']])
        block.tensor(lambda e: [f(e) for f in P.ops['t']])
        block.vector(lambda e: [f(e) for f in P.ops['v']])
        block.sync(lambda e: [f(e) for f in P.ops['y']])
    return nc, dbg


def kernel(**inputs):
    lay = host_layouts(inputs)
    x = np.ascontiguousarray(np.asarray(inputs["L0_attn_x"], dtype=np.float32))
    cc = np.ascontiguousarray(np.asarray(inputs["L0_c"], dtype=np.float32))
    nc, dbg = build_program(taps=tuple(_TAPS), upto=_UPTO)
    in_maps = []
    for c in range(NCORES):
        in_maps.append({
            "x": np.ascontiguousarray(x[2 * c:2 * c + 2]),
            "cvec": np.ascontiguousarray(cc[2 * c:2 * c + 2]),
            "wqkv": lay["wqkv"], "wout": lay["wout"], "gatew": lay["gate"],
            "wmod": lay["wmod"], "w12": lay["w12"], "w3": lay["w3"],
        })
    res = run_bass_kernel_spmd(nc, in_maps, core_ids=list(range(NCORES)))
    out = np.empty((16, T, D), dtype=np.float32)
    for c in range(NCORES):
        out[2 * c:2 * c + 2] = res.results[c]["y"]
    if _TAPS:
        kernel._dbg = [{k: r["dbg_" + k] for k in _TAPS} for r in res.results]
    return out


# version 04
# speedup vs baseline: 1.2270x; 1.0307x over previous
import contextlib
import math

import numpy as np
import ml_dtypes

import concourse.bass as bass
import concourse.mybir as mybir
from concourse.bass_utils import run_bass_kernel_spmd

F32 = mybir.dt.float32
BF16 = mybir.dt.bfloat16
AF = mybir.ActivationFunctionType
ALU = mybir.AluOpType
AXX = mybir.AxisListType.X
BFD = ml_dtypes.bfloat16

NCORES = 8
NB = 2
T = 2048
D = 1024
NT = 16
NH = 16
NE = 9
BSHIFT = 12.0

_UPTO = None
_TAPS = []


def _slopes():
    start = 2 ** (-2 ** (-(math.log2(NH) - 3)))
    return np.array([start * start**i for i in range(NH)], dtype=np.float32)


def host_layouts(inp):
    f32 = np.float32
    Wqkv = np.ascontiguousarray(inp["L0_attn_W_qkv"], dtype=f32)
    Wout = np.ascontiguousarray(inp["L0_attn_W_out"], dtype=f32)
    gw = np.ascontiguousarray(inp["L0_ffn_gate_w"], dtype=f32)
    ew1 = np.asarray(inp["L0_ffn_ew1"], dtype=f32)
    ew2 = np.asarray(inp["L0_ffn_ew2"], dtype=f32)
    ew3 = np.asarray(inp["L0_ffn_ew3"], dtype=f32)
    sw1 = np.asarray(inp["L0_ffn_sw1"], dtype=f32)
    sw2 = np.asarray(inp["L0_ffn_sw2"], dtype=f32)
    sw3 = np.asarray(inp["L0_ffn_sw3"], dtype=f32)
    wmod = np.ascontiguousarray(inp["L0_wmod"], dtype=f32)

    wqkv_l = np.ascontiguousarray(
        Wqkv.reshape(1024, 3, 8, 128).transpose(2, 0, 1, 3).reshape(8, 1024, 384))
    wout_l = np.ascontiguousarray(Wout.reshape(8, 128, 1024))
    gate_l = np.ascontiguousarray(gw.reshape(8, 128, 8))
    wmod_l = np.ascontiguousarray(wmod.reshape(8, 128, 6144))
    w1 = np.concatenate([ew1, sw1[None]], axis=0)
    w2 = np.concatenate([ew2, sw2[None]], axis=0)
    w3 = np.concatenate([ew3, sw3[None]], axis=0)
    t1 = w1.reshape(9, 8, 128, 16, 128).transpose(0, 3, 2, 1, 4).reshape(9, 16, 128, 1024)
    t2 = w2.reshape(9, 8, 128, 16, 128).transpose(0, 3, 2, 1, 4).reshape(9, 16, 128, 1024)
    w12_l = np.ascontiguousarray(np.concatenate([t1, t2], axis=3))
    w3_l = np.ascontiguousarray(w3.reshape(9, 16, 128, 1024))
    return dict(wqkv=wqkv_l, wout=wout_l, gate=gate_l, wmod=wmod_l,
                w12=w12_l, w3=w3_l)


def host_consts():
    sl = _slopes()
    k = np.arange(T, dtype=np.float32)
    bias = -sl[:, None] * k[None, :]
    bhi = bias.astype(BFD).astype(np.float32)
    blo = (bias - bhi).astype(BFD)
    qrow = (sl[:, None] * k[None, :]).astype(np.float32)
    ones = np.ones((16, T), np.float32)
    zero = np.zeros((16, T), np.float32)
    khlA = np.stack([bhi, blo.astype(np.float32), ones, zero], 1).astype(BFD)
    qA = np.stack([ones, ones, qrow, zero], 1).astype(BFD)
    khlB = np.stack([ones, bhi, blo.astype(np.float32), zero], 1).astype(BFD)
    qB = np.stack([qrow, ones, ones, zero], 1).astype(BFD)
    identb = np.eye(128, dtype=BFD)
    identf = np.eye(128, dtype=np.float32)
    kk = np.arange(128)
    tri = np.where(kk[:, None] >= kk[None, :], 0.0, -1e30).astype(BFD)
    ind32 = ((kk[:, None] == 32) * np.ones((1, 128))).astype(BFD)
    ind64 = ((kk[:, None] == 64) * np.ones((1, 128))).astype(BFD)
    indat = np.ascontiguousarray(np.concatenate([ind32, ind64], axis=1))
    e8 = np.arange(8)
    ind8 = np.repeat((e8[:, None, None] == e8[None, :, None]), 128, axis=2)
    ind8 = np.ascontiguousarray(ind8.reshape(8, 1024).astype(BFD))
    cvals = np.ascontiguousarray(
        np.tile(np.array([[1e-5, -BSHIFT]], np.float32), (128, 1)))
    return dict(khlA=np.ascontiguousarray(khlA), qA=np.ascontiguousarray(qA),
                khlB=np.ascontiguousarray(khlB), qB=np.ascontiguousarray(qB),
                identb=identb, identf=identf, tri=tri, indat=indat, ind8=ind8,
                cvals=cvals)


class Prog:
    def __init__(self):
        self.ops = {k: [] for k in "tvsgy"}
        self.cnt = {}
        self.sem = {}
        self.mark = {}

    def raw(self, eng, fn):
        self.ops[eng].append(fn)

    def wait(self, eng, sem, val=None):
        v = self.cnt.get(sem, 0) if val is None else val
        assert v <= self.cnt.get(sem, 0), f"wait on future event {sem} {v} > {self.cnt.get(sem, 0)}"
        if v > 0:
            self.raw(eng, lambda e, s=sem, vv=v: e.wait_ge(self.sem[s], vv))

    def vdep(self):
        self.wait('v', 's_v')

    def do(self, eng, fn, inc=None, by=1):
        if inc is None:
            self.raw(eng, fn)
            return None
        self.cnt[inc] = self.cnt.get(inc, 0) + by
        def run(e, fn=fn, s=inc, b=by):
            fn(e).then_inc(self.sem[s], b)
        self.raw(eng, run)
        return self.cnt[inc]


STAGES = ["ln1", "attn", "oproj", "ln2", "gate", "ffn", "out"]


def _before(upto, b, stage):
    if upto is None:
        return True
    ub, us = upto
    if b != ub:
        return b < ub
    return STAGES.index(stage) <= STAGES.index(us)


def build_program(taps=(), upto=None):
    nc = bass.Bass("TRN2", target_bir_lowering=False)
    C = host_consts()

    x_d = nc.dram_tensor("x", [NB, T, D], F32, kind="ExternalInput").ap()
    c_d = nc.dram_tensor("cvec", [NB, D], F32, kind="ExternalInput").ap()
    wqkv_d = nc.dram_tensor("wqkv", [8, 1024, 384], F32, kind="ExternalInput").ap()
    wout_d = nc.dram_tensor("wout", [8, 128, 1024], F32, kind="ExternalInput").ap()
    gate_d = nc.dram_tensor("gatew", [8, 128, 8], F32, kind="ExternalInput").ap()
    wmod_d = nc.dram_tensor("wmod", [8, 128, 6144], F32, kind="ExternalInput").ap()
    w12_d = nc.dram_tensor("w12", [9, 16, 128, 2048], F32, kind="ExternalInput").ap()
    w3_d = nc.dram_tensor("w3", [9, 16, 128, 1024], F32, kind="ExternalInput").ap()
    y_d = nc.dram_tensor("y", [NB, T, D], F32, kind="ExternalOutput").ap()

    khlA_d = nc.inline_tensor(C["khlA"], name="khlA").ap()
    qA_d = nc.inline_tensor(C["qA"], name="qA").ap()
    khlB_d = nc.inline_tensor(C["khlB"], name="khlB").ap()
    qB_d = nc.inline_tensor(C["qB"], name="qB").ap()
    identb_d = nc.inline_tensor(C["identb"], name="identb").ap()
    identf_d = nc.inline_tensor(C["identf"], name="identf").ap()
    tri_d = nc.inline_tensor(C["tri"], name="tri").ap()
    indat_d = nc.inline_tensor(C["indat"], name="indat").ap()
    ind8_d = nc.inline_tensor(C["ind8"], name="ind8").ap()
    cvals_d = nc.inline_tensor(C["cvals"], name="cvals").ap()

    dbg = {}
    def tap(name, shape, dtype):
        if name in taps:
            dbg[name] = nc.dram_tensor("dbg_" + name, shape, dtype,
                                       kind="ExternalOutput").ap()
            return True
        return False

    ARENA = nc.sbuf_bytes_remaining - 2048
    base = nc.bump_sbuf(ARENA)[0]
    cur = [base]
    def alloc(name, shape, dtype):
        a = (cur[0] + 31) & ~31
        t = nc.alloc_sbuf_tensor_at(name, shape, dtype, offset=a)
        cur[0] = a + shape[1] * {F32: 4, BF16: 2}[dtype]
        return t

    x_buf = alloc("x_buf", [128, NT * 1024], F32)
    XOFF = nc.lookup_mloc(x_buf).addr
    h_buf = alloc("h_buf", [128, 8 * 2048], BF16)
    og = alloc("og", [128, 16384], BF16)
    w3sb = alloc("w3sb", [128, 16 * 1024], BF16)
    w12sb = alloc("w12sb", [128, 2 * 2048], BF16)
    Wb = alloc("Wb", [128, 1024], BF16)
    tmp_si = alloc("tmp_si", [128, 1024], BF16)
    tmp_gb = alloc("tmp_gb", [128, 1024], BF16)
    S1 = alloc("S1", [128, 1024], BF16)
    S2 = alloc("S2", [128, 1024], BF16)
    S3 = alloc("S3", [128, 1024], BF16)
    Wrow = alloc("Wrow", [8, 2048], BF16)
    Wg = alloc("Wg", [128, 128], F32)
    gatewsb = alloc("gatewsb", [128, 64], BF16)
    modsb = alloc("modsb", [128, 96], F32)
    dpb = alloc("dpb", [128, 96], BF16)
    gtmp = alloc("gtmp", [128, 32], F32)

    cfmb = alloc("cfmb", [128, 16], BF16)
    rowt = alloc("rowt", [1, 1024], BF16)
    identb_s = alloc("identb_s", [128, 128], BF16)
    identf_s = alloc("identf_s", [128, 128], F32)
    tri_s = alloc("tri_s", [128, 128], BF16)
    indat_s = alloc("indat_s", [128, 256], BF16)
    ind8_s = alloc("ind8_s", [8, 1024], BF16)
    onesb = alloc("onesb", [1, 128], BF16)
    s_sums = alloc("s_sums", [128, 512], BF16)
    sq_tmp = alloc("sq_tmp", [128, 1024], F32)
    csil = nc.alloc_sbuf_tensor_at("csil", [2, 1024], F32,
                                   offset=nc.lookup_mloc(sq_tmp).addr)
    xnb = alloc("xnb", [128, 1024], BF16)
    stats_a = alloc("stats_a", [128, 4], F32)
    stats_b = alloc("stats_b", [128, 4], F32)
    stats_c = alloc("stats_c", [128, 4], F32)
    gA = alloc("gA", [128, 16], F32)
    gB = alloc("gB", [128, 16], F32)
    gC = alloc("gC", [128, 16], F32)
    gtmp2 = alloc("gtmp2", [128, 32], F32)
    xnb2 = alloc("xnb2", [128, 1024], BF16)
    tmp_si2 = alloc("tmp_si2", [128, 1024], BF16)
    cvals_s = alloc("cvals_s", [128, 2], F32)
    nc.const_aps.register(F32, 1e-5, cvals_s[:, 0:1])
    nc.const_aps.register(F32, -BSHIFT, cvals_s[:, 1:2])
    assert cur[0] <= base + ARENA, f"sbuf over: {cur[0] - base} > {ARENA}"

    Qta = nc.alloc_sbuf_tensor_at("Qta", [128, 2048], BF16, offset=XOFF + 0)
    Qtb = nc.alloc_sbuf_tensor_at("Qtb", [128, 2048], BF16, offset=XOFF + 4096)
    Kta = nc.alloc_sbuf_tensor_at("Kta", [128, 2048], BF16, offset=XOFF + 8192)
    Ktb = nc.alloc_sbuf_tensor_at("Ktb", [128, 2048], BF16, offset=XOFF + 12288)
    Vab = nc.alloc_sbuf_tensor_at("Vab", [128, 16 * 196], BF16, offset=XOFF + 16384)

    wqkvsb = nc.alloc_sbuf_tensor_at("wqkvsb", [128, 3072], BF16, offset=XOFF + 24768)
    xstg = nc.alloc_sbuf_tensor_at("xstg", [128, 2048], F32, offset=XOFF + 57344)
    W3OFF = nc.lookup_mloc(w3sb).addr
    Pt = nc.alloc_sbuf_tensor_at("Pt", [128, 2048], BF16, offset=W3OFF + 16384)
    csb = nc.alloc_sbuf_tensor_at("csb", [2, 1024], F32, offset=XOFF + 40960)

    PS = nc.alloc_psum_tensor("PS", [128, 3072], F32)
    PSb = nc.alloc_psum_tensor("PSb", [128, 1024], BF16)
    PSb2 = nc.alloc_psum_tensor("PSb2", [128, 1024], BF16)
    def bank(i, w=512):
        return PS[:, i * 512:i * 512 + w]

    P = Prog()
    free_bank = {}

    def bank_wait(eng, *banks):
        for bk in banks:
            ev = free_bank.pop(bk, None)
            if ev is not None:
                P.wait(eng, ev[0], ev[1])

    for dst, src in [(identb_s, identb_d), (identf_s, identf_d), (tri_s, tri_d),
                     (indat_s, indat_d), (ind8_s, ind8_d)]:
        P.do('g', lambda e, d=dst, s=src: e.dma_start(out=d[:], in_=s), 'd_misc', 16)
    P.do('g', lambda e: e.dma_start(out=gatewsb[:],
         in_=gate_d.rearrange("kc kp e -> kp kc e")), 'd_misc', 16)
    P.do('g', lambda e: e.dma_start(out=csb[:], in_=c_d), 'd_misc', 16)
    P.do('g', lambda e: e.dma_start(out=cvals_s[:], in_=cvals_d), 'd_misc', 16)
    misc_total = P.cnt['d_misc']
    P.do('v', lambda e: e.memset(onesb[:], 1.0), 's_v')
    P.do('v', lambda e: e.memset(s_sums[:], 1.0), 's_v')

    P.wait('s', 'd_misc', misc_total)
    P.do('s', lambda e: e.activation(csil[:], csb[:], AF.Silu), 's_s')
    P.wait('t', 'd_misc', misc_total)
    P.wait('t', 's_s')
    cfm_ev = None
    for kc in range(8):
        if cfm_ev is not None:
            P.wait('t', 's_v', cfm_ev)
        tr = P.do('t', lambda e, kc=kc: e.transpose(
            PS[:, 5 * 512: 5 * 512 + 2],
            csil[0:2, kc * 128:(kc + 1) * 128], identf_s[0:2, 0:2]), 's_t')
        P.wait('v', 's_t', tr)
        cfm_ev = P.do('v', lambda e, kc=kc: e.tensor_copy(
            cfmb[:, kc * 2:(kc + 1) * 2], PS[:, 5 * 512:5 * 512 + 2]), 's_v')

    wm_done = {}
    def emit_wmod_dma(kc):
        if kc - 2 in wm_done:
            P.wait('g', 's_t', wm_done[kc - 2])
        sem = 'd_wm' if kc % 2 == 0 else 'd_wmb'
        P.do('g', lambda e, kc=kc: e.dma_start(
            out=h_buf[:, (kc % 2) * 6144:((kc % 2) + 1) * 6144], in_=wmod_d[kc]),
            sem, 16)
        P.mark[('wm', kc)] = (sem, P.cnt[sem])
    emit_wmod_dma(0)
    emit_wmod_dma(1)
    P.wait('t', 's_v', cfm_ev)
    for kc in range(8):
        P.wait('t', *P.mark[('wm', kc)])
        for Mc in range(48):
            P.do('t', lambda e, kc=kc, Mc=Mc: e.matmul(
                PS[:, 4 * 512 + Mc * 2: 4 * 512 + Mc * 2 + 2],
                h_buf[:, (kc % 2) * 6144 + Mc * 128: (kc % 2) * 6144 + (Mc + 1) * 128],
                cfmb[:, kc * 2:(kc + 1) * 2],
                start=(kc == 0 and Mc == 0), stop=(kc == 7 and Mc == 47),
                skip_group_check=True), 's_t')
        wm_done[kc] = P.cnt['s_t']
        if kc + 2 < 8:
            emit_wmod_dma(kc + 2)
    P.wait('v', 's_t')
    P.do('v', lambda e: e.tensor_copy(modsb[:], PS[:, 4 * 512:4 * 512 + 96]), 's_v')
    free_bank[4] = ('s_v', P.cnt['s_v'])
    free_bank[5] = ('s_v', P.cnt['s_v'])

    for sp in range(6):
        for bi in range(NB):
            sl = slice(sp * 16 + bi, sp * 16 + 16, 2)
            src, dst = modsb[:, sl], dpb[:, sl]
            if sp in (1, 4):
                P.do('v', lambda e, s=src, d=dst: e.tensor_scalar_add(d, s, 1.0), 's_v')
            elif sp in (0, 3):
                P.do('v', lambda e, s=src, d=dst: e.tensor_copy(d, s), 's_v')
            else:
                p_, n_, t_ = gtmp[:, 0:8], gtmp[:, 8:16], gtmp[:, 16:24]
                m_, d_ = gtmp2[:, 0:8], gtmp2[:, 8:16]
                P.do('v', lambda e, s=src, p=p_: e.tensor_scalar_max(p, s, 0.25), 's_v')
                P.do('v', lambda e, s=src, n=n_: e.tensor_scalar_min(n, s, -0.25), 's_v')
                P.do('v', lambda e, s=src, m=m_: e.tensor_single_scalar(m, s, 0.0, ALU.is_ge), 's_v')
                P.vdep()
                P.do('v', lambda e, p=p_, n=n_, d=d_: e.tensor_sub(d, p, n), 's_v')
                P.vdep()
                P.do('v', lambda e, m=m_, d=d_, t=t_: e.tensor_mul(t, m, d), 's_v')
                P.vdep()
                P.do('v', lambda e, n=n_, t=t_, ds=dst: e.tensor_add(ds, n, t), 's_v')
    mod_done_v = P.cnt['s_v']

    if tap("mod", [128, 96], F32):
        P.wait('y', 's_v', mod_done_v)
        P.do('y', lambda e: e.dma_start(out=dbg["mod"], in_=modsb[:]), 'd_db', 16)
    if tap("dpb", [128, 96], BF16):
        P.wait('y', 's_v', mod_done_v)
        P.do('y', lambda e: e.dma_start(out=dbg["dpb"], in_=dpb[:]), 'd_db', 16)

    def build_bcast(sp, bi, dst):
        bank_wait('t', 4)
        bank_wait('t', 5)
        P.wait('t', 's_v', mod_done_v)
        last = None
        for fc in range(8):
            col = sp * 16 + fc * 2 + bi
            last = P.do('t', lambda e, col=col, fc=fc: e.matmul(
                PS[0:1, 2048 + fc * 128:2048 + (fc + 1) * 128],
                dpb[:, col:col + 1], identb_s[:],
                start=(fc % 4 == 0), stop=(fc % 4 == 3),
                skip_group_check=True), 's_t')
        P.wait('v', 's_t', last)
        cp = P.do('v', lambda e: e.tensor_copy(rowt[0:1, :], PS[0:1, 2048:3072]), 's_v')
        P.wait('t', 's_v', cp)
        for nck in range(2):
            last = P.do('t', lambda e, nck=nck: e.matmul(
                PS[:, 2048 + nck * 512:2048 + (nck + 1) * 512],
                onesb[0:1, 0:128], rowt[0:1, nck * 512:(nck + 1) * 512],
                start=True, stop=True), 's_t')
        P.wait('v', 's_t', last)
        ev = P.do('v', lambda e, d=dst: e.tensor_copy(d[:], PS[:, 2048:3072]), 's_v')
        free_bank[4] = ('s_v', ev)
        free_bank[5] = ('s_v', ev)
        return ev

    if "bct" in taps:
        dbg["bct"] = nc.dram_tensor("dbg_bct", [128, 2048], BF16, kind="ExternalOutput").ap()
        build_bcast(1, 0, S1)
        bev = build_bcast(0, 0, S2)
        P.wait('y', 's_v', bev)
        P.do('y', lambda e: e.dma_start(out=dbg["bct"][:, 0:1024], in_=S1[:]), 'd_db', 16)
        P.do('y', lambda e: e.dma_start(out=dbg["bct"][:, 1024:2048], in_=S2[:]), 'd_db', 16)

    def ln_phase(b, which):
        sev = build_bcast(4 if which == 2 else 1, b, S1)
        if "s1early" in taps and b == 0 and which == 1:
            if "s1early" not in dbg:
                dbg["s1early"] = nc.dram_tensor("dbg_s1early", [128, 1024], BF16,
                                                kind="ExternalOutput").ap()
            P.wait('y', 's_v', sev)
            P.do('y', lambda e: e.dma_start(out=dbg["s1early"], in_=S1[:]), 'd_db', 16)
        ev2 = build_bcast(3 if which == 2 else 0, b, S2)
        P.wait('y', 's_v', ev2)
        P.wait('s', 's_v')
        apply_done = {}
        red_done = {}

        def stage_dma(t):
            slot = t % 2
            sem = 'd_xsa' if slot == 0 else 'd_xsb'
            if t - 2 in apply_done:
                P.wait('y', 's_v', apply_done[t - 2])
            P.do('y', lambda e, b=b, t=t, slot=slot: e.dma_start(
                out=xstg[:, slot * 1024:(slot + 1) * 1024],
                in_=x_d[b, t * 128:(t + 1) * 128, :]), sem, 16)
            P.mark[('lnx', b, which, t)] = P.cnt[sem]

        if which == 1:
            stage_dma(0)
            stage_dma(1)
        for t in range(NT):
            if which == 1:
                slot = t % 2
                sem = 'd_xsa' if slot == 0 else 'd_xsb'
                P.wait('s', sem, P.mark[('lnx', b, which, t)])
                P.wait('v', sem, P.mark[('lnx', b, which, t)])
                xt = xstg[:, slot * 1024:(slot + 1) * 1024]
            else:
                xt = x_buf[:, t * 1024:(t + 1) * 1024]
            if t - 1 in red_done:
                P.wait('s', 's_v', red_done[t - 1])
            sq = P.do('s', lambda e, xt=xt: e.activation(sq_tmp[:], xt, AF.Square), 's_s')
            P.do('v', lambda e, xt=xt: e.reduce_sum(stats_a[:, 0:1], xt, axis=AXX, negate=True), 's_v')
            P.wait('v', 's_s', sq)
            rd = P.do('v', lambda e: e.reduce_sum(stats_a[:, 1:2], sq_tmp[:], axis=AXX), 's_v')
            red_done[t] = rd
            P.vdep()
            P.do('v', lambda e: e.tensor_scalar_mul(stats_b[:, 0:1], stats_a[:, 0:1], 1.0 / D), 's_v')
            P.do('v', lambda e: e.tensor_scalar_mul(stats_b[:, 1:2], stats_a[:, 1:2], 1.0 / D), 's_v')
            P.vdep()
            P.do('v', lambda e: e.tensor_mul(stats_c[:, 0:1], stats_b[:, 0:1], stats_b[:, 0:1]), 's_v')
            P.vdep()
            vv = P.do('v', lambda e: e.tensor_sub(stats_a[:, 2:3], stats_b[:, 1:2], stats_c[:, 0:1]), 's_v')
            P.wait('s', 's_v', vv)
            st = P.do('s', lambda e: e.activation(stats_b[:, 2:3], stats_a[:, 2:3], AF.Sqrt, bias=1e-5), 's_s')
            P.wait('v', 's_s', st)
            P.do('v', lambda e: e.reciprocal(stats_c[:, 1:2], stats_b[:, 2:3]), 's_v')
            P.vdep()
            nm = P.do('v', lambda e: e.tensor_mul(stats_a[:, 3:4], stats_b[:, 0:1], stats_c[:, 1:2]), 's_v')
            P.vdep()
            ap = P.do('v', lambda e, xt=xt: e.tensor_scalar(
                xnb[:], xt, stats_c[:, 1:2], stats_a[:, 3:4], ALU.mult, ALU.add), 's_v')
            apply_done[t] = ap
            if which == 1 and t + 2 < NT:
                stage_dma(t + 2)
            P.wait('v', 's_v', ev2)
            P.vdep()
            P.do('v', lambda e: e.tensor_mul(xnb2[:], xnb[:], S1[:]), 's_v')
            P.vdep()
            md = P.do('v', lambda e: e.tensor_add(xnb[:], xnb2[:], S2[:]), 's_v')
            P.wait('t', 's_v', md)
            for j in range(8):
                pb = PSb if j % 2 == 0 else PSb2
                ev = free_bank.pop(('tr', j % 2), None)
                if ev is not None:
                    P.wait('t', ev[0], ev[1])
                tr = P.do('t', lambda e, pb=pb, j=j: e.transpose(
                    pb[:, 0:128], xnb[:, j * 128:(j + 1) * 128], identb_s[:]), 's_t')
                P.wait('v', 's_t', tr)
                cp = P.do('v', lambda e, pb=pb, j=j, t=t: e.tensor_copy(
                    h_buf[:, j * 2048 + t * 128: j * 2048 + (t + 1) * 128],
                    pb[:, 0:128]), 's_v')
                free_bank[('tr', j % 2)] = ('s_v', cp)
        return P.cnt['s_v']

    ffn_evac = {}
    out_dma_total = 0
    w12_seq = [0]

    for b in range(NB):
        if not _before(upto, b, "ln1"):
            break
        if b == 1:
            for eng in "tvsgy":
                P.wait(eng, 'd_xo', out_dma_total)

        hv = ln_phase(b, 1)
        if b == 0 and tap("S1", [128, 1024], BF16):
            P.wait('y', 's_v')
            P.do('y', lambda e: e.dma_start(out=dbg["S1"], in_=S1[:]), 'd_db', 16)
        if b == 0 and tap("S2", [128, 1024], BF16):
            P.wait('y', 's_v')
            P.do('y', lambda e: e.dma_start(out=dbg["S2"], in_=S2[:]), 'd_db', 16)
        if b == 0 and tap("rowt", [1, 1024], BF16):
            P.wait('y', 's_v')
            P.do('y', lambda e: e.dma_start(out=dbg["rowt"], in_=rowt[:]), 'd_db', 16)
        if b == 0 and tap("xnb", [128, 1024], BF16):
            P.wait('y', 's_v')
            P.do('y', lambda e: e.dma_start(out=dbg["xnb"], in_=xnb[:]), 'd_db', 16)
        if b == 0 and tap("sts", [128, 12], F32):
            P.wait('y', 's_v')
            P.do('y', lambda e: e.dma_start(out=dbg["sts"][:, 0:4], in_=stats_a[:]), 'd_db', 16)
            P.do('y', lambda e: e.dma_start(out=dbg["sts"][:, 4:8], in_=stats_b[:]), 'd_db', 16)
            P.do('y', lambda e: e.dma_start(out=dbg["sts"][:, 8:12], in_=stats_c[:]), 'd_db', 16)
        if b == 0 and tap("ln1", [128, 16384], BF16):
            P.wait('y', 's_v', hv)
            P.do('y', lambda e: e.dma_start(out=dbg["ln1"], in_=h_buf[:]), 'd_db', 16)
        if not _before(upto, b, "attn"):
            break

        for buf in (Qta, Qtb, Kta, Ktb):
            P.do('v', lambda e, bf=buf: e.memset(bf[:], 0.0), 's_v')
        P.do('v', lambda e: e.memset(Vab[:], 0.0), 's_v')
        P.do('v', lambda e: e.memset(Vab[:, 64:16 * 196:196], 1.0), 's_v')
        ms_done = P.do('v', lambda e: e.memset(Vab[:, 100:16 * 196:196], 1.0), 's_v')

        if b == 1:
            P.wait('g', 's_t', P.cnt['s_t'])
        P.do('g', lambda e: e.dma_start(
            out=w3sb[:, 0:8192],
            in_=wout_d.rearrange("kc kp m -> kp kc m")), 'd_w3', 16)
        wout_target = P.cnt['d_w3']

        def load_pair_tables(p):
            P.do('y', lambda e, p=p: e.dma_start(out=Kta[64:68, :], in_=khlA_d[2 * p]), 'd_kr', 16)
            P.do('y', lambda e, p=p: e.dma_start(out=Qta[64:68, :], in_=qA_d[2 * p]), 'd_kr', 16)
            P.do('y', lambda e, p=p: e.dma_start(out=Ktb[32:36, :], in_=khlB_d[2 * p + 1]), 'd_kr', 16)
            P.do('y', lambda e, p=p: e.dma_start(out=Qtb[32:36, :], in_=qB_d[2 * p + 1]), 'd_kr', 16)

        def load_pair_w(p):
            P.do('g', lambda e, p=p: e.dma_start(
                out=wqkvsb[:],
                in_=wqkv_d[p].rearrange("(kc kp) j -> kp kc j", kp=128)), 'd_wq', 16)

        P.wait('y', 's_v', ms_done)
        load_pair_tables(0)
        P.wait('g', 's_v', hv)
        load_pair_w(0)

        exp_bank = {}
        pv_slot = {0: 0, 1: 0, 2: 0, 3: 0}

        for p in range(8):
            P.wait('t', 'd_wq', 16 * (p + 1) + (16 * 8 * b))
            P.wait('t', 's_v', hv)
            for role, dstA, dstB, scale in ((0, Qta, Qtb, 0.125), (1, Kta, Ktb, 1.0)):
                for bki in range(4):
                    bank_wait('t', bki)
                if p == 0 and role == 0:
                    P.wait('t', 's_v', ms_done)
                mm = None
                for K in range(8):
                    for ncx in range(4):
                        mm = P.do('t', lambda e, K=K, ncx=ncx, role=role: e.matmul(
                            bank(ncx),
                            wqkvsb[:, K * 384 + role * 128: K * 384 + (role + 1) * 128],
                            h_buf[:, K * 2048 + ncx * 512: K * 2048 + (ncx + 1) * 512],
                            start=(K == 0), stop=(K == 7)), 's_t')
                for ncx in range(4):
                    P.wait('s', 's_t', mm)
                    P.do('s', lambda e, dstA=dstA, ncx=ncx, scale=scale: e.activation(
                        dstA[0:64, ncx * 512:(ncx + 1) * 512],
                        PS[0:64, ncx * 512:(ncx + 1) * 512], AF.Copy, scale=scale), 's_s')
                    ev = P.do('s', lambda e, dstB=dstB, ncx=ncx, scale=scale: e.activation(
                        dstB[64:128, ncx * 512:(ncx + 1) * 512],
                        PS[64:128, ncx * 512:(ncx + 1) * 512], AF.Copy, scale=scale), 's_s')
                    free_bank[ncx] = ('s_s', ev)
            for tt in range(NT):
                bki = 4 + tt % 2
                bank_wait('t', bki)
                mm = None
                for K in range(8):
                    mm = P.do('t', lambda e, K=K, tt=tt, bki=bki: e.matmul(
                        PS[:, bki * 512:bki * 512 + 128],
                        h_buf[:, K * 2048 + tt * 128: K * 2048 + (tt + 1) * 128],
                        wqkvsb[:, K * 384 + 256: K * 384 + 384],
                        start=(K == 0), stop=(K == 7)), 's_t')
                P.wait('v', 's_t', mm)
                if p == 0 and tt < 2:
                    P.wait('v', 's_v', ms_done)
                P.do('v', lambda e, tt=tt, bki=bki: e.tensor_copy(
                    Vab[:, tt * 196: tt * 196 + 64], PS[:, bki * 512:bki * 512 + 64]), 's_v')
                ev = P.do('v', lambda e, tt=tt, bki=bki: e.tensor_copy(
                    Vab[:, tt * 196 + 132: tt * 196 + 196],
                    PS[:, bki * 512 + 64:bki * 512 + 128]), 's_v')
                free_bank[bki] = ('s_v', ev)
            qkv_done_t = P.cnt['s_t']
            qkv_done_s = P.cnt['s_s']
            qkv_done_v = P.cnt['s_v']

            if p < 7:
                P.wait('g', 's_t', qkv_done_t)
                load_pair_w(p + 1)

            P.wait('t', 'd_kr', P.cnt['d_kr'])
            P.wait('t', 's_s', qkv_done_s)
            P.wait('t', 's_v', qkv_done_v)
            pend = [None]

            def emit_tail():
                if pend[0] is None:
                    return
                orow, orows, obank, pp, qq0, tsl, rc = pend[0]
                pend[0] = None
                P.wait('s', 's_t', P.mark[('pvlast', obank)])
                if ('ntail', tsl) in P.mark:
                    P.wait('s', 's_v', P.mark[('ntail', tsl)])
                oc = P.do('s', lambda e, orows=orows, obank=obank, tsl=tsl: e.activation(
                    tmp_gb[orows, tsl * 512:(tsl + 1) * 512],
                    PS[orows, obank * 512:(obank + 1) * 512], AF.Copy), 's_s')
                P.wait('t', 's_v', rc)
                P.wait('t', 's_s', oc)
                sb = P.do('t', lambda e, orow=orow, obank=obank: e.matmul(
                    bank(obank), indat_s[:, 128:256] if orow == 64 else indat_s[:, 0:128],
                    s_sums[:], start=True, stop=True), 's_t')
                P.wait('v', 's_t', sb)
                P.wait('v', 's_s', oc)
                nm = P.do('v', lambda e, orows=orows, obank=obank, pp=pp, qq0=qq0, tsl=tsl: e.tensor_mul(
                    og[orows, pp * 2048 + qq0: pp * 2048 + qq0 + 512],
                    tmp_gb[orows, tsl * 512:(tsl + 1) * 512],
                    PS[orows, obank * 512:(obank + 1) * 512]), 's_v')
                P.mark[('ntail', tsl)] = nm
                free_bank[obank] = ('s_v', nm)

            for hh in range(2):
                if hh == 0:
                    Qx, Kx, krows, orow = Qta, Kta, 128, 64
                    orows = slice(0, 64)
                    vlo, vhi = 0, 128
                else:
                    Qx, Kx, krows, orow = Qtb, Ktb, 128, 32
                    orows = slice(64, 128)
                    vlo, vhi = 68, 196
                sbanks = [0, 1, 4, 5]
                for qb in range(4):
                    q0 = qb * 512
                    kc0 = 4 * qb
                    oq = P.mark.get('oq', 0)
                    P.mark['oq'] = oq + 1
                    obank = 2 + oq % 2
                    tsl = oq % 2
                    bank_wait('t', obank)
                    pv = None
                    kcs = list(range(kc0, 16))
                    LA = 3
                    exk = {}
                    pmk = {}

                    def emit_pv(kc):
                        nonlocal pv
                        slot = (kc - kc0) % 4
                        P.wait('t', 's_s', exk[kc])
                        if pmk.get(kc) is not None:
                            P.wait('t', 's_v', pmk[kc])
                        pv = P.do('t', lambda e, kc=kc, slot=slot, obank=obank, vlo=vlo, vhi=vhi, kc0=kc0: e.matmul(
                            PS[0:(vhi - vlo), obank * 512:(obank + 1) * 512],
                            Vab[:, kc * 196 + vlo: kc * 196 + vhi],
                            Pt[:, slot * 512:(slot + 1) * 512],
                            start=(kc == kc0), stop=(kc == 15)), 's_t')
                        pv_slot[slot] = pv

                    for i, kc in enumerate(kcs):
                        sbk = sbanks[(kc - kc0) % 4]
                        slot = (kc - kc0) % 4
                        bank_wait('t', sbk)
                        P.wait('t', 's_s', exp_bank.get(sbk, 0))
                        mmS = P.do('t', lambda e, Kx=Kx, kc=kc, q0=q0, krows=krows, sbk=sbk, Qx=Qx: e.matmul(
                            bank(sbk), Kx[0:krows, kc * 128:(kc + 1) * 128],
                            Qx[0:krows, q0:q0 + 512], start=True, stop=True), 's_t')
                        if pend[0] is not None and (i == 6 or i == len(kcs) - 1):
                            emit_tail()
                        diag = kc0 <= kc < kc0 + 4
                        r = kc * 128 - q0
                        vmask = None
                        if diag:
                            P.wait('v', 's_t', mmS)
                            vmask = P.do('v', lambda e, sbk=sbk, r=r: e.tensor_add(
                                PS[:, sbk * 512 + r: sbk * 512 + r + 128],
                                PS[:, sbk * 512 + r: sbk * 512 + r + 128], tri_s[:]), 's_v')
                        w = r + 128 if diag else 512
                        P.wait('s', 's_t', mmS)
                        if vmask is not None:
                            P.wait('s', 's_v', vmask)
                        P.wait('s', 's_t', pv_slot[slot])
                        ex = P.do('s', lambda e, slot=slot, sbk=sbk, w=w: e.activation(
                            Pt[:, slot * 512: slot * 512 + w],
                            PS[:, sbk * 512: sbk * 512 + w], AF.Exp, bias=-BSHIFT), 's_s')
                        exp_bank[sbk] = ex
                        exk[kc] = ex
                        pmem = None
                        if diag and w < 512:
                            P.wait('v', 's_t', pv_slot[slot])
                            pmem = P.do('v', lambda e, slot=slot, w=w: e.memset(
                                Pt[:, slot * 512 + w: (slot + 1) * 512], 0.0), 's_v')
                        pmk[kc] = pmem
                        if i >= LA:
                            emit_pv(kcs[i - LA])
                    for kc in kcs[-LA:]:
                        emit_pv(kc)
                    P.mark[('pvlast', obank)] = pv
                    P.wait('v', 's_t', pv)
                    rc = P.do('v', lambda e, orow=orow, obank=obank: e.reciprocal(
                        s_sums[orow:orow + 1, :],
                        PS[orow:orow + 1, obank * 512:(obank + 1) * 512]), 's_v')
                    pend[0] = (orow, orows, obank, p, q0, tsl, rc)
            emit_tail()
            if p < 7:
                P.wait('y', 's_t', P.cnt['s_t'])
                load_pair_tables(p + 1)

        attn_done_v = P.cnt['s_v']
        if b == 0 and tap("og", [128, 16384], BF16):
            P.wait('y', 's_v', attn_done_v)
            P.do('y', lambda e: e.dma_start(out=dbg["og"], in_=og[:]), 'd_db', 16)
        if not _before(upto, b, "oproj"):
            break

        ga_ev = build_bcast(2, b, S3)
        P.wait('t', 'd_w3', wout_target)
        P.wait('t', 's_v', attn_done_v)
        evac_op = {}

        def oproj_dma(t):
            slot = t % 2
            sem = 'd_xsa' if slot == 0 else 'd_xsb'
            if t - 2 in evac_op:
                P.wait('y', 's_v', evac_op[t - 2])
            else:
                P.wait('y', 's_v', attn_done_v)
            P.do('y', lambda e, b=b, t=t, slot=slot: e.dma_start(
                out=xstg[:, slot * 1024:(slot + 1) * 1024],
                in_=x_d[b, t * 128:(t + 1) * 128, :]), sem, 16)
            P.mark[('opx', b, t)] = P.cnt[sem]

        oproj_dma(0)
        oproj_dma(1)
        for t in range(NT):
            slot = t % 2
            sem = 'd_xsa' if slot == 0 else 'd_xsb'
            b0k, b1k = 2 * (t % 2), 2 * (t % 2) + 1
            bank_wait('t', b0k)
            bank_wait('t', b1k)
            mm = None
            for K in range(8):
                for ncx in range(2):
                    mm = P.do('t', lambda e, K=K, t=t, b0k=b0k, ncx=ncx: e.matmul(
                        PS[:, b0k * 512 + ncx * 512: b0k * 512 + (ncx + 1) * 512],
                        og[:, K * 2048 + t * 128: K * 2048 + (t + 1) * 128],
                        w3sb[:, K * 1024 + ncx * 512: K * 1024 + (ncx + 1) * 512],
                        start=(K == 0), stop=(K == 7)), 's_t')
            P.wait('v', 's_t', mm)
            P.wait('v', sem, P.mark[('opx', b, t)])
            P.wait('v', 's_v', ga_ev)
            P.do('v', lambda e, b0k=b0k: e.tensor_mul(
                tmp_gb[:], PS[:, b0k * 512: b0k * 512 + 1024], S3[:]), 's_v')
            P.vdep()
            ev = P.do('v', lambda e, t=t, slot=slot: e.tensor_add(
                x_buf[:, t * 1024:(t + 1) * 1024],
                xstg[:, slot * 1024:(slot + 1) * 1024], tmp_gb[:]), 's_v')
            evac_op[t] = ev
            free_bank[b0k] = ('s_v', ev)
            free_bank[b1k] = ('s_v', ev)
            if t + 2 < NT:
                oproj_dma(t + 2)
        oproj_done_v = P.cnt['s_v']
        if b == 0 and tap("x2", [128, 16384], F32):
            P.wait('y', 's_v', oproj_done_v)
            P.do('y', lambda e: e.dma_start(out=dbg["x2"], in_=x_buf[:]), 'd_db', 16)
        if not _before(upto, b, "ln2"):
            break

        P.wait('v', 's_t', P.cnt['s_t'])
        hv2 = ln_phase(b, 2)
        if b == 0 and tap("h2", [128, 16384], BF16):
            P.wait('y', 's_v', hv2)
            P.do('y', lambda e: e.dma_start(out=dbg["h2"], in_=h_buf[:]), 'd_db', 16)
        if not _before(upto, b, "gate"):
            break

        wg_done = {}
        for t in range(NT):
            bank_wait('t', 4)
            P.wait('t', 's_v', hv2)
            mm = None
            for K in range(8):
                mm = P.do('t', lambda e, K=K, t=t: e.matmul(
                    PS[:, 4 * 512: 4 * 512 + 8],
                    h_buf[:, K * 2048 + t * 128: K * 2048 + (t + 1) * 128],
                    gatewsb[:, K * 8:(K + 1) * 8],
                    start=(K == 0), stop=(K == 7)), 's_t')
            P.wait('v', 's_t', mm)
            P.wait('v', 's_s')
            P.do('v', lambda e: e.tensor_copy(gA[:, 0:8], PS[:, 4 * 512:4 * 512 + 8]), 's_v')
            P.vdep()
            ev = P.do('v', lambda e: e.reduce_max(gB[:, 0:1], gA[:, 0:8], axis=AXX, negate=True), 's_v')
            free_bank[4] = ('s_v', ev)
            P.vdep()
            P.do('v', lambda e: e.tensor_scalar_add(gC[:, 0:8], gA[:, 0:8], gB[:, 0:1]), 's_v')
            P.vdep()
            P.do('v', lambda e: e.tensor_single_scalar(gB[:, 1:9], gC[:, 0:8], 0.0, ALU.is_ge), 's_v')
            P.vdep()
            P.do('v', lambda e: e.scalar_tensor_tensor(
                gC[:, 8:16], gB[:, 1:9], -1e30, gA[:, 0:8], ALU.mult, ALU.add), 's_v')
            P.vdep()
            vm = P.do('v', lambda e: e.reduce_max(gB[:, 9:10], gC[:, 8:16], axis=AXX, negate=True), 's_v')
            P.wait('s', 's_v', vm)
            ex = P.do('s', lambda e: e.activation(gC[:, 0:8], gA[:, 0:8], AF.Exp, bias=gB[:, 0:1]), 's_s')
            P.vdep()
            P.do('v', lambda e: e.tensor_scalar_add(gtmp[:, 0:8], gA[:, 0:8], gB[:, 9:10]), 's_v')
            P.vdep()
            P.do('v', lambda e: e.tensor_single_scalar(gB[:, 1:9], gtmp[:, 0:8], 0.0, ALU.is_ge), 's_v')
            P.wait('v', 's_s', ex)
            P.vdep()
            P.do('v', lambda e: e.tensor_mul(gA[:, 8:16], gC[:, 0:8], gB[:, 1:9]), 's_v')
            P.vdep()
            P.do('v', lambda e: e.reduce_sum(gB[:, 10:11], gA[:, 8:16], axis=AXX), 's_v')
            P.vdep()
            P.do('v', lambda e: e.reciprocal(gC[:, 1:2], gB[:, 10:11]), 's_v')
            P.vdep()
            wg = P.do('v', lambda e, t=t: e.tensor_scalar_mul(
                Wg[:, t * 8:(t + 1) * 8], gA[:, 8:16], gC[:, 1:2]), 's_v')
            wg_done[t] = wg
        bank_wait('t', 4)
        bank_wait('t', 5)
        for t in range(NT):
            sl = (4 + t % 2) * 512
            ev = free_bank.pop(('gt', t % 2), None)
            if ev is not None:
                P.wait('t', ev[0], ev[1])
            P.wait('t', 's_v', wg_done[t])
            tr = P.do('t', lambda e, t=t, sl=sl: e.transpose(
                PS[0:8, sl:sl + 128], Wg[:, t * 8:(t + 1) * 8], identf_s[:]), 's_t')
            P.wait('v', 's_t', tr)
            cp = P.do('v', lambda e, t=t, sl=sl: e.tensor_copy(
                Wrow[0:8, t * 128:(t + 1) * 128], PS[0:8, sl:sl + 128]), 's_v')
            free_bank[('gt', t % 2)] = ('s_v', cp)
        gate_done_v = P.cnt['s_v']
        free_bank[4] = ('s_v', gate_done_v)
        free_bank[5] = ('s_v', gate_done_v)
        if b == 0 and tap("wrow", [8, 2048], BF16):
            P.wait('y', 's_v', gate_done_v)
            P.do('y', lambda e: e.dma_start(out=dbg["wrow"], in_=Wrow[:]), 'd_db', 16)
        if not _before(upto, b, "ffn"):
            break

        gb_ev = build_bcast(5, b, S3)
        P.wait('v', 's_t', P.cnt['s_t'])
        og_free_v = P.cnt['s_v']
        w12_done = {}
        w12_fill = {}
        glast = None

        def w12_dma(e_i, s, M):
            blk = w12_seq[0]
            w12_seq[0] += 1
            slot12 = blk % 2
            sem12 = 'd_w12a' if slot12 == 0 else 'd_w12b'
            if blk - 2 in w12_done:
                P.wait('g', 's_t', w12_done[blk - 2])
            P.do('g', lambda e, e_i=e_i, M=M, slot12=slot12: e.dma_start(
                out=w12sb[:, slot12 * 2048:(slot12 + 1) * 2048],
                in_=w12_d[e_i, M]), sem12, 16)
            w12_fill[blk] = P.cnt[sem12]
            if P.mark.get('w3chunk', 8) < 8:
                P.mark['w3fn']()
            return blk

        pending = []
        for e_i in range(NE):
            for s in range(2):
                for M in range(16):
                    pending.append((e_i, s, M))
        pend_i = [0]
        def pump_w12():
            if pend_i[0] < len(pending):
                e_i, s, M = pending[pend_i[0]]
                pend_i[0] += 1
                return w12_dma(e_i, s, M)
            return None
        blk0 = pump_w12()
        blk1 = pump_w12()

        blkc = [blk0 if blk0 is not None else 0]
        blk_iter = [w12_seq[0] - 2]

        for e_i in range(NE):
            P.wait('g', 's_t', P.cnt['s_t'])
            P.mark['w3todo'] = e_i
            P.mark['w3chunk'] = 0
            def w3_chunk(e_i=None):
                k = P.mark['w3chunk']
                if k >= 8:
                    return
                ei = P.mark['w3todo']
                P.mark['w3chunk'] = k + 1
                P.do('g', lambda e, ei=ei, k=k: e.dma_start(
                    out=w3sb[:, k * 2048:(k + 1) * 2048],
                    in_=w3_d[ei, 2 * k:2 * k + 2].rearrange("kh kp m -> kp kh m")),
                    'd_w3', 16)
            P.mark['w3fn'] = w3_chunk
            w3_chunk()
            w3_target = P.cnt['d_w3'] + 16 * 7
            for s in range(2):
                if e_i < 8:
                    bank_wait('t', 4)
                    bank_wait('t', 5)
                    P.wait('t', 's_v', gate_done_v)
                    P.do('t', lambda e, e_i=e_i, s=s: e.matmul(
                        PS[:, 2048:2048 + 512], ind8_s[:, e_i * 128:(e_i + 1) * 128],
                        Wrow[:, s * 1024: s * 1024 + 512], start=True, stop=True), 's_t')
                    wbm = P.do('t', lambda e, e_i=e_i, s=s: e.matmul(
                        PS[:, 2560:2560 + 512], ind8_s[:, e_i * 128:(e_i + 1) * 128],
                        Wrow[:, s * 1024 + 512: (s + 1) * 1024], start=True, stop=True), 's_t')
                    P.wait('v', 's_t', wbm)
                    wbe = P.do('v', lambda e: e.tensor_copy(Wb[:], PS[:, 2048:3072]), 's_v')
                    free_bank[4] = ('s_v', wbe)
                    free_bank[5] = ('s_v', wbe)
                else:
                    wbe = None
                for M in range(16):
                    blk = blk_iter[0]
                    blk_iter[0] += 1
                    slot12 = blk % 2
                    sem12 = 'd_w12a' if slot12 == 0 else 'd_w12b'
                    sA = (2 * M) % 3
                    sB = (2 * M + 1) % 3
                    P.wait('t', sem12, w12_fill[blk])
                    if M == 0:
                        P.wait('t', 's_v', hv2)
                        if wbe is not None:
                            P.wait('t', 's_v', wbe)
                        P.wait('t', 's_v', og_free_v)
                    mh1 = mh2 = None
                    for half, bki in ((0, 2 * sA), (1, 2 * sA + 1)):
                        bank_wait('t', bki)
                        for K in range(8):
                            mh1 = P.do('t', lambda e, K=K, half=half, bki=bki, slot12=slot12, s=s: e.matmul(
                                bank(bki),
                                w12sb[:, slot12 * 2048 + K * 128:slot12 * 2048 + (K + 1) * 128],
                                h_buf[:, K * 2048 + s * 1024 + half * 512: K * 2048 + s * 1024 + (half + 1) * 512],
                                start=(K == 0), stop=(K == 7)), 's_t')
                    for half, bki in ((0, 2 * sB), (1, 2 * sB + 1)):
                        bank_wait('t', bki)
                        for K in range(8):
                            mh2 = P.do('t', lambda e, K=K, half=half, bki=bki, slot12=slot12, s=s: e.matmul(
                                bank(bki),
                                w12sb[:, slot12 * 2048 + 1024 + K * 128:slot12 * 2048 + 1024 + (K + 1) * 128],
                                h_buf[:, K * 2048 + s * 1024 + half * 512: K * 2048 + s * 1024 + (half + 1) * 512],
                                start=(K == 0), stop=(K == 7)), 's_t')
                    w12_done[blk] = P.cnt['s_t']
                    pump_w12()
                    P.wait('s', 's_t', mh1)
                    if glast is not None:
                        P.wait('s', 's_v', glast)
                    si = P.do('s', lambda e, sA=sA: e.activation(
                        tmp_si[:], PS[:, 2 * sA * 512: 2 * sA * 512 + 1024], AF.Silu), 's_s')
                    free_bank[2 * sA] = ('s_s', si)
                    free_bank[2 * sA + 1] = ('s_s', si)
                    P.wait('v', 's_t', mh2)
                    P.wait('v', 's_s', si)
                    siop = tmp_si
                    if e_i < 8:
                        P.vdep()
                        P.do('v', lambda e: e.tensor_mul(tmp_si2[:], tmp_si[:], Wb[:]), 's_v')
                        P.vdep()
                        siop = tmp_si2
                    gv = P.do('v', lambda e, sB=sB, M=M, siop=siop: e.tensor_mul(
                        og[:, M * 1024:(M + 1) * 1024], siop[:],
                        PS[:, 2 * sB * 512: 2 * sB * 512 + 1024]), 's_v')
                    free_bank[2 * sB] = ('s_v', gv)
                    free_bank[2 * sB + 1] = ('s_v', gv)
                    glast = gv
                P.wait('t', 'd_w3', w3_target)
                P.wait('t', 's_v', glast)
                for t in range(8):
                    tt = 8 * s + t
                    b0k, b1k = 2 * (t % 2), 2 * (t % 2) + 1
                    bank_wait('t', b0k)
                    bank_wait('t', b1k)
                    mm = None
                    for K in range(16):
                        for ncx in range(2):
                            mm = P.do('t', lambda e, K=K, t=t, b0k=b0k, ncx=ncx: e.matmul(
                                PS[:, b0k * 512 + ncx * 512: b0k * 512 + (ncx + 1) * 512],
                                og[:, K * 1024 + t * 128: K * 1024 + (t + 1) * 128],
                                w3sb[:, K * 1024 + ncx * 512: K * 1024 + (ncx + 1) * 512],
                                start=(K == 0), stop=(K == 15)), 's_t')
                    P.wait('v', 's_t', mm)
                    P.wait('v', 's_v', gb_ev)
                    P.do('v', lambda e, b0k=b0k: e.tensor_mul(
                        tmp_gb[:], PS[:, b0k * 512: b0k * 512 + 1024], S3[:]), 's_v')
                    P.vdep()
                    ev = P.do('v', lambda e, tt=tt: e.tensor_add(
                        x_buf[:, tt * 1024:(tt + 1) * 1024],
                        x_buf[:, tt * 1024:(tt + 1) * 1024], tmp_gb[:]), 's_v')
                    ffn_evac[(b, s, t)] = ev
                    free_bank[b0k] = ('s_v', ev)
                    free_bank[b1k] = ('s_v', ev)
        if not _before(upto, b, "out"):
            break

        for tt in range(NT):
            P.wait('y', 's_v', ffn_evac[(b, tt // 8, tt % 8)])
            P.do('y', lambda e, b=b, tt=tt: e.dma_start(
                out=y_d[b, tt * 128:(tt + 1) * 128, :],
                in_=x_buf[:, tt * 1024:(tt + 1) * 1024]), 'd_xo', 16)
        out_dma_total = P.cnt['d_xo']

    P.wait('y', 'd_xo', P.cnt.get('d_xo', 0))
    P.wait('y', 'd_db', P.cnt.get('d_db', 0))

    semnames = ['s_t', 's_v', 's_s', 'd_xsa', 'd_xsb', 'd_xo', 'd_kr', 'd_wq',
                'd_w12a', 'd_w12b', 'd_w3', 'd_wm', 'd_wmb', 'd_misc', 'd_db']
    with contextlib.ExitStack() as stack:
        stack.enter_context(nc.allow_low_precision(
            reason="bf16 matmul operands / normalizers; fp32 accumulation kept"))
        for s in semnames:
            P.sem[s] = stack.enter_context(nc.semaphore(s))
        block = stack.enter_context(nc.Block())
        block.gpsimd(lambda e: [f(e) for f in P.ops['g']])
        block.scalar(lambda e: [f(e) for f in P.ops['s']])
        block.tensor(lambda e: [f(e) for f in P.ops['t']])
        block.vector(lambda e: [f(e) for f in P.ops['v']])
        block.sync(lambda e: [f(e) for f in P.ops['y']])
    return nc, dbg


def kernel(**inputs):
    lay = host_layouts(inputs)
    x = np.ascontiguousarray(np.asarray(inputs["L0_attn_x"], dtype=np.float32))
    cc = np.ascontiguousarray(np.asarray(inputs["L0_c"], dtype=np.float32))
    nc, dbg = build_program(taps=tuple(_TAPS), upto=_UPTO)
    in_maps = []
    for c in range(NCORES):
        in_maps.append({
            "x": np.ascontiguousarray(x[2 * c:2 * c + 2]),
            "cvec": np.ascontiguousarray(cc[2 * c:2 * c + 2]),
            "wqkv": lay["wqkv"], "wout": lay["wout"], "gatew": lay["gate"],
            "wmod": lay["wmod"], "w12": lay["w12"], "w3": lay["w3"],
        })
    res = run_bass_kernel_spmd(nc, in_maps, core_ids=list(range(NCORES)))
    out = np.empty((16, T, D), dtype=np.float32)
    for c in range(NCORES):
        out[2 * c:2 * c + 2] = res.results[c]["y"]
    if _TAPS:
        kernel._dbg = [{k: r["dbg_" + k] for k in _TAPS} for r in res.results]
    return out


# version 05
# speedup vs baseline: 1.2816x; 1.0445x over previous
import contextlib
import math

import numpy as np
import ml_dtypes

import concourse.bass as bass
import concourse.mybir as mybir
from concourse.bass_utils import run_bass_kernel_spmd

F32 = mybir.dt.float32
BF16 = mybir.dt.bfloat16
AF = mybir.ActivationFunctionType
ALU = mybir.AluOpType
AXX = mybir.AxisListType.X
BFD = ml_dtypes.bfloat16

NCORES = 8
NB = 2
T = 2048
D = 1024
NT = 16
NH = 16
NE = 9
BSHIFT = 12.0

_UPTO = None
_TAPS = []


def _slopes():
    start = 2 ** (-2 ** (-(math.log2(NH) - 3)))
    return np.array([start * start**i for i in range(NH)], dtype=np.float32)


def host_layouts(inp):
    f32 = np.float32
    Wqkv = np.ascontiguousarray(inp["L0_attn_W_qkv"], dtype=f32)
    Wout = np.ascontiguousarray(inp["L0_attn_W_out"], dtype=f32)
    gw = np.ascontiguousarray(inp["L0_ffn_gate_w"], dtype=f32)
    ew1 = np.asarray(inp["L0_ffn_ew1"], dtype=f32)
    ew2 = np.asarray(inp["L0_ffn_ew2"], dtype=f32)
    ew3 = np.asarray(inp["L0_ffn_ew3"], dtype=f32)
    sw1 = np.asarray(inp["L0_ffn_sw1"], dtype=f32)
    sw2 = np.asarray(inp["L0_ffn_sw2"], dtype=f32)
    sw3 = np.asarray(inp["L0_ffn_sw3"], dtype=f32)
    wmod = np.ascontiguousarray(inp["L0_wmod"], dtype=f32)

    wqkv_l = np.ascontiguousarray(
        Wqkv.reshape(1024, 3, 8, 128).transpose(2, 0, 1, 3).reshape(8, 1024, 384))
    wout_l = np.ascontiguousarray(Wout.reshape(8, 128, 1024))
    gate_l = np.ascontiguousarray(gw.reshape(8, 128, 8))
    wmod_l = np.ascontiguousarray(wmod.reshape(8, 128, 6144))
    w1 = np.concatenate([ew1, sw1[None]], axis=0)
    w2 = np.concatenate([ew2, sw2[None]], axis=0)
    w3 = np.concatenate([ew3, sw3[None]], axis=0)
    t1 = w1.reshape(9, 8, 128, 16, 128).transpose(0, 3, 2, 1, 4).reshape(9, 16, 128, 1024)
    t2 = w2.reshape(9, 8, 128, 16, 128).transpose(0, 3, 2, 1, 4).reshape(9, 16, 128, 1024)
    w12_l = np.ascontiguousarray(np.concatenate([t1, t2], axis=3))
    w3_l = np.ascontiguousarray(w3.reshape(9, 16, 128, 1024))
    return dict(wqkv=wqkv_l, wout=wout_l, gate=gate_l, wmod=wmod_l,
                w12=w12_l, w3=w3_l)


def host_consts():
    sl = _slopes()
    k = np.arange(T, dtype=np.float32)
    bias = -sl[:, None] * k[None, :]
    bhi = bias.astype(BFD).astype(np.float32)
    blo = (bias - bhi).astype(BFD)
    qrow = (sl[:, None] * k[None, :]).astype(np.float32)
    ones = np.ones((16, T), np.float32)
    zero = np.zeros((16, T), np.float32)
    khlA = np.stack([bhi, blo.astype(np.float32), ones, zero], 1).astype(BFD)
    qA = np.stack([ones, ones, qrow, zero], 1).astype(BFD)
    khlB = np.stack([ones, bhi, blo.astype(np.float32), zero], 1).astype(BFD)
    qB = np.stack([qrow, ones, ones, zero], 1).astype(BFD)
    identb = np.eye(128, dtype=BFD)
    identf = np.eye(128, dtype=np.float32)
    kk = np.arange(128)
    tri = np.where(kk[:, None] >= kk[None, :], 0.0, -1e30).astype(BFD)
    ind32 = ((kk[:, None] == 32) * np.ones((1, 128))).astype(BFD)
    ind64 = ((kk[:, None] == 64) * np.ones((1, 128))).astype(BFD)
    indat = np.ascontiguousarray(np.concatenate([ind32, ind64], axis=1))
    e8 = np.arange(8)
    ind8 = np.repeat((e8[:, None, None] == e8[None, :, None]), 128, axis=2)
    ind8 = np.ascontiguousarray(ind8.reshape(8, 1024).astype(BFD))
    cvals = np.ascontiguousarray(
        np.tile(np.array([[1e-5, -BSHIFT]], np.float32), (128, 1)))
    return dict(khlA=np.ascontiguousarray(khlA), qA=np.ascontiguousarray(qA),
                khlB=np.ascontiguousarray(khlB), qB=np.ascontiguousarray(qB),
                identb=identb, identf=identf, tri=tri, indat=indat, ind8=ind8,
                cvals=cvals)


class Prog:
    def __init__(self):
        self.ops = {k: [] for k in "tvsgy"}
        self.cnt = {}
        self.sem = {}
        self.mark = {}

    def raw(self, eng, fn):
        self.ops[eng].append(fn)

    def wait(self, eng, sem, val=None):
        v = self.cnt.get(sem, 0) if val is None else val
        assert v <= self.cnt.get(sem, 0), f"wait on future event {sem} {v} > {self.cnt.get(sem, 0)}"
        if v > 0:
            self.raw(eng, lambda e, s=sem, vv=v: e.wait_ge(self.sem[s], vv))

    def vdep(self):
        self.wait('v', 's_v')

    def do(self, eng, fn, inc=None, by=1):
        if inc is None:
            self.raw(eng, fn)
            return None
        self.cnt[inc] = self.cnt.get(inc, 0) + by
        def run(e, fn=fn, s=inc, b=by):
            fn(e).then_inc(self.sem[s], b)
        self.raw(eng, run)
        return self.cnt[inc]


STAGES = ["ln1", "attn", "oproj", "ln2", "gate", "ffn", "out"]


def _before(upto, b, stage):
    if upto is None:
        return True
    ub, us = upto
    if b != ub:
        return b < ub
    return STAGES.index(stage) <= STAGES.index(us)


def build_program(taps=(), upto=None):
    nc = bass.Bass("TRN2", target_bir_lowering=False)
    C = host_consts()

    x_d = nc.dram_tensor("x", [NB, T, D], F32, kind="ExternalInput").ap()
    c_d = nc.dram_tensor("cvec", [NB, D], F32, kind="ExternalInput").ap()
    wqkv_d = nc.dram_tensor("wqkv", [8, 1024, 384], F32, kind="ExternalInput").ap()
    wout_d = nc.dram_tensor("wout", [8, 128, 1024], F32, kind="ExternalInput").ap()
    gate_d = nc.dram_tensor("gatew", [8, 128, 8], F32, kind="ExternalInput").ap()
    wmod_d = nc.dram_tensor("wmod", [8, 128, 6144], F32, kind="ExternalInput").ap()
    w12_d = nc.dram_tensor("w12", [9, 16, 128, 2048], F32, kind="ExternalInput").ap()
    w3_d = nc.dram_tensor("w3", [9, 16, 128, 1024], F32, kind="ExternalInput").ap()
    y_d = nc.dram_tensor("y", [NB, T, D], F32, kind="ExternalOutput").ap()

    khlA_d = nc.inline_tensor(C["khlA"], name="khlA").ap()
    qA_d = nc.inline_tensor(C["qA"], name="qA").ap()
    khlB_d = nc.inline_tensor(C["khlB"], name="khlB").ap()
    qB_d = nc.inline_tensor(C["qB"], name="qB").ap()
    identb_d = nc.inline_tensor(C["identb"], name="identb").ap()
    identf_d = nc.inline_tensor(C["identf"], name="identf").ap()
    tri_d = nc.inline_tensor(C["tri"], name="tri").ap()
    indat_d = nc.inline_tensor(C["indat"], name="indat").ap()
    ind8_d = nc.inline_tensor(C["ind8"], name="ind8").ap()
    cvals_d = nc.inline_tensor(C["cvals"], name="cvals").ap()

    dbg = {}
    def tap(name, shape, dtype):
        if name in taps:
            dbg[name] = nc.dram_tensor("dbg_" + name, shape, dtype,
                                       kind="ExternalOutput").ap()
            return True
        return False

    ARENA = nc.sbuf_bytes_remaining - 2048
    base = nc.bump_sbuf(ARENA)[0]
    cur = [base]
    def alloc(name, shape, dtype):
        a = (cur[0] + 31) & ~31
        t = nc.alloc_sbuf_tensor_at(name, shape, dtype, offset=a)
        cur[0] = a + shape[1] * {F32: 4, BF16: 2}[dtype]
        return t

    x_buf = alloc("x_buf", [128, NT * 1024], F32)
    XOFF = nc.lookup_mloc(x_buf).addr
    h_buf = alloc("h_buf", [128, 8 * 2048], BF16)
    og = alloc("og", [128, 16384], BF16)
    w3sb = alloc("w3sb", [128, 16 * 1024], BF16)
    w12sb = alloc("w12sb", [128, 3 * 2048], BF16)
    Wb = alloc("Wb", [128, 1024], BF16)
    tmp_si = alloc("tmp_si", [128, 1024], BF16)
    tmp_gb = alloc("tmp_gb", [128, 1024], BF16)
    S1 = alloc("S1", [128, 1024], BF16)
    S2 = alloc("S2", [128, 1024], BF16)
    S3 = alloc("S3", [128, 1024], BF16)
    Wrow = alloc("Wrow", [8, 2048], BF16)
    Wg = alloc("Wg", [128, 128], F32)
    gatewsb = alloc("gatewsb", [128, 64], BF16)
    modsb = alloc("modsb", [128, 96], F32)
    dpb = alloc("dpb", [128, 96], BF16)
    gtmp = alloc("gtmp", [128, 32], F32)

    cfmb = alloc("cfmb", [128, 16], BF16)
    rowt = alloc("rowt", [1, 1024], BF16)
    identb_s = alloc("identb_s", [128, 128], BF16)
    identf_s = alloc("identf_s", [128, 128], F32)
    tri_s = alloc("tri_s", [128, 128], BF16)
    indat_s = alloc("indat_s", [128, 256], BF16)
    ind8_s = alloc("ind8_s", [8, 1024], BF16)
    onesb = alloc("onesb", [1, 128], BF16)
    s_sums = alloc("s_sums", [128, 512], BF16)
    csil = None
    xnb = alloc("xnb", [128, 1024], BF16)
    stats_a = alloc("stats_a", [128, 4], F32)
    stats_b = alloc("stats_b", [128, 4], F32)
    stats_c = alloc("stats_c", [128, 4], F32)
    gA = alloc("gA", [128, 16], F32)
    gB = alloc("gB", [128, 16], F32)
    gC = alloc("gC", [128, 16], F32)
    gtmp2 = alloc("gtmp2", [128, 32], F32)
    xnb2 = alloc("xnb2", [128, 1024], BF16)
    tmp_si2 = alloc("tmp_si2", [128, 1024], BF16)
    cvals_s = alloc("cvals_s", [128, 2], F32)
    nc.const_aps.register(F32, 1e-5, cvals_s[:, 0:1])
    nc.const_aps.register(F32, -BSHIFT, cvals_s[:, 1:2])
    assert cur[0] <= base + ARENA, f"sbuf over: {cur[0] - base} > {ARENA}"

    Qta = nc.alloc_sbuf_tensor_at("Qta", [128, 2048], BF16, offset=XOFF + 0)
    Qtb = nc.alloc_sbuf_tensor_at("Qtb", [128, 2048], BF16, offset=XOFF + 4096)
    Kta = nc.alloc_sbuf_tensor_at("Kta", [128, 2048], BF16, offset=XOFF + 8192)
    Ktb = nc.alloc_sbuf_tensor_at("Ktb", [128, 2048], BF16, offset=XOFF + 12288)
    Vab = nc.alloc_sbuf_tensor_at("Vab", [128, 16 * 196], BF16, offset=XOFF + 16384)

    wqkvsb = nc.alloc_sbuf_tensor_at("wqkvsb", [128, 3072], BF16, offset=XOFF + 24768)
    xstg = nc.alloc_sbuf_tensor_at("xstg", [128, 2048], F32, offset=XOFF + 57344)
    W3OFF = nc.lookup_mloc(w3sb).addr
    Pt = nc.alloc_sbuf_tensor_at("Pt", [128, 2048], BF16, offset=W3OFF + 16384)
    wq2 = nc.alloc_sbuf_tensor_at("wq2", [128, 3072], BF16, offset=XOFF + 31744)
    lnst = nc.alloc_sbuf_tensor_at("lnst", [128, 16], F32, offset=W3OFF + 24576)
    lnag = nc.alloc_sbuf_tensor_at("lnag", [128, 4], F32, offset=W3OFF + 24640)
    mn_a = nc.alloc_sbuf_tensor_at("mn_a", [128, 16], F32, offset=W3OFF + 24704)
    va_a = nc.alloc_sbuf_tensor_at("va_a", [128, 16], F32, offset=W3OFF + 24768)
    ri_a = nc.alloc_sbuf_tensor_at("ri_a", [128, 16], F32, offset=W3OFF + 24832)
    nm_a = nc.alloc_sbuf_tensor_at("nm_a", [128, 16], F32, offset=W3OFF + 24896)
    sd_a = nc.alloc_sbuf_tensor_at("sd_a", [128, 16], F32, offset=W3OFF + 24960)
    csil = nc.alloc_sbuf_tensor_at("csil", [2, 1024], F32, offset=W3OFF + 25088)
    csb = nc.alloc_sbuf_tensor_at("csb", [2, 1024], F32, offset=W3OFF + 20480)

    PS = nc.alloc_psum_tensor("PS", [128, 3072], F32)
    PSb = nc.alloc_psum_tensor("PSb", [128, 1024], BF16)
    PSb2 = nc.alloc_psum_tensor("PSb2", [128, 1024], BF16)
    def bank(i, w=512):
        return PS[:, i * 512:i * 512 + w]

    P = Prog()
    free_bank = {}

    def bank_wait(eng, *banks):
        for bk in banks:
            ev = free_bank.pop(bk, None)
            if ev is not None:
                P.wait(eng, ev[0], ev[1])

    for dst, src in [(identb_s, identb_d), (identf_s, identf_d), (tri_s, tri_d),
                     (indat_s, indat_d), (ind8_s, ind8_d)]:
        P.do('g', lambda e, d=dst, s=src: e.dma_start(out=d[:], in_=s), 'd_misc', 16)
    P.do('g', lambda e: e.dma_start(out=gatewsb[:],
         in_=gate_d.rearrange("kc kp e -> kp kc e")), 'd_misc', 16)
    P.do('g', lambda e: e.dma_start(out=csb[:], in_=c_d), 'd_misc', 16)
    P.do('g', lambda e: e.dma_start(out=cvals_s[:], in_=cvals_d), 'd_misc', 16)
    misc_total = P.cnt['d_misc']
    P.do('v', lambda e: e.memset(onesb[:], 1.0), 's_v')
    P.do('v', lambda e: e.memset(s_sums[:], 1.0), 's_v')

    P.wait('s', 'd_misc', misc_total)
    P.do('s', lambda e: e.activation(csil[:], csb[:], AF.Silu), 's_s')
    P.wait('t', 'd_misc', misc_total)
    P.wait('t', 's_s')
    cfm_ev = None
    for kc in range(8):
        if cfm_ev is not None:
            P.wait('t', 's_v', cfm_ev)
        tr = P.do('t', lambda e, kc=kc: e.transpose(
            PS[:, 5 * 512: 5 * 512 + 2],
            csil[0:2, kc * 128:(kc + 1) * 128], identf_s[0:2, 0:2]), 's_t')
        P.wait('v', 's_t', tr)
        cfm_ev = P.do('v', lambda e, kc=kc: e.tensor_copy(
            cfmb[:, kc * 2:(kc + 1) * 2], PS[:, 5 * 512:5 * 512 + 2]), 's_v')

    wm_done = {}
    def emit_wmod_dma(kc):
        if kc - 2 in wm_done:
            P.wait('g', 's_t', wm_done[kc - 2])
        sem = 'd_wm' if kc % 2 == 0 else 'd_wmb'
        P.do('g', lambda e, kc=kc: e.dma_start(
            out=h_buf[:, (kc % 2) * 6144:((kc % 2) + 1) * 6144], in_=wmod_d[kc]),
            sem, 16)
        P.mark[('wm', kc)] = (sem, P.cnt[sem])
    emit_wmod_dma(0)
    emit_wmod_dma(1)
    P.wait('t', 's_v', cfm_ev)
    for kc in range(8):
        P.wait('t', *P.mark[('wm', kc)])
        for Mc in range(48):
            P.do('t', lambda e, kc=kc, Mc=Mc: e.matmul(
                PS[:, 4 * 512 + Mc * 2: 4 * 512 + Mc * 2 + 2],
                h_buf[:, (kc % 2) * 6144 + Mc * 128: (kc % 2) * 6144 + (Mc + 1) * 128],
                cfmb[:, kc * 2:(kc + 1) * 2],
                start=(kc == 0 and Mc == 0), stop=(kc == 7 and Mc == 47),
                skip_group_check=True), 's_t')
        wm_done[kc] = P.cnt['s_t']
        if kc + 2 < 8:
            emit_wmod_dma(kc + 2)
    P.wait('v', 's_t')
    P.do('v', lambda e: e.tensor_copy(modsb[:], PS[:, 4 * 512:4 * 512 + 96]), 's_v')
    free_bank[4] = ('s_v', P.cnt['s_v'])
    free_bank[5] = ('s_v', P.cnt['s_v'])

    for sp in range(6):
        for bi in range(NB):
            sl = slice(sp * 16 + bi, sp * 16 + 16, 2)
            src, dst = modsb[:, sl], dpb[:, sl]
            if sp in (1, 4):
                P.do('v', lambda e, s=src, d=dst: e.tensor_scalar_add(d, s, 1.0), 's_v')
            elif sp in (0, 3):
                P.do('v', lambda e, s=src, d=dst: e.tensor_copy(d, s), 's_v')
            else:
                p_, n_, t_ = gtmp[:, 0:8], gtmp[:, 8:16], gtmp[:, 16:24]
                m_, d_ = gtmp2[:, 0:8], gtmp2[:, 8:16]
                P.do('v', lambda e, s=src, p=p_: e.tensor_scalar_max(p, s, 0.25), 's_v')
                P.do('v', lambda e, s=src, n=n_: e.tensor_scalar_min(n, s, -0.25), 's_v')
                P.do('v', lambda e, s=src, m=m_: e.tensor_single_scalar(m, s, 0.0, ALU.is_ge), 's_v')
                P.vdep()
                P.do('v', lambda e, p=p_, n=n_, d=d_: e.tensor_sub(d, p, n), 's_v')
                P.vdep()
                P.do('v', lambda e, m=m_, d=d_, t=t_: e.tensor_mul(t, m, d), 's_v')
                P.vdep()
                P.do('v', lambda e, n=n_, t=t_, ds=dst: e.tensor_add(ds, n, t), 's_v')
    mod_done_v = P.cnt['s_v']

    if tap("mod", [128, 96], F32):
        P.wait('y', 's_v', mod_done_v)
        P.do('y', lambda e: e.dma_start(out=dbg["mod"], in_=modsb[:]), 'd_db', 16)
    if tap("dpb", [128, 96], BF16):
        P.wait('y', 's_v', mod_done_v)
        P.do('y', lambda e: e.dma_start(out=dbg["dpb"], in_=dpb[:]), 'd_db', 16)

    def build_bcast(sp, bi, dst):
        bank_wait('t', 4)
        bank_wait('t', 5)
        P.wait('t', 's_v', mod_done_v)
        last = None
        for fc in range(8):
            col = sp * 16 + fc * 2 + bi
            last = P.do('t', lambda e, col=col, fc=fc: e.matmul(
                PS[0:1, 2048 + fc * 128:2048 + (fc + 1) * 128],
                dpb[:, col:col + 1], identb_s[:],
                start=(fc % 4 == 0), stop=(fc % 4 == 3),
                skip_group_check=True), 's_t')
        P.wait('v', 's_t', last)
        cp = P.do('v', lambda e: e.tensor_copy(rowt[0:1, :], PS[0:1, 2048:3072]), 's_v')
        P.wait('t', 's_v', cp)
        for nck in range(2):
            last = P.do('t', lambda e, nck=nck: e.matmul(
                PS[:, 2048 + nck * 512:2048 + (nck + 1) * 512],
                onesb[0:1, 0:128], rowt[0:1, nck * 512:(nck + 1) * 512],
                start=True, stop=True), 's_t')
        P.wait('v', 's_t', last)
        ev = P.do('v', lambda e, d=dst: e.tensor_copy(d[:], PS[:, 2048:3072]), 's_v')
        free_bank[4] = ('s_v', ev)
        free_bank[5] = ('s_v', ev)
        return ev

    if "bct" in taps:
        dbg["bct"] = nc.dram_tensor("dbg_bct", [128, 2048], BF16, kind="ExternalOutput").ap()
        build_bcast(1, 0, S1)
        bev = build_bcast(0, 0, S2)
        P.wait('y', 's_v', bev)
        P.do('y', lambda e: e.dma_start(out=dbg["bct"][:, 0:1024], in_=S1[:]), 'd_db', 16)
        P.do('y', lambda e: e.dma_start(out=dbg["bct"][:, 1024:2048], in_=S2[:]), 'd_db', 16)

    def ln_phase(b, which):
        if which == 1:
            for t in range(NT):
                P.do('y', lambda e, b=b, t=t: e.dma_start(
                    out=x_buf[:, t * 1024:(t + 1) * 1024],
                    in_=x_d[b, t * 128:(t + 1) * 128, :]), 'd_xsa', 16)
            xin = P.cnt['d_xsa']
        build_bcast(4 if which == 2 else 1, b, S1)
        ev2 = build_bcast(3 if which == 2 else 0, b, S2)
        P.wait('y', 's_v', ev2)
        if which == 1:
            P.wait('v', 'd_xsa', xin)
            P.wait('s', 'd_xsa', xin)
        P.wait('s', 's_v')
        for t in range(NT):
            xt = x_buf[:, t * 1024:(t + 1) * 1024]
            P.do('v', lambda e, xt=xt: e.bn_stats(lnst[:, 0:6], xt[:, 0:512]), 's_v')
            P.do('v', lambda e, xt=xt: e.bn_stats(lnst[:, 6:12], xt[:, 512:1024]), 's_v')
            P.vdep()
            P.do('v', lambda e: e.bn_aggr(lnag[:, 0:2], lnst[:, 0:12]), 's_v')
            P.vdep()
            P.do('v', lambda e, t=t: e.tensor_scalar_mul(mn_a[:, t:t + 1], lnag[:, 0:1], -1.0), 's_v')
            P.do('v', lambda e, t=t: e.tensor_copy(va_a[:, t:t + 1], lnag[:, 1:2]), 's_v')
        P.vdep()
        P.wait('s', 's_v')
        st = P.do('s', lambda e: e.activation(sd_a[:], va_a[:], AF.Sqrt, bias=1e-5), 's_s')
        P.wait('v', 's_s', st)
        P.do('v', lambda e: e.reciprocal(ri_a[:], sd_a[:]), 's_v')
        P.vdep()
        nmv = P.do('v', lambda e: e.tensor_mul(nm_a[:], mn_a[:], ri_a[:]), 's_v')
        for t in range(NT):
            xt = x_buf[:, t * 1024:(t + 1) * 1024]
            P.do('v', lambda e, xt=xt, t=t: e.tensor_scalar(
                xnb[:], xt, ri_a[:, t:t + 1], nm_a[:, t:t + 1], ALU.mult, ALU.add), 's_v')
            P.vdep()
            P.do('v', lambda e: e.tensor_mul(xnb2[:], xnb[:], S1[:]), 's_v')
            P.vdep()
            md = P.do('v', lambda e: e.tensor_add(xnb[:], xnb2[:], S2[:]), 's_v')
            P.wait('t', 's_v', md)
            for j in range(8):
                pb = PSb if j % 2 == 0 else PSb2
                ev = free_bank.pop(('tr', j % 2), None)
                if ev is not None:
                    P.wait('t', ev[0], ev[1])
                tr = P.do('t', lambda e, pb=pb, j=j: e.transpose(
                    pb[:, 0:128], xnb[:, j * 128:(j + 1) * 128], identb_s[:]), 's_t')
                P.wait('v', 's_t', tr)
                cp = P.do('v', lambda e, pb=pb, j=j, t=t: e.tensor_copy(
                    h_buf[:, j * 2048 + t * 128: j * 2048 + (t + 1) * 128],
                    pb[:, 0:128]), 's_v')
                free_bank[('tr', j % 2)] = ('s_v', cp)
        return P.cnt['s_v']

    ffn_evac = {}
    out_dma_total = 0
    w12_seq = [0]

    for b in range(NB):
        if not _before(upto, b, "ln1"):
            break
        if b == 1:
            for eng in "tvsgy":
                P.wait(eng, 'd_xo', out_dma_total)

        hv = ln_phase(b, 1)
        if b == 0 and tap("S1", [128, 1024], BF16):
            P.wait('y', 's_v')
            P.do('y', lambda e: e.dma_start(out=dbg["S1"], in_=S1[:]), 'd_db', 16)
        if b == 0 and tap("S2", [128, 1024], BF16):
            P.wait('y', 's_v')
            P.do('y', lambda e: e.dma_start(out=dbg["S2"], in_=S2[:]), 'd_db', 16)
        if b == 0 and tap("rowt", [1, 1024], BF16):
            P.wait('y', 's_v')
            P.do('y', lambda e: e.dma_start(out=dbg["rowt"], in_=rowt[:]), 'd_db', 16)
        if b == 0 and tap("xnb", [128, 1024], BF16):
            P.wait('y', 's_v')
            P.do('y', lambda e: e.dma_start(out=dbg["xnb"], in_=xnb[:]), 'd_db', 16)
        if b == 0 and tap("sts", [128, 64], F32):
            P.wait('y', 's_v')
            P.do('y', lambda e: e.dma_start(out=dbg["sts"][:, 0:16], in_=mn_a[:]), 'd_db', 16)
            P.do('y', lambda e: e.dma_start(out=dbg["sts"][:, 16:32], in_=va_a[:]), 'd_db', 16)
            P.do('y', lambda e: e.dma_start(out=dbg["sts"][:, 32:48], in_=ri_a[:]), 'd_db', 16)
            P.do('y', lambda e: e.dma_start(out=dbg["sts"][:, 48:64], in_=nm_a[:]), 'd_db', 16)
        if b == 0 and tap("ln1", [128, 16384], BF16):
            P.wait('y', 's_v', hv)
            P.do('y', lambda e: e.dma_start(out=dbg["ln1"], in_=h_buf[:]), 'd_db', 16)
        if not _before(upto, b, "attn"):
            break

        for buf in (Qta, Qtb, Kta, Ktb):
            P.do('v', lambda e, bf=buf: e.memset(bf[:], 0.0), 's_v')
        P.do('v', lambda e: e.memset(Vab[:], 0.0), 's_v')
        P.do('v', lambda e: e.memset(Vab[:, 64:16 * 196:196], 1.0), 's_v')
        ms_done = P.do('v', lambda e: e.memset(Vab[:, 100:16 * 196:196], 1.0), 's_v')

        if b == 1:
            P.wait('g', 's_t', P.cnt['s_t'])
        P.do('g', lambda e: e.dma_start(
            out=w3sb[:, 0:8192],
            in_=wout_d.rearrange("kc kp m -> kp kc m")), 'd_w3', 16)
        wout_target = P.cnt['d_w3']

        def load_pair_tables(p):
            P.do('y', lambda e, p=p: e.dma_start(out=Kta[64:68, :], in_=khlA_d[2 * p]), 'd_kr', 16)
            P.do('y', lambda e, p=p: e.dma_start(out=Qta[64:68, :], in_=qA_d[2 * p]), 'd_kr', 16)
            P.do('y', lambda e, p=p: e.dma_start(out=Ktb[32:36, :], in_=khlB_d[2 * p + 1]), 'd_kr', 16)
            P.do('y', lambda e, p=p: e.dma_start(out=Qtb[32:36, :], in_=qB_d[2 * p + 1]), 'd_kr', 16)

        def load_pair_w(p):
            dst = wqkvsb if p % 2 == 0 else wq2
            P.do('g', lambda e, p=p, dst=dst: e.dma_start(
                out=dst[:],
                in_=wqkv_d[p].rearrange("(kc kp) j -> kp kc j", kp=128)), 'd_wq', 16)

        P.wait('y', 's_v', ms_done)
        load_pair_tables(0)
        P.wait('g', 's_v', hv)
        load_pair_w(0)

        exp_bank = {}
        pv_slot = {0: 0, 1: 0, 2: 0, 3: 0}

        for p in range(8):
            wsrc = wqkvsb if p % 2 == 0 else wq2
            P.wait('t', 'd_wq', 16 * (p + 1) + (16 * 8 * b))
            P.wait('t', 's_v', hv)
            P.mark[('projstart', b, p)] = P.cnt['s_t'] + 1
            for role, dstA, dstB, scale in ((0, Qta, Qtb, 0.125), (1, Kta, Ktb, 1.0)):
                for bki in range(4):
                    bank_wait('t', bki)
                if p == 0 and role == 0:
                    P.wait('t', 's_v', ms_done)
                mm = None
                for K in range(8):
                    for ncx in range(4):
                        mm = P.do('t', lambda e, K=K, ncx=ncx, role=role, wsrc=wsrc: e.matmul(
                            bank(ncx),
                            wsrc[:, K * 384 + role * 128: K * 384 + (role + 1) * 128],
                            h_buf[:, K * 2048 + ncx * 512: K * 2048 + (ncx + 1) * 512],
                            start=(K == 0), stop=(K == 7)), 's_t')
                for ncx in range(4):
                    P.wait('s', 's_t', mm)
                    P.do('s', lambda e, dstA=dstA, ncx=ncx, scale=scale: e.activation(
                        dstA[0:64, ncx * 512:(ncx + 1) * 512],
                        PS[0:64, ncx * 512:(ncx + 1) * 512], AF.Copy, scale=scale), 's_s')
                    ev = P.do('s', lambda e, dstB=dstB, ncx=ncx, scale=scale: e.activation(
                        dstB[64:128, ncx * 512:(ncx + 1) * 512],
                        PS[64:128, ncx * 512:(ncx + 1) * 512], AF.Copy, scale=scale), 's_s')
                    free_bank[ncx] = ('s_s', ev)
            for tt in range(NT):
                bki = 4 + tt % 2
                bank_wait('t', bki)
                mm = None
                for K in range(8):
                    mm = P.do('t', lambda e, K=K, tt=tt, bki=bki, wsrc=wsrc: e.matmul(
                        PS[:, bki * 512:bki * 512 + 128],
                        h_buf[:, K * 2048 + tt * 128: K * 2048 + (tt + 1) * 128],
                        wsrc[:, K * 384 + 256: K * 384 + 384],
                        start=(K == 0), stop=(K == 7)), 's_t')
                P.wait('v', 's_t', mm)
                if p == 0 and tt < 2:
                    P.wait('v', 's_v', ms_done)
                P.do('v', lambda e, tt=tt, bki=bki: e.tensor_copy(
                    Vab[:, tt * 196: tt * 196 + 64], PS[:, bki * 512:bki * 512 + 64]), 's_v')
                ev = P.do('v', lambda e, tt=tt, bki=bki: e.tensor_copy(
                    Vab[:, tt * 196 + 132: tt * 196 + 196],
                    PS[:, bki * 512 + 64:bki * 512 + 128]), 's_v')
                free_bank[bki] = ('s_v', ev)
            qkv_done_t = P.cnt['s_t']
            qkv_done_s = P.cnt['s_s']
            qkv_done_v = P.cnt['s_v']

            if p < 7:
                P.wait('g', 's_t', P.mark[('projstart', b, p)])
                load_pair_w(p + 1)

            P.wait('t', 'd_kr', P.cnt['d_kr'])
            P.wait('t', 's_s', qkv_done_s)
            P.wait('t', 's_v', qkv_done_v)
            pend = [None]

            def emit_tail():
                if pend[0] is None:
                    return
                orow, orows, obank, pp, qq0, tsl, rc = pend[0]
                pend[0] = None
                if ('ntail', tsl) in P.mark:
                    P.wait('v', 's_v', P.mark[('ntail', tsl)])
                oc = P.do('v', lambda e, orows=orows, obank=obank, tsl=tsl: e.tensor_copy(
                    tmp_gb[orows, tsl * 512:(tsl + 1) * 512],
                    PS[orows, obank * 512:(obank + 1) * 512]), 's_v')
                P.wait('t', 's_v', oc)
                sb = P.do('t', lambda e, orow=orow, obank=obank: e.matmul(
                    bank(obank), indat_s[:, 128:256] if orow == 64 else indat_s[:, 0:128],
                    s_sums[:], start=True, stop=True), 's_t')
                P.wait('v', 's_t', sb)
                nm = P.do('v', lambda e, orows=orows, obank=obank, pp=pp, qq0=qq0, tsl=tsl: e.tensor_mul(
                    og[orows, pp * 2048 + qq0: pp * 2048 + qq0 + 512],
                    tmp_gb[orows, tsl * 512:(tsl + 1) * 512],
                    PS[orows, obank * 512:(obank + 1) * 512]), 's_v')
                P.mark[('ntail', tsl)] = nm
                free_bank[obank] = ('s_v', nm)

            for hh in range(2):
                if hh == 0:
                    Qx, Kx, krows, orow = Qta, Kta, 128, 64
                    orows = slice(0, 64)
                    vlo, vhi = 0, 128
                else:
                    Qx, Kx, krows, orow = Qtb, Ktb, 128, 32
                    orows = slice(64, 128)
                    vlo, vhi = 68, 196
                sbanks = [0, 1, 4, 5]
                for qb in range(4):
                    q0 = qb * 512
                    kc0 = 4 * qb
                    oq = P.mark.get('oq', 0)
                    P.mark['oq'] = oq + 1
                    obank = 2 + oq % 2
                    tsl = oq % 2
                    bank_wait('t', obank)
                    pv = None
                    kcs = list(range(kc0, 16))
                    LA = 3
                    exk = {}
                    pmk = {}

                    def emit_pv(kc):
                        nonlocal pv
                        slot = (kc - kc0) % 4
                        P.wait('t', 's_s', exk[kc])
                        if pmk.get(kc) is not None:
                            P.wait('t', 's_v', pmk[kc])
                        pv = P.do('t', lambda e, kc=kc, slot=slot, obank=obank, vlo=vlo, vhi=vhi, kc0=kc0: e.matmul(
                            PS[0:(vhi - vlo), obank * 512:(obank + 1) * 512],
                            Vab[:, kc * 196 + vlo: kc * 196 + vhi],
                            Pt[:, slot * 512:(slot + 1) * 512],
                            start=(kc == kc0), stop=(kc == 15)), 's_t')
                        pv_slot[slot] = pv

                    for i, kc in enumerate(kcs):
                        sbk = sbanks[(kc - kc0) % 4]
                        slot = (kc - kc0) % 4
                        bank_wait('t', sbk)
                        P.wait('t', 's_s', exp_bank.get(sbk, 0))
                        mmS = P.do('t', lambda e, Kx=Kx, kc=kc, q0=q0, krows=krows, sbk=sbk, Qx=Qx: e.matmul(
                            bank(sbk), Kx[0:krows, kc * 128:(kc + 1) * 128],
                            Qx[0:krows, q0:q0 + 512], start=True, stop=True), 's_t')
                        if pend[0] is not None and (i == 6 or i == len(kcs) - 1):
                            emit_tail()
                        diag = kc0 <= kc < kc0 + 4
                        r = kc * 128 - q0
                        vmask = None
                        if diag:
                            P.wait('v', 's_t', mmS)
                            vmask = P.do('v', lambda e, sbk=sbk, r=r: e.tensor_add(
                                PS[:, sbk * 512 + r: sbk * 512 + r + 128],
                                PS[:, sbk * 512 + r: sbk * 512 + r + 128], tri_s[:]), 's_v')
                        w = r + 128 if diag else 512
                        P.wait('s', 's_t', mmS)
                        if vmask is not None:
                            P.wait('s', 's_v', vmask)
                        P.wait('s', 's_t', pv_slot[slot])
                        ex = P.do('s', lambda e, slot=slot, sbk=sbk, w=w: e.activation(
                            Pt[:, slot * 512: slot * 512 + w],
                            PS[:, sbk * 512: sbk * 512 + w], AF.Exp, bias=-BSHIFT), 's_s')
                        exp_bank[sbk] = ex
                        exk[kc] = ex
                        pmem = None
                        if diag and w < 512:
                            P.wait('v', 's_t', pv_slot[slot])
                            pmem = P.do('v', lambda e, slot=slot, w=w: e.memset(
                                Pt[:, slot * 512 + w: (slot + 1) * 512], 0.0), 's_v')
                        pmk[kc] = pmem
                        if i >= LA:
                            emit_pv(kcs[i - LA])
                    for kc in kcs[-LA:]:
                        emit_pv(kc)
                    P.mark[('pvlast', obank)] = pv
                    P.wait('v', 's_t', pv)
                    rc = P.do('v', lambda e, orow=orow, obank=obank: e.reciprocal(
                        s_sums[orow:orow + 1, :],
                        PS[orow:orow + 1, obank * 512:(obank + 1) * 512]), 's_v')
                    pend[0] = (orow, orows, obank, p, q0, tsl, rc)
            emit_tail()
            if p < 7:
                P.wait('y', 's_t', P.cnt['s_t'])
                load_pair_tables(p + 1)

        attn_done_v = P.cnt['s_v']
        if b == 0 and tap("og", [128, 16384], BF16):
            P.wait('y', 's_v', attn_done_v)
            P.do('y', lambda e: e.dma_start(out=dbg["og"], in_=og[:]), 'd_db', 16)
        if not _before(upto, b, "oproj"):
            break

        ga_ev = build_bcast(2, b, S3)
        P.wait('t', 'd_w3', wout_target)
        P.wait('t', 's_v', attn_done_v)
        evac_op = {}

        def oproj_dma(t):
            slot = t % 2
            sem = 'd_xsa' if slot == 0 else 'd_xsb'
            if t - 2 in evac_op:
                P.wait('y', 's_v', evac_op[t - 2])
            else:
                P.wait('y', 's_v', attn_done_v)
            P.do('y', lambda e, b=b, t=t, slot=slot: e.dma_start(
                out=xstg[:, slot * 1024:(slot + 1) * 1024],
                in_=x_d[b, t * 128:(t + 1) * 128, :]), sem, 16)
            P.mark[('opx', b, t)] = P.cnt[sem]

        oproj_dma(0)
        oproj_dma(1)
        for t in range(NT):
            slot = t % 2
            sem = 'd_xsa' if slot == 0 else 'd_xsb'
            b0k, b1k = 2 * (t % 2), 2 * (t % 2) + 1
            bank_wait('t', b0k)
            bank_wait('t', b1k)
            mm = None
            for K in range(8):
                for ncx in range(2):
                    mm = P.do('t', lambda e, K=K, t=t, b0k=b0k, ncx=ncx: e.matmul(
                        PS[:, b0k * 512 + ncx * 512: b0k * 512 + (ncx + 1) * 512],
                        og[:, K * 2048 + t * 128: K * 2048 + (t + 1) * 128],
                        w3sb[:, K * 1024 + ncx * 512: K * 1024 + (ncx + 1) * 512],
                        start=(K == 0), stop=(K == 7)), 's_t')
            P.wait('v', 's_t', mm)
            P.wait('v', sem, P.mark[('opx', b, t)])
            P.wait('v', 's_v', ga_ev)
            P.do('v', lambda e, b0k=b0k: e.tensor_mul(
                tmp_gb[:], PS[:, b0k * 512: b0k * 512 + 1024], S3[:]), 's_v')
            P.vdep()
            ev = P.do('v', lambda e, t=t, slot=slot: e.tensor_add(
                x_buf[:, t * 1024:(t + 1) * 1024],
                xstg[:, slot * 1024:(slot + 1) * 1024], tmp_gb[:]), 's_v')
            evac_op[t] = ev
            free_bank[b0k] = ('s_v', ev)
            free_bank[b1k] = ('s_v', ev)
            if t + 2 < NT:
                oproj_dma(t + 2)
        oproj_done_v = P.cnt['s_v']
        if b == 0 and tap("x2", [128, 16384], F32):
            P.wait('y', 's_v', oproj_done_v)
            P.do('y', lambda e: e.dma_start(out=dbg["x2"], in_=x_buf[:]), 'd_db', 16)
        if not _before(upto, b, "ln2"):
            break

        P.wait('v', 's_t', P.cnt['s_t'])
        hv2 = ln_phase(b, 2)
        if b == 0 and tap("h2", [128, 16384], BF16):
            P.wait('y', 's_v', hv2)
            P.do('y', lambda e: e.dma_start(out=dbg["h2"], in_=h_buf[:]), 'd_db', 16)
        if not _before(upto, b, "gate"):
            break

        wg_done = {}
        for t in range(NT):
            bank_wait('t', 4)
            P.wait('t', 's_v', hv2)
            mm = None
            for K in range(8):
                mm = P.do('t', lambda e, K=K, t=t: e.matmul(
                    PS[:, 4 * 512: 4 * 512 + 8],
                    h_buf[:, K * 2048 + t * 128: K * 2048 + (t + 1) * 128],
                    gatewsb[:, K * 8:(K + 1) * 8],
                    start=(K == 0), stop=(K == 7)), 's_t')
            P.wait('v', 's_t', mm)
            P.wait('v', 's_s')
            P.do('v', lambda e: e.tensor_copy(gA[:, 0:8], PS[:, 4 * 512:4 * 512 + 8]), 's_v')
            P.vdep()
            ev = P.do('v', lambda e: e.reduce_max(gB[:, 0:1], gA[:, 0:8], axis=AXX, negate=True), 's_v')
            free_bank[4] = ('s_v', ev)
            P.vdep()
            P.do('v', lambda e: e.tensor_scalar_add(gC[:, 0:8], gA[:, 0:8], gB[:, 0:1]), 's_v')
            P.vdep()
            P.do('v', lambda e: e.tensor_single_scalar(gB[:, 1:9], gC[:, 0:8], 0.0, ALU.is_ge), 's_v')
            P.vdep()
            P.do('v', lambda e: e.scalar_tensor_tensor(
                gC[:, 8:16], gB[:, 1:9], -1e30, gA[:, 0:8], ALU.mult, ALU.add), 's_v')
            P.vdep()
            vm = P.do('v', lambda e: e.reduce_max(gB[:, 9:10], gC[:, 8:16], axis=AXX, negate=True), 's_v')
            P.wait('s', 's_v', vm)
            ex = P.do('s', lambda e: e.activation(gC[:, 0:8], gA[:, 0:8], AF.Exp, bias=gB[:, 0:1]), 's_s')
            P.vdep()
            P.do('v', lambda e: e.tensor_scalar_add(gtmp[:, 0:8], gA[:, 0:8], gB[:, 9:10]), 's_v')
            P.vdep()
            P.do('v', lambda e: e.tensor_single_scalar(gB[:, 1:9], gtmp[:, 0:8], 0.0, ALU.is_ge), 's_v')
            P.wait('v', 's_s', ex)
            P.vdep()
            P.do('v', lambda e: e.tensor_mul(gA[:, 8:16], gC[:, 0:8], gB[:, 1:9]), 's_v')
            P.vdep()
            P.do('v', lambda e: e.reduce_sum(gB[:, 10:11], gA[:, 8:16], axis=AXX), 's_v')
            P.vdep()
            P.do('v', lambda e: e.reciprocal(gC[:, 1:2], gB[:, 10:11]), 's_v')
            P.vdep()
            wg = P.do('v', lambda e, t=t: e.tensor_scalar_mul(
                Wg[:, t * 8:(t + 1) * 8], gA[:, 8:16], gC[:, 1:2]), 's_v')
            wg_done[t] = wg
        bank_wait('t', 4)
        bank_wait('t', 5)
        for t in range(NT):
            sl = (4 + t % 2) * 512
            ev = free_bank.pop(('gt', t % 2), None)
            if ev is not None:
                P.wait('t', ev[0], ev[1])
            P.wait('t', 's_v', wg_done[t])
            tr = P.do('t', lambda e, t=t, sl=sl: e.transpose(
                PS[0:8, sl:sl + 128], Wg[:, t * 8:(t + 1) * 8], identf_s[:]), 's_t')
            P.wait('v', 's_t', tr)
            cp = P.do('v', lambda e, t=t, sl=sl: e.tensor_copy(
                Wrow[0:8, t * 128:(t + 1) * 128], PS[0:8, sl:sl + 128]), 's_v')
            free_bank[('gt', t % 2)] = ('s_v', cp)
        gate_done_v = P.cnt['s_v']
        free_bank[4] = ('s_v', gate_done_v)
        free_bank[5] = ('s_v', gate_done_v)
        if b == 0 and tap("wrow", [8, 2048], BF16):
            P.wait('y', 's_v', gate_done_v)
            P.do('y', lambda e: e.dma_start(out=dbg["wrow"], in_=Wrow[:]), 'd_db', 16)
        if not _before(upto, b, "ffn"):
            break

        gb_ev = build_bcast(5, b, S3)
        P.wait('v', 's_t', P.cnt['s_t'])
        og_free_v = P.cnt['s_v']
        w12_done = {}
        w12_fill = {}
        glast = None

        def w12_dma(e_i, s, M):
            blk = w12_seq[0]
            w12_seq[0] += 1
            slot12 = blk % 2
            sem12 = 'd_w12a' if slot12 == 0 else 'd_w12b'
            if blk - 2 in w12_done:
                P.wait('g', 's_t', w12_done[blk - 2])
            P.do('g', lambda e, e_i=e_i, M=M, slot12=slot12: e.dma_start(
                out=w12sb[:, slot12 * 2048:(slot12 + 1) * 2048],
                in_=w12_d[e_i, M]), sem12, 16)
            w12_fill[blk] = P.cnt[sem12]
            if P.mark.get('w3chunk', 8) < 8:
                P.mark['w3fn']()
            return blk

        pending = []
        for e_i in range(NE):
            for s in range(2):
                for M in range(16):
                    pending.append((e_i, s, M))
        pend_i = [0]
        def pump_w12():
            if pend_i[0] < len(pending):
                e_i, s, M = pending[pend_i[0]]
                pend_i[0] += 1
                return w12_dma(e_i, s, M)
            return None
        blk0 = pump_w12()
        blk1 = pump_w12()

        blkc = [blk0 if blk0 is not None else 0]
        blk_iter = [w12_seq[0] - 2]

        for e_i in range(NE):
            P.wait('g', 's_t', P.cnt['s_t'])
            P.mark['w3todo'] = e_i
            P.mark['w3chunk'] = 0
            def w3_chunk(e_i=None):
                k = P.mark['w3chunk']
                if k >= 8:
                    return
                ei = P.mark['w3todo']
                P.mark['w3chunk'] = k + 1
                P.do('g', lambda e, ei=ei, k=k: e.dma_start(
                    out=w3sb[:, k * 2048:(k + 1) * 2048],
                    in_=w3_d[ei, 2 * k:2 * k + 2].rearrange("kh kp m -> kp kh m")),
                    'd_w3', 16)
            P.mark['w3fn'] = w3_chunk
            w3_chunk()
            w3_target = P.cnt['d_w3'] + 16 * 7
            for s in range(2):
                if e_i < 8:
                    bank_wait('t', 4)
                    bank_wait('t', 5)
                    P.wait('t', 's_v', gate_done_v)
                    P.do('t', lambda e, e_i=e_i, s=s: e.matmul(
                        PS[:, 2048:2048 + 512], ind8_s[:, e_i * 128:(e_i + 1) * 128],
                        Wrow[:, s * 1024: s * 1024 + 512], start=True, stop=True), 's_t')
                    wbm = P.do('t', lambda e, e_i=e_i, s=s: e.matmul(
                        PS[:, 2560:2560 + 512], ind8_s[:, e_i * 128:(e_i + 1) * 128],
                        Wrow[:, s * 1024 + 512: (s + 1) * 1024], start=True, stop=True), 's_t')
                    P.wait('v', 's_t', wbm)
                    wbe = P.do('v', lambda e: e.tensor_copy(Wb[:], PS[:, 2048:3072]), 's_v')
                    free_bank[4] = ('s_v', wbe)
                    free_bank[5] = ('s_v', wbe)
                else:
                    wbe = None
                for M in range(16):
                    blk = blk_iter[0]
                    blk_iter[0] += 1
                    slot12 = blk % 2
                    sem12 = 'd_w12a' if slot12 == 0 else 'd_w12b'
                    sA = (2 * M) % 3
                    sB = (2 * M + 1) % 3
                    P.wait('t', sem12, w12_fill[blk])
                    if M == 0:
                        P.wait('t', 's_v', hv2)
                        if wbe is not None:
                            P.wait('t', 's_v', wbe)
                        P.wait('t', 's_v', og_free_v)
                    mh1 = mh2 = None
                    for half, bki in ((0, 2 * sA), (1, 2 * sA + 1)):
                        bank_wait('t', bki)
                        for K in range(8):
                            mh1 = P.do('t', lambda e, K=K, half=half, bki=bki, slot12=slot12, s=s: e.matmul(
                                bank(bki),
                                w12sb[:, slot12 * 2048 + K * 128:slot12 * 2048 + (K + 1) * 128],
                                h_buf[:, K * 2048 + s * 1024 + half * 512: K * 2048 + s * 1024 + (half + 1) * 512],
                                start=(K == 0), stop=(K == 7)), 's_t')
                    for half, bki in ((0, 2 * sB), (1, 2 * sB + 1)):
                        bank_wait('t', bki)
                        for K in range(8):
                            mh2 = P.do('t', lambda e, K=K, half=half, bki=bki, slot12=slot12, s=s: e.matmul(
                                bank(bki),
                                w12sb[:, slot12 * 2048 + 1024 + K * 128:slot12 * 2048 + 1024 + (K + 1) * 128],
                                h_buf[:, K * 2048 + s * 1024 + half * 512: K * 2048 + s * 1024 + (half + 1) * 512],
                                start=(K == 0), stop=(K == 7)), 's_t')
                    w12_done[blk] = P.cnt['s_t']
                    pump_w12()
                    P.wait('s', 's_t', mh1)
                    if glast is not None:
                        P.wait('s', 's_v', glast)
                    si = P.do('s', lambda e, sA=sA: e.activation(
                        tmp_si[:], PS[:, 2 * sA * 512: 2 * sA * 512 + 1024], AF.Silu), 's_s')
                    free_bank[2 * sA] = ('s_s', si)
                    free_bank[2 * sA + 1] = ('s_s', si)
                    P.wait('v', 's_t', mh2)
                    P.wait('v', 's_s', si)
                    siop = tmp_si
                    if e_i < 8:
                        P.vdep()
                        P.do('v', lambda e: e.tensor_mul(tmp_si2[:], tmp_si[:], Wb[:]), 's_v')
                        P.vdep()
                        siop = tmp_si2
                    gv = P.do('v', lambda e, sB=sB, M=M, siop=siop: e.tensor_mul(
                        og[:, M * 1024:(M + 1) * 1024], siop[:],
                        PS[:, 2 * sB * 512: 2 * sB * 512 + 1024]), 's_v')
                    free_bank[2 * sB] = ('s_v', gv)
                    free_bank[2 * sB + 1] = ('s_v', gv)
                    glast = gv
                P.wait('t', 'd_w3', w3_target)
                P.wait('t', 's_v', glast)
                for t in range(8):
                    tt = 8 * s + t
                    b0k, b1k = 2 * (t % 2), 2 * (t % 2) + 1
                    bank_wait('t', b0k)
                    bank_wait('t', b1k)
                    mm = None
                    for K in range(16):
                        for ncx in range(2):
                            mm = P.do('t', lambda e, K=K, t=t, b0k=b0k, ncx=ncx: e.matmul(
                                PS[:, b0k * 512 + ncx * 512: b0k * 512 + (ncx + 1) * 512],
                                og[:, K * 1024 + t * 128: K * 1024 + (t + 1) * 128],
                                w3sb[:, K * 1024 + ncx * 512: K * 1024 + (ncx + 1) * 512],
                                start=(K == 0), stop=(K == 15)), 's_t')
                    P.wait('v', 's_t', mm)
                    P.wait('v', 's_v', gb_ev)
                    P.do('v', lambda e, b0k=b0k: e.tensor_mul(
                        tmp_gb[:], PS[:, b0k * 512: b0k * 512 + 1024], S3[:]), 's_v')
                    P.vdep()
                    ev = P.do('v', lambda e, tt=tt: e.tensor_add(
                        x_buf[:, tt * 1024:(tt + 1) * 1024],
                        x_buf[:, tt * 1024:(tt + 1) * 1024], tmp_gb[:]), 's_v')
                    ffn_evac[(b, s, t)] = ev
                    free_bank[b0k] = ('s_v', ev)
                    free_bank[b1k] = ('s_v', ev)
        if not _before(upto, b, "out"):
            break

        for tt in range(NT):
            P.wait('y', 's_v', ffn_evac[(b, tt // 8, tt % 8)])
            P.do('y', lambda e, b=b, tt=tt: e.dma_start(
                out=y_d[b, tt * 128:(tt + 1) * 128, :],
                in_=x_buf[:, tt * 1024:(tt + 1) * 1024]), 'd_xo', 16)
        out_dma_total = P.cnt['d_xo']

    P.wait('y', 'd_xo', P.cnt.get('d_xo', 0))
    P.wait('y', 'd_db', P.cnt.get('d_db', 0))

    semnames = ['s_t', 's_v', 's_s', 'd_xsa', 'd_xsb', 'd_xo', 'd_kr', 'd_wq',
                'd_w12a', 'd_w12b', 'd_w3', 'd_wm', 'd_wmb', 'd_misc', 'd_db']
    with contextlib.ExitStack() as stack:
        stack.enter_context(nc.allow_low_precision(
            reason="bf16 matmul operands / normalizers; fp32 accumulation kept"))
        for s in semnames:
            P.sem[s] = stack.enter_context(nc.semaphore(s))
        block = stack.enter_context(nc.Block())
        block.gpsimd(lambda e: [f(e) for f in P.ops['g']])
        block.scalar(lambda e: [f(e) for f in P.ops['s']])
        block.tensor(lambda e: [f(e) for f in P.ops['t']])
        block.vector(lambda e: [f(e) for f in P.ops['v']])
        block.sync(lambda e: [f(e) for f in P.ops['y']])
    return nc, dbg


def kernel(**inputs):
    lay = host_layouts(inputs)
    x = np.ascontiguousarray(np.asarray(inputs["L0_attn_x"], dtype=np.float32))
    cc = np.ascontiguousarray(np.asarray(inputs["L0_c"], dtype=np.float32))
    nc, dbg = build_program(taps=tuple(_TAPS), upto=_UPTO)
    in_maps = []
    for c in range(NCORES):
        in_maps.append({
            "x": np.ascontiguousarray(x[2 * c:2 * c + 2]),
            "cvec": np.ascontiguousarray(cc[2 * c:2 * c + 2]),
            "wqkv": lay["wqkv"], "wout": lay["wout"], "gatew": lay["gate"],
            "wmod": lay["wmod"], "w12": lay["w12"], "w3": lay["w3"],
        })
    res = run_bass_kernel_spmd(nc, in_maps, core_ids=list(range(NCORES)))
    out = np.empty((16, T, D), dtype=np.float32)
    for c in range(NCORES):
        out[2 * c:2 * c + 2] = res.results[c]["y"]
    if _TAPS:
        kernel._dbg = [{k: r["dbg_" + k] for k in _TAPS} for r in res.results]
    return out
